# version 09
# baseline (speedup 1.0000x reference)
import numpy as np
import ml_dtypes
import concourse.bass as bass
import concourse.mybir as mybir
from concourse.bass_utils import run_bass_kernel_spmd
from contextlib import ExitStack

F32 = mybir.dt.float32
BF16 = mybir.dt.bfloat16
AF = mybir.ActivationFunctionType
ALU = mybir.AluOpType

T = 2048; D = 2048; NT = 16; NST = 16; NH = 16
CH = 1024; NCH = T // CH
SC = 512; NSC = T // SC
SCPC = CH // SC
MAGIC = 12582912.0
INVSQ = float(1.0 / np.sqrt(128.0))
SOFTBIAS = -20.0
UPAD = 8
UW = SC + UPAD


class Ctr:
    def __init__(self, name):
        self.name = name
        self.n = 0


class Prog:
    def __init__(self):
        self.items = {e: [] for e in ("gp", "sy", "te", "ve", "sc")}

    def op(self, eng, fn, waits=(), inc=None, k=1):
        wl = [(c.name, v) for (c, v) in waits if v and v > 0]
        tgt = None
        if inc is not None:
            inc.n += k
            tgt = (inc.name, k)

        def run(E, sems):
            for nm, v in wl:
                E.wait_ge(sems[nm], v)
            ins = fn(E)
            if tgt is not None:
                ins.then_inc(sems[tgt[0]], tgt[1])
        self.items[eng].append(run)


def _finish(P, cGP, dGP, dSY):
    P.op("gp", lambda E: None, waits=[(cGP, cGP.n), (dGP, dGP.n)])
    P.op("sy", lambda E: None, waits=[(cGP, cGP.n), (dSY, dSY.n)])


def _close(nc, P):
    with (nc.semaphore("sdGP") as s0, nc.semaphore("sdSY") as s1,
          nc.semaphore("scTE") as s2, nc.semaphore("scVE") as s3,
          nc.semaphore("scSC") as s4, nc.semaphore("scGP") as s5,
          nc.Block() as block):
        sems = {"dGP": s0, "dSY": s1, "cTE": s2, "cVE": s3,
                "cSC": s4, "cGP": s5}

        @block.gpsimd
        def _(E):
            for fn in P.items["gp"]:
                fn(E, sems)

        @block.sync
        def _(E):
            for fn in P.items["sy"]:
                fn(E, sems)

        @block.tensor
        def _(E):
            for fn in P.items["te"]:
                fn(E, sems)

        @block.vector
        def _(E):
            for fn in P.items["ve"]:
                fn(E, sems)

        @block.scalar
        def _(E):
            for fn in P.items["sc"]:
                fn(E, sems)

    return nc


def build_program(taps=(), stage=99):
    nc = bass.Bass("TRN2", target_bir_lowering=False,
                   detect_race_conditions=False)
    P = Prog()
    taps = set(taps)

    def din(name, shape, dt=F32):
        return nc.dram_tensor(name, shape, dt, kind="ExternalInput").ap()

    xT = din("xT", [D, T])
    convw = din("convw", [128, 4 * NT])
    convb = din("convb", [128, NT])
    xprojT_d = din("xprojT", [D, 160])
    dtprojT_d = din("dtprojT", [128, D])
    dtprojb = din("dtprojb", [128, NT])
    alog = din("alog", [128, NST * NT])
    dpar = din("dpar", [128, NT])
    w0T = din("w0T", [D, D])
    b0c = din("b0c", [128, NT])
    wqT = din("wqT", [D, D])
    wkT = din("wkT", [D, D])
    wvT = din("wvT", [D, D])
    woT = din("woT", [D, D])
    gate = din("gate", [1, NH])
    w1T = din("w1T", [D, D])
    b1c = din("b1c", [128, NT])
    cosTd = din("cosT", [D, T])
    sinTd = din("sinT", [D, T])
    identd = din("ident", [128, 128])
    out = nc.dram_tensor("y", [T, D], F32, kind="ExternalOutput").ap()

    def scr(name, shape, dt):
        return nc.dram_tensor(name, shape, dt, kind="Internal").ap()

    s_bc = scr("s_bc", [NSC, 32, SC], BF16)
    s_sign = scr("s_sign", [D, D], BF16)
    s_wq1 = scr("s_wq1", [D, D], BF16)
    s_woT = scr("s_woT", [D, D], BF16)
    s_qT = scr("s_qT", [D, T], BF16)
    s_kT = scr("s_kT", [D, T], BF16)
    s_v = scr("s_v", [T, D], BF16)
    s_row = scr("s_row", [8, 2048], F32)
    s_R = scr("s_R", [D, T], BF16)
    s_qTr = scr("s_qTr", [D, T], BF16)
    s_kTr = scr("s_kTr", [D, T], BF16)

    dGP = Ctr("dGP"); dSY = Ctr("dSY")
    cTE = Ctr("cTE"); cVE = Ctr("cVE"); cSC = Ctr("cSC"); cGP = Ctr("cGP")

    tap_info = {}

    with ExitStack() as ctx:
        def sbuf(name, shape, dt):
            return ctx.enter_context(nc.sbuf_tensor(name, shape, dt))

        R = sbuf("R", [128, NT * CH], BF16)
        U1 = sbuf("U1", [128, NT * UW], BF16)
        U23 = sbuf("U23", [128, NT * CH], BF16)
        BC4 = sbuf("BC4", [128, 16384], BF16)
        XC2 = sbuf("XC2", [128, NT * SC], BF16)
        KB2 = sbuf("KB2", [128, 8192], BF16)
        MSC = sbuf("MSC", [128, 4096], BF16)
        F32A = sbuf("F32A", [128, 3584], F32)
        F32B = sbuf("F32B", [128, 2048], F32)
        PAR = sbuf("PAR", [128, 768], F32)
        PARB = sbuf("PARB", [128, 4640], BF16)
        ROWS = sbuf("ROWS", [1, 4096], F32)
        OUTS = sbuf("OUTS", [128, 1024], F32)
        IDEN = sbuf("IDEN", [128, 128], F32)
        ONER = sbuf("ONER", [1, 128], F32)
        IDENB = sbuf("IDENB", [128, 128], BF16)
        GS = sbuf("GS", [1, 32], F32)
        CAPS = sbuf("CAPS", [128, 8], F32)

        PS3 = ctx.enter_context(nc.psum_tensor("PS3", [128, 1536], F32))
        pb = [PS3[:, 512 * i:512 * (i + 1)] for i in range(3)]
        pb += [ctx.enter_context(nc.psum_tensor(f"pb{i}", [128, 512], F32))
               for i in range(3, 7)]
        pbt = ctx.enter_context(nc.psum_tensor("pbt", [128, 1024], BF16))

        P_convw = 0; P_convb = 64; P_dtb = 80; P_dpar = 96
        P_b0 = 112; P_b1 = 128; P_alpha = 144; P_ws1 = 160
        P_aneg = 176
        P_hc = 432
        P_meanf = 688
        B_dtp = 0
        B_xp = 2048
        B_ones = 4608
        B_mean = 4610
        B_norm = 4612


        def gdma(dst, src, waits=(), slow=False):
            P.op("gp", lambda E, d=dst, s=src, sl=slow: E.dma_start(
                out=d, in_=s, allow_slow_non_contiguous=sl),
                 waits=waits, inc=dGP, k=16)
            return dGP.n

        def sdma(dst, src, waits=(), transpose=False):
            P.op("sy", lambda E, d=dst, s=src, t=transpose:
                 E.dma_start(out=d, in_=s, transpose=t),
                 waits=waits, inc=dSY, k=16)
            return dSY.n

        def te(fn, waits=()):
            P.op("te", fn, waits=waits, inc=cTE)
            return cTE.n

        def ve(fn, waits=()):
            P.op("ve", fn, waits=waits, inc=cVE)
            return cVE.n

        def sca(fn, waits=()):
            P.op("sc", fn, waits=waits, inc=cSC)
            return cSC.n

        def tap(name, ap, shape, dt=F32):
            if name not in taps:
                return
            t = nc.dram_tensor("tap_" + name, shape, dt, kind="ExternalOutput").ap()
            tap_info[name] = True
            w = [(cTE, cTE.n), (cVE, cVE.n), (cSC, cSC.n), (dSY, dSY.n)]
            gdma(t, ap, waits=w)

        gdma(PARB[:, B_dtp:B_dtp + 2048], dtprojT_d[:, :])
        gdma(PARB[:, B_xp:B_xp + 2560].rearrange("p (i q) -> p i q", q=160),
             xprojT_d.rearrange("(i p) q -> p i q", p=128))
        gdma(PAR[:, P_convw:P_convw + 64], convw[:, :])
        gdma(PAR[:, P_convb:P_convb + 16], convb[:, :])
        gdma(PAR[:, P_dtb:P_dtb + 16], dtprojb[:, :])
        gdma(PAR[:, P_dpar:P_dpar + 16], dpar[:, :])
        gdma(PAR[:, P_b0:P_b0 + 16], b0c[:, :])
        gdma(PAR[:, P_b1:P_b1 + 16], b1c[:, :])
        gdma(PAR[:, P_aneg:P_aneg + 256], alog[:, :])
        gdma(IDEN[:, :], identd[:, :])
        gdma(IDENB[:, :], identd[:, :])
        gdma(ROWS[0:1, 0:16], gate[:, :])
        d_par = dGP.n
        sca(lambda E: E.activation(PAR[:, P_aneg:P_aneg + 256],
                                   PAR[:, P_aneg:P_aneg + 256], AF.Exp),
            waits=[(dGP, d_par)])
        ve(lambda E: E.tensor_scalar_mul(PAR[:, P_aneg:P_aneg + 256],
                                         PAR[:, P_aneg:P_aneg + 256], -1.0),
           waits=[(cSC, cSC.n)])
        ve(lambda E: E.memset(PARB[:, B_ones:B_ones + 1], 1.0))
        ve(lambda E: E.memset(ONER[0:1, :], 1.0))
        ve(lambda E: E.memset(PAR[:, P_meanf:P_meanf + 1], 1.0 / 2048.0))
        ve(lambda E: E.memset(CAPS[:, 0:1], 1e-5))
        ve(lambda E: E.memset(CAPS[:, 1:2], -0.5))
        ve(lambda E: E.memset(CAPS[:, 2:3], -1.0))
        ve(lambda E: E.memset(CAPS[:, 3:4], 1.0))
        ve(lambda E: E.memset(CAPS[:, 4:5], SOFTBIAS))
        ve(lambda E: E.memset(CAPS[:, 5:6], INVSQ))
        m_ones = ve(lambda E: E.memset(PARB[:, B_mean:B_mean + 1], 1.0 / 2048.0))
        ve(lambda E: E.memset(PAR[:, P_hc:P_hc + 256], 0.0))
        m_ut0 = ve(lambda E: E.memset(U1[:, :], 0.0))
        m_gs = sca(lambda E: E.activation(GS[0:1, 0:16], ROWS[0:1, 0:16],
                                          AF.Sigmoid), waits=[(dGP, d_par)])
        m_g1m = ve(lambda E: E.tensor_scalar(GS[0:1, 16:32], GS[0:1, 0:16],
                                             -1.0, 1.0, op0=ALU.mult, op1=ALU.add),
                   waits=[(cSC, m_gs)])

        for i in range(NT):
            kb = KB2[:, (i % 2) * 2048:(i % 2) * 2048 + 2048]
            dW = gdma(kb, w0T[128 * i:128 * (i + 1), :],
                      waits=[(cVE, cVE.n), (cSC, cSC.n)])
            m1 = ve(lambda E, b=kb: E.tensor_scalar(
                MSC[:, 0:2048], b, 0.0, None, op0=ALU.is_ge),
                waits=[(dGP, dW), (dSY, dSY.n)])
            m2 = ve(lambda E: E.tensor_scalar(
                MSC[:, 0:2048], MSC[:, 0:2048], 2.0, -1.0,
                op0=ALU.mult, op1=ALU.add))
            sdma(s_sign[128 * i:128 * (i + 1), :], MSC[:, 0:2048],
                 waits=[(cVE, m2)])
            ma = sca(lambda E, b=kb: E.activation(MSC[:, 2048:4096], b, AF.Abs),
                     waits=[(dGP, dW), (cTE, cTE.n)])
            for j in range(4):
                te(lambda E, j=j, i=i: E.matmul(
                    pb[3 + j][0:1, :], PARB[:, B_mean:B_mean + 1],
                    MSC[:, 2048 + 512 * j:2048 + 512 * (j + 1)],
                    start=(i == 0), stop=(i == NT - 1)),
                   waits=[(cSC, ma), (cVE, m_ones)])
        m_amm = cTE.n
        m_acp = 0
        for j in range(4):
            m_acp = sca(lambda E, j=j: E.copy(ROWS[0:1, 512 * j:512 * (j + 1)],
                                              pb[3 + j][0:1, :]),
                        waits=[(cTE, m_amm)])
        d_aw = gdma(s_row[1:2, 0:2048], ROWS[0:1, 0:2048], waits=[(cSC, m_acp)])
        gdma(PAR[:, P_alpha:P_alpha + 16],
             s_row[1, 0:2048].rearrange("(i p) -> p i", p=128),
             waits=[(dGP, d_aw)], slow=True)

        mxm = 0
        for i in range(NT):
            kb = KB2[:, (i % 2) * 2048:(i % 2) * 2048 + 2048]
            dW = gdma(kb, w1T[128 * i:128 * (i + 1), :],
                      waits=[(cVE, cVE.n), (cSC, cSC.n)])
            ma = sca(lambda E, b=kb: E.activation(MSC[:, 0:2048], b, AF.Abs),
                     waits=[(dGP, dW), (cVE, mxm)])
            if i == 0:
                mxm = ve(lambda E: E.tensor_copy(F32A[:, 0:2048], MSC[:, 0:2048]),
                         waits=[(cSC, ma)])
            else:
                mxm = ve(lambda E: E.tensor_tensor(
                    F32A[:, 0:2048], F32A[:, 0:2048], MSC[:, 0:2048], op=ALU.max),
                    waits=[(cSC, ma)])
        width = 64
        while width >= 1:
            dsh = gdma(F32B[0:width, 0:2048], F32A[width:2 * width, 0:2048],
                       waits=[(cVE, mxm)])
            mxm = ve(lambda E, w=width: E.tensor_tensor(
                F32A[0:w, 0:2048], F32A[0:w, 0:2048], F32B[0:w, 0:2048],
                op=ALU.max), waits=[(dGP, dsh)])
            width //= 2
        m_ws = ve(lambda E: E.tensor_scalar(ROWS[0:1, 0:2048],
                                            F32A[0:1, 0:2048], 1e-8, 1.0 / 127.0,
                                            op0=ALU.max, op1=ALU.mult),
                  waits=[(dGP, dGP.n)])
        m_wsr = ve(lambda E: E.reciprocal(ROWS[0:1, 2048:4096],
                                          ROWS[0:1, 0:2048]),
                   waits=[(cVE, m_ws)])
        d_ww = gdma(s_row[2:3, 0:2048], ROWS[0:1, 0:2048], waits=[(cVE, m_ws)])
        gdma(PAR[:, P_ws1:P_ws1 + 16],
             s_row[2, 0:2048].rearrange("(i p) -> p i", p=128),
             waits=[(dGP, d_ww)], slow=True)
        for j in range(4):
            te(lambda E, j=j: E.matmul(
                pb[j][:, :], ONER[0:1, :],
                ROWS[0:1, 2048 + 512 * j:2048 + 512 * (j + 1)],
                start=True, stop=True),
               waits=[(cVE, m_wsr), (cVE, m_ones)])
        m_wmm = cTE.n
        m_wcp = 0
        for j in range(4):
            m_wcp = ve(lambda E, j=j: E.tensor_copy(
                F32B[:, 512 * j:512 * (j + 1)], pb[j][:, :]),
                waits=[(cTE, m_wmm)])
        for i in range(NT):
            kb = KB2[:, (i % 2) * 2048:(i % 2) * 2048 + 2048]
            dW = gdma(kb, w1T[128 * i:128 * (i + 1), :],
                      waits=[(cVE, cVE.n), (dSY, dSY.n)])
            mq1 = ve(lambda E, b=kb: E.tensor_tensor(
                F32A[:, 0:2048], b, F32B[:, 0:2048], op=ALU.mult),
                waits=[(dGP, dW), (cVE, m_wcp)])
            mq2 = ve(lambda E: E.tensor_scalar(
                MSC[:, 0:2048], F32A[:, 0:2048], MAGIC, MAGIC,
                op0=ALU.add, op1=ALU.subtract))
            sdma(s_wq1[128 * i:128 * (i + 1), :], MSC[:, 0:2048],
                 waits=[(cVE, mq2)])
        for i in range(NT):
            kb = KB2[:, (i % 2) * 2048:(i % 2) * 2048 + 2048]
            dW = gdma(kb, woT[128 * i:128 * (i + 1), :], waits=[(dSY, dSY.n)])
            sdma(s_woT[128 * i:128 * (i + 1), :], kb, waits=[(dGP, dW)])
        m_prep_ve = cVE.n
        m_prep_sc = cSC.n
        m_prep_sy = dSY.n

        sq_marks = []

        def ln_feat(dst_cols, src_col, width, wait_ve=0, wait_x=()):
            sqs = [0, 512]
            mm_sq = {}
            for i in range(NT):
                rsl = R[:, CH * i + src_col:CH * i + src_col + width]
                w_sq = []
                if i >= 2:
                    w_sq = [(cTE, mm_sq[i - 2])]
                msq = ve(lambda E, r=rsl, s=sqs[i % 2]: E.tensor_tensor(
                    F32B[:, s:s + width], r, r, op=ALU.mult),
                    waits=w_sq + (list(wait_x) if i == 0 else []))
                w_mm = []
                if i == 0:
                    w_mm = [(cVE, max(wait_ve, cVE.n)), (cSC, cSC.n)] + list(wait_x)
                te(lambda E, r=rsl, i=i: E.matmul(
                    pb[3][0:1, 0:width], PARB[:, B_mean:B_mean + 1], r,
                    start=(i == 0), stop=(i == NT - 1)), waits=w_mm)
                mm_sq[i] = te(lambda E, s=sqs[i % 2], i=i: E.matmul(
                    pb[4][0:1, 0:width], PAR[:, P_meanf:P_meanf + 1],
                    F32B[:, s:s + width],
                    start=(i == 0), stop=(i == NT - 1)), waits=[(cVE, msq)])
            m_stat = cTE.n
            mvc = ve(lambda E: E.tensor_copy(ROWS[0:1, 1024:1024 + width],
                                             pb[3][0:1, 0:width]),
                     waits=[(cTE, cTE.n)])
            mv = ve(lambda E: E.tensor_tensor(
                ROWS[0:1, 0:width], ROWS[0:1, 1024:1024 + width],
                ROWS[0:1, 1024:1024 + width], op=ALU.mult))
            mv = ve(lambda E: E.tensor_tensor(
                ROWS[0:1, 0:width], pb[4][0:1, 0:width], ROWS[0:1, 0:width],
                op=ALU.subtract))
            ms = sca(lambda E: E.activation(ROWS[0:1, 512:512 + width],
                                            ROWS[0:1, 0:width],
                                            AF.Ln, bias=CAPS[0:1, 0:1]),
                     waits=[(cVE, mv), (cTE, cTE.n)])
            ms = sca(lambda E: E.activation(ROWS[0:1, 512:512 + width],
                                            ROWS[0:1, 512:512 + width],
                                            AF.Exp, scale=CAPS[0:1, 1:2]))
            mn = ve(lambda E: E.tensor_scalar(ROWS[0:1, 1536:1536 + width],
                                              ROWS[0:1, 0:width], 1e-5, None,
                                              op0=ALU.add), waits=[(cSC, ms)])
            mn = ve(lambda E: E.tensor_tensor(
                ROWS[0:1, 1536 + width:1536 + 2 * width] if width <= 256 else
                ROWS[0:1, 0:width],
                ROWS[0:1, 512:512 + width], ROWS[0:1, 512:512 + width],
                op=ALU.mult))
            mn = ve(lambda E: E.tensor_tensor(
                ROWS[0:1, 0:width], ROWS[0:1, 0:width],
                ROWS[0:1, 1536:1536 + width], op=ALU.mult))
            mn = ve(lambda E: E.tensor_scalar(
                ROWS[0:1, 0:width], ROWS[0:1, 0:width], -0.5, 1.5,
                op0=ALU.mult, op1=ALU.add))
            mn = ve(lambda E: E.tensor_tensor(
                ROWS[0:1, 512:512 + width], ROWS[0:1, 512:512 + width],
                ROWS[0:1, 0:width], op=ALU.mult))
            te(lambda E: E.matmul(pb[5][:, 0:width], ONER[0:1, :],
                                  ROWS[0:1, 1024:1024 + width],
                                  start=True, stop=True), waits=[(cVE, mvc)])
            m_bc = te(lambda E: E.matmul(
                pb[6][:, 0:width], ONER[0:1, :],
                ROWS[0:1, 512:512 + width], start=True, stop=True),
                waits=[(cVE, mn)])
            last = 0
            for i in range(NT):
                rsl = R[:, CH * i + src_col:CH * i + src_col + width]
                dsl = dst_cols(i)
                ve(lambda E, r=rsl, d=dsl: E.tensor_tensor(
                    d, r, pb[5][:, 0:width], op=ALU.subtract),
                   waits=[(cTE, m_bc)] if i == 0 else [])
                last = ve(lambda E, d=dsl: E.tensor_tensor(
                    d, d, pb[6][:, 0:width], op=ALU.mult))
            return last

        def mamba_sc(s, first_wait=(), bc_free=0, feeder=None):
            base = SC * s
            rbase = SC * (s % SCPC)
            if s > 0:
                ve(lambda E: E.tensor_copy(
                    U1.rearrange("p (i c) -> p i c", c=UW)[:, :, 5:8],
                    U1.rearrange("p (i c) -> p i c", c=UW)
                    [:, :, UPAD + SC - 3:UPAD + SC]), waits=list(first_wait))
                first_wait = ()
            wx = list(first_wait)
            if s == 0:
                wx += [(cVE, m_ut0)]
            m_u = ln_feat(lambda i: U1[:, i * UW + UPAD:i * UW + UPAD + SC],
                          rbase, SC, wait_x=wx + [(cTE, cTE.n)])
            if stage == 15:
                tap("rows", ROWS[0:1, :], [1, 4096], F32)
                return m_u
            silu_marks = {}
            d_bcw = dGP.n
            for i in range(NT):
                ub = U1[:, i * UW:(i + 1) * UW]
                ac = 2560 + 512 * (i % 2)
                w0_ = []
                if i < 2:
                    w0_ = [(cSC, cSC.n)]
                else:
                    w0_ = [(cSC, silu_marks[i - 2])]
                ve(lambda E, u=ub, a=ac, i=i: E.tensor_scalar(
                    F32A[:, a:a + SC], u[:, 5:5 + SC],
                    PAR[:, P_convw + 4 * i:P_convw + 4 * i + 1], None,
                    op0=ALU.mult), waits=w0_)
                for j in range(1, 4):
                    ve(lambda E, u=ub, a=ac, i=i, j=j: E.scalar_tensor_tensor(
                        F32A[:, a:a + SC], u[:, 5 + j:5 + j + SC],
                        PAR[:, P_convw + 4 * i + j:P_convw + 4 * i + j + 1],
                        F32A[:, a:a + SC], op0=ALU.mult, op1=ALU.add))
                m_cv = cVE.n
                silu_marks[i] = sca(lambda E, i=i, a=ac: E.activation(
                    XC2[:, i * SC:(i + 1) * SC], F32A[:, a:a + SC], AF.Silu,
                    bias=PAR[:, P_convb + i:P_convb + i + 1]),
                    waits=[(cVE, m_cv)])
            m_xc = cSC.n
            if stage == 17:
                return m_xc
            for i in range(NT):
                w = [(cSC, m_xc if i == 0 else 0), (cVE, cVE.n if i == 0 else 0)]
                w = [x for x in w if x[1]]
                te(lambda E, i=i: E.matmul(
                    pb[3][:, :], PARB[:, B_xp + 160 * i:B_xp + 160 * i + 128],
                    XC2[:, i * SC:(i + 1) * SC],
                    start=(i == 0), stop=(i == NT - 1)), waits=w)
                te(lambda E, i=i: E.matmul(
                    pb[4][0:32, :],
                    PARB[:, B_xp + 160 * i + 128:B_xp + 160 * i + 160],
                    XC2[:, i * SC:(i + 1) * SC],
                    start=(i == 0), stop=(i == NT - 1)))
            m_proj = cTE.n
            m_dtin = sca(lambda E: E.copy(MSC[:, 2048:2560], pb[3][:, :]),
                         waits=[(cTE, m_proj), (cVE, cVE.n)])
            m_bcv = sca(lambda E: E.copy(F32A[0:32, 2560:2560 + SC],
                                         pb[4][0:32, :]),
                        waits=[(cTE, m_proj), (cVE, cVE.n)])
            d_bcw = gdma(s_bc[s, :, :], F32A[0:32, 2560:2560 + SC],
                         waits=[(cSC, m_bcv)])
            d_ball = sdma(BC4[:, 0:8192].rearrange("p (n c) -> p n c", c=SC),
                          s_bc[s:s + 1, 0:16, :].broadcast_to([128, 16, SC]),
                          waits=[(dGP, d_bcw), (cVE, bc_free)])
            d_call = sdma(BC4[:, 8192:16384].rearrange("p (n c) -> p n c", c=SC),
                          s_bc[s:s + 1, 16:32, :].broadcast_to([128, 16, SC]),
                          waits=[])
            if stage == 18:
                return cVE.n
            prev_sp = (0, 0)
            m_last = cVE.n
            for i in range(NT):
                dcol = 512 * (i % 2)
                wmt = [(cSC, m_dtin)]
                if i == 0:
                    wmt.append((cVE, m_u))
                if prev_sp[0]:
                    wmt += [(cSC, prev_sp[0]), (cVE, prev_sp[1])]
                mt = te(lambda E, i=i: E.matmul(
                    pb[5][:, :], PARB[:, B_dtp + 128 * i:B_dtp + 128 * (i + 1)],
                    MSC[:, 2048:2560], start=True, stop=True), waits=wmt)
                if stage == 1801:
                    return mt
                ms1 = sca(lambda E, i=i, d=dcol: E.activation(
                    F32A[:, d:d + SC], pb[5][:, :], AF.Abs,
                    bias=PAR[:, P_dtb + i:P_dtb + i + 1]), waits=[(cTE, mt)])
                if stage == 1802:
                    return cVE.n
                sca(lambda E, d=dcol: E.activation(
                    F32A[:, d:d + SC], F32A[:, d:d + SC], AF.Exp, scale=CAPS[:, 2:3]))
                if stage == 1803:
                    return cVE.n
                ms3 = sca(lambda E, d=dcol: E.activation(
                    F32A[:, d:d + SC], F32A[:, d:d + SC], AF.Ln, bias=CAPS[:, 3:4]))
                if stage == 1804:
                    return cVE.n
                mr = ve(lambda E, i=i: E.tensor_scalar(
                    F32B[:, 0:SC], pb[5][:, :],
                    PAR[:, P_dtb + i:P_dtb + i + 1], 0.0,
                    op0=ALU.add, op1=ALU.max), waits=[(cTE, mt), (cSC, ms1)])
                prev_sp = (ms1, mr)
                if stage == 1805:
                    return cVE.n
                mdelta = ve(lambda E, d=dcol: E.tensor_tensor(
                    F32A[:, d:d + SC], F32A[:, d:d + SC], F32B[:, 0:SC],
                    op=ALU.add), waits=[(cSC, ms3)])
                if stage == 181:
                    return mdelta
                mw_w = [(dSY, dSY.n)] if (s == 0 and i == 0) else []
                m_w = ve(lambda E, i=i, d=dcol: E.tensor_tensor(
                    MSC[:, 0:512], F32A[:, d:d + SC],
                    XC2[:, i * SC:(i + 1) * SC], op=ALU.mult), waits=mw_w)
                ve(lambda E, i=i: E.tensor_scalar(
                    MSC[:, 2560:3072],
                    U1[:, i * UW + UPAD:i * UW + UPAD + SC],
                    PAR[:, P_dpar + i:P_dpar + i + 1], None, op0=ALU.mult))
                if stage == 182:
                    return cVE.n
                h_marks = {}
                for n in range(NST):
                    acol = 1024 + 512 * (n % 2)
                    wsa = [(cVE, mdelta)]
                    if n >= 2:
                        wsa.append((cVE, h_marks[n - 2]))
                    sa_ = sca(lambda E, i=i, n=n, a=acol, d=dcol: E.activation(
                        F32A[:, a:a + SC], F32A[:, d:d + SC], AF.Exp,
                        scale=PAR[:, P_aneg + 16 * i + n:P_aneg + 16 * i + n + 1]),
                        waits=wsa)
                    wb = [(dSY, d_ball)] if n == 0 else []
                    ve(lambda E, n=n: E.tensor_tensor(
                        MSC[:, 512:1024], MSC[:, 0:512],
                        BC4[:, 512 * n:512 * (n + 1)], op=ALU.mult), waits=wb)
                    m_h = ve(lambda E, i=i, n=n, a=acol: E.tensor_tensor_scan(
                        MSC[:, 1024:1536], F32A[:, a:a + SC], MSC[:, 512:1024],
                        PAR[:, P_hc + 16 * i + n:P_hc + 16 * i + n + 1],
                        op0=ALU.mult, op1=ALU.add), waits=[(cSC, sa_)])
                    h_marks[n] = m_h
                    if stage == 183:
                        return m_h
                    wc = [(dSY, d_call)] if n == 0 else []
                    ve(lambda E, n=n: E.tensor_tensor(
                        MSC[:, 1536:2048], MSC[:, 1024:1536],
                        BC4[:, 8192 + 512 * n:8192 + 512 * (n + 1)],
                        op=ALU.mult), waits=wc)
                    m_ya = ve(lambda E: E.tensor_tensor(
                        MSC[:, 2560:3072], MSC[:, 2560:3072],
                        MSC[:, 1536:2048], op=ALU.add))
                    ve(lambda E, i=i, n=n: E.tensor_copy(
                        PAR[:, P_hc + 16 * i + n:P_hc + 16 * i + n + 1],
                        MSC[:, 1535:1536]))
                    if stage == 184:
                        return m_ya
                m_last = ve(lambda E, i=i, b=rbase: E.tensor_tensor(
                    R[:, CH * i + b:CH * i + b + SC],
                    R[:, CH * i + b:CH * i + b + SC],
                    MSC[:, 2560:3072], op=ALU.add))
                if stage == 19:
                    return m_last
                if feeder:
                    feeder.pop(0)()
                    if feeder:
                        feeder.pop(0)()
            return m_last

        def ffn0(c, m_u2):
            lastv = 0
            for o in range(NT):
                kb = KB2[:, (o % 2) * 2048:(o % 2) * 2048 + 2048]
                wkb = [(cTE, cTE.n)]
                if c == 0 and o < 2:
                    wkb += [(cVE, m_prep_ve), (cSC, m_prep_sc)]
                dW = sdma(kb.rearrange("p (i q) -> p i q", q=128),
                          s_sign[:, 128 * o:128 * (o + 1)]
                          .rearrange("(i p) q -> p i q", p=128), waits=wkb)
                for half in range(2):
                    bank = pb[half]
                    for i in range(NT):
                        w = []
                        if i == 0:
                            w = [(dSY, dW), (cVE, max(m_u2, lastv)),
                                 (cSC, cSC.n)]
                        te(lambda E, i=i, h=half, b=bank, kb=kb: E.matmul(
                            b[:, :], kb[:, 128 * i:128 * (i + 1)],
                            U23[:, CH * i + 512 * h:CH * i + 512 * (h + 1)],
                            start=(i == 0), stop=(i == NT - 1)), waits=w)
                    m_mm = cTE.n
                    ve(lambda E, o=o, b=bank: E.tensor_scalar(
                        F32B[:, 1024:1536], b[:, :],
                        PAR[:, P_alpha + o:P_alpha + o + 1],
                        PAR[:, P_b0 + o:P_b0 + o + 1],
                        op0=ALU.mult, op1=ALU.add), waits=[(cTE, m_mm)])
                    rsl = R[:, CH * o + 512 * half:CH * o + 512 * half + 512]
                    lastv = ve(lambda E, r=rsl: E.tensor_tensor(
                        r, r, F32B[:, 1024:1536], op=ALU.add))
            return lastv

        def qkv_feeder(c, m_u3):
            base = CH * c
            st = {"ev": {}, "sp": {}, "pend": None,
                  "mmdone": {0: cTE.n, 1: cTE.n}}
            items = []

            def qk_load(wT, o, slot):
                def emit():
                    kb = KB2[:, (o % 2) * 2048:(o % 2) * 2048 + 2048]
                    slot[0] = gdma(kb.rearrange("p (i q) -> p i q", q=128),
                                   wT[:, 128 * o:128 * (o + 1)]
                                   .rearrange("(i p) q -> p i q", p=128),
                                   waits=[(cTE, st["mmdone"].get(o % 2, 0))])
                return emit

            def qk_mm(wT, o, slot):
                def emit():
                    kb = KB2[:, (o % 2) * 2048:(o % 2) * 2048 + 2048]
                    dW = slot[0]
                    mms = []
                    for half in range(2):
                        bk = (2 * o + half) % 3 if True else 0
                        bank = pb[bk]
                        for i in range(NT):
                            w = []
                            if i == 0:
                                w = [(dGP, dW), (cVE, m_u3),
                                     (cSC, st["ev"].get(bk, 0))]
                            te(lambda E, i=i, b=bank, kb=kb, h=half: E.matmul(
                                b[:, :], kb[:, 128 * i:128 * (i + 1)],
                                U23[:, CH * i + 512 * h:CH * i + 512 * (h + 1)],
                                start=(i == 0), stop=(i == NT - 1)), waits=w)
                        mms.append((bk, cTE.n))
                    st["mmdone"][o % 2] = cTE.n
                    return mms

                return emit

            def qk_ev(sdst, o, mms):
                def emit():
                    marks = []
                    for half, (bk, m_mm) in enumerate(mms()):
                        m_ev = sca(lambda E, b=pb[mms()[half][0]], h=half: E.copy(
                            KB2[:, 4096 + 512 * h:4096 + 512 * (h + 1)], b[:, :]),
                            waits=[(cTE, m_mm), (dSY, st["sp"].get("qk", 0))])
                        st["ev"][bk] = m_ev
                        marks.append(m_ev)
                    st["sp"]["qk"] = sdma(
                        sdst[128 * o:128 * (o + 1), base:base + CH],
                        KB2[:, 4096:5120], waits=[(cSC, marks[-1])])
                return emit

            def v_mm(ob, tt, dwv):
                def emit():
                    if tt == 0:
                        dwv[0] = gdma(KB2.rearrange("p (i q) -> p i q", q=512),
                                      wvT[:, 512 * ob:512 * (ob + 1)]
                                      .rearrange("(i p) q -> p i q", p=128),
                                      waits=[(cTE, cTE.n), (dSY, dSY.n),
                                             (cSC, st["ev"].get(tt % 2, 0))])
                    bank = pb[tt % 2]
                    for i in range(NT):
                        w = []
                        if i == 0:
                            w = [(dGP, dwv[0]), (cVE, m_u3),
                                 (cSC, st["ev"].get(tt % 2, 0))]
                        te(lambda E, i=i, tt=tt, b=bank: E.matmul(
                            b[:, :],
                            U23[:, CH * i + 128 * tt:CH * i + 128 * (tt + 1)],
                            KB2[:, 512 * i:512 * (i + 1)],
                            start=(i == 0), stop=(i == NT - 1)), waits=w)
                    return [(tt % 2, cTE.n)]
                return emit

            def v_ev(ob, tt, mms):
                def emit():
                    bk, m_mm = mms()[0]
                    m_ev = sca(lambda E, b=pb[bk]: E.copy(F32B[:, 1536:2048],
                                                          b[:, :]),
                               waits=[(cTE, m_mm), (dGP, st["sp"].get("v", 0))])
                    st["ev"][bk] = m_ev
                    st["sp"]["v"] = gdma(
                        s_v[base + 128 * tt:base + 128 * (tt + 1),
                            512 * ob:512 * (ob + 1)], F32B[:, 1536:2048],
                        waits=[(cSC, m_ev)])
                return emit

            units = []
            for o in range(NT):
                units.append(("qk", wqT, s_qT, o))
            for o in range(NT):
                units.append(("qk", wkT, s_kT, o))
            for ob in range(4):
                dwv = [0]
                for tt in range(8):
                    units.append(("v", ob, tt, dwv))
            pend = [None]
            pre = []
            for u in units:
                if u[0] == "qk":
                    _, wT, sdst, o = u
                    res = {}
                    slot = [0]
                    ld = qk_load(wT, o, slot)
                    mm = qk_mm(wT, o, slot)
                    def item(mm=mm, res=res):
                        res["mms"] = mm()
                    ev = qk_ev(sdst, o, lambda res=res: res["mms"])
                else:
                    _, ob, tt, dwv = u
                    res = {}
                    ld = None
                    mm = v_mm(ob, tt, dwv)
                    def item(mm=mm, res=res):
                        res["mms"] = mm()
                    ev = v_ev(ob, tt, lambda res=res: res["mms"])
                pre.append(ld)
                units[len(pre) - 1] = (item, ev)
            for k, (item, ev) in enumerate(units):
                def combined(item=item, prev_ev=pend[0],
                             nld=(pre[k + 1] if k + 1 < len(pre) else None),
                             first=(pre[k] if k == 0 else None)):
                    if first is not None:
                        first()
                    if nld is not None:
                        nld()
                    if prev_ev is not None:
                        prev_ev()
                    item()
                items.append(combined)
                pend[0] = ev
            def final(prev_ev=pend[0]):
                if prev_ev is not None:
                    prev_ev()
            items.append(final)
            return items

        m_u3_prev = 0
        if stage < 2:
            _finish(P, cGP, dGP, dSY)
            return _close(nc, P)
        m_mamba = 0
        d_rspill = 0
        feeder = []
        for c in range(NCH):
            wR = []
            if c > 0:
                wR = [(dSY, d_rspill), (cVE, m_u3_prev)]
            d_xload = gdma(R.rearrange("p (i q) -> p i q", q=CH),
                           xT[:, CH * c:CH * (c + 1)]
                           .rearrange("(i p) q -> p i q", p=128), waits=wR)
            for sci in range(SCPC):
                s = SCPC * c + sci
                fw = []
                if sci == 0:
                    fw = [(dGP, d_xload)]
                m_mamba = mamba_sc(s, first_wait=fw, bc_free=m_mamba,
                                   feeder=feeder)
                if stage == 22 and s == 0:
                    tap("hc", PAR[:, :], [128, 768], F32)
                    _finish(P, cGP, dGP, dSY)
                    return _close(nc, P)
                if stage == 23 and s == 1:
                    tap("R_c0", R[:, :], [128, NT * CH], BF16)
                    _finish(P, cGP, dGP, dSY)
                    return _close(nc, P)
                if stage in (15, 17, 18, 1801, 1802, 1803, 1804, 1805, 181, 182, 183, 184, 19, 2):
                    tap("R_sc0", R[:, :], [128, NT * CH], BF16)
                    tap("u_sc0", U1[:, :], [128, NT * UW], BF16)
                    tap("xc_sc0", XC2[:, :], [128, NT * SC], BF16)
                    _finish(P, cGP, dGP, dSY)
                    return _close(nc, P)
            for fn in feeder:
                fn()
            feeder = []
            m_u2 = 0
            for half in range(2):
                m_u2 = ln_feat(lambda i, h=half: U23[:, CH * i + 512 * h:
                                                     CH * i + 512 * h + 512],
                               512 * half, 512,
                               wait_ve=m_mamba, wait_x=[(cTE, cTE.n)])
            m_ffn = ffn0(c, m_u2)
            if stage == 25:
                tap("R_ffn0", R[:, :], [128, NT * CH], BF16)
                _finish(P, cGP, dGP, dSY)
                return _close(nc, P)
            if stage == 24:
                tap("u2", U23[:, :], [128, NT * CH], BF16)
                _finish(P, cGP, dGP, dSY)
                return _close(nc, P)
            m_u3 = 0
            for half in range(2):
                m_u3 = ln_feat(lambda i, h=half: U23[:, CH * i + 512 * h:
                                                     CH * i + 512 * h + 512],
                               512 * half, 512,
                               wait_ve=m_ffn, wait_x=[(cTE, cTE.n)])
            d_rspill = sdma(s_R[:, CH * c:CH * (c + 1)]
                            .rearrange("(i p) q -> p i q", p=128),
                            R.rearrange("p (i q) -> p i q", q=CH),
                            waits=[(cTE, cTE.n), (cVE, m_u3)])
            feeder = qkv_feeder(c, m_u3)
            m_u3_prev = m_u3
        for fn in feeder:
            fn()
        feeder = []
        tap("R_A", R[:, :], [128, NT * CH], BF16)
        if stage < 4:
            _finish(P, cGP, dGP, dSY)
            return _close(nc, P)
        tap("qT", None, None) if False else None

        for h in range(NH):
            cb = (h % 2) * 2048
            hp = (h + 8) % 16
            d_kt = sdma(KB2[:, cb:cb + 2048], s_kT[128 * h:128 * (h + 1), :],
                        waits=[(cTE, cTE.n), (cVE, cVE.n)])
            d_kp = sdma(KB2[:, 4096 + cb:4096 + cb + 2048],
                        s_kT[128 * hp:128 * (hp + 1), :], waits=[])
            d_cs = gdma(U23[:, 0:2048], cosTd[128 * h:128 * (h + 1), :],
                        waits=[(cVE, cVE.n), (cTE, cTE.n)])
            d_sn = gdma(U23[:, 2048:4096], sinTd[128 * h:128 * (h + 1), :],
                        waits=[])
            d_vh = sdma(XC2[:, 4096 + cb:4096 + cb + 2048]
                        .rearrange("p (j q) -> p j q", q=128),
                        s_v[:, 128 * h:128 * (h + 1)]
                        .rearrange("(j p) q -> p j q", p=128),
                        waits=[(cTE, cTE.n), (dGP, dGP.n)])
            ve(lambda E, cb=cb: E.tensor_tensor(
                U23[:, 4096:6144], KB2[:, cb:cb + 2048], U23[:, 0:2048],
                op=ALU.mult),
               waits=[(dSY, d_kp), (dGP, d_sn), (cSC, cSC.n), (cTE, cTE.n)])
            ve(lambda E, cb=cb: E.tensor_tensor(
                MSC[:, 2048:4096], KB2[:, 4096 + cb:4096 + cb + 2048],
                U23[:, 2048:4096], op=ALU.mult))
            mkr = ve(lambda E: E.tensor_tensor(
                U23[:, 4096:6144], U23[:, 4096:6144], MSC[:, 2048:4096],
                op=ALU.add))
            sdma(s_kTr[128 * h:128 * (h + 1), :], U23[:, 4096:6144],
                 waits=[(cVE, mkr)])
            mtr_ev = 0
            for g in range(2):
                mtr = 0
                for jj in range(8):
                    j = 8 * g + jj
                    wtr = [(cVE, mkr)]
                    if jj == 0 and mtr_ev:
                        wtr.append((cSC, mtr_ev))
                    mtr = te(lambda E, j=j, jj=jj: E.transpose(
                        pbt[:, 128 * jj:128 * (jj + 1)],
                        U23[:, 4096 + 128 * j:4096 + 128 * (j + 1)],
                        IDENB[:, :]), waits=wtr)
                mtr_ev = sca(lambda E, g=g, cb=cb: E.copy(
                    XC2[:, cb + 1024 * g:cb + 1024 * (g + 1)], pbt[:, :]),
                    waits=[(cTE, mtr)])
            d_qt = sdma(KB2[:, cb:cb + 2048], s_qT[128 * h:128 * (h + 1), :],
                        waits=[(cVE, cVE.n)])
            d_qp = sdma(KB2[:, 4096 + cb:4096 + cb + 2048],
                        s_qT[128 * hp:128 * (hp + 1), :], waits=[])
            ve(lambda E, cb=cb: E.tensor_tensor(
                U23[:, 6144:8192], KB2[:, cb:cb + 2048], U23[:, 0:2048],
                op=ALU.mult), waits=[(dSY, d_qp)])
            ve(lambda E, cb=cb: E.tensor_tensor(
                MSC[:, 2048:4096], KB2[:, 4096 + cb:4096 + cb + 2048],
                U23[:, 2048:4096], op=ALU.mult))
            mqr = ve(lambda E: E.tensor_tensor(
                U23[:, 6144:8192], U23[:, 6144:8192], MSC[:, 2048:4096],
                op=ALU.add))
            sdma(s_qTr[128 * h:128 * (h + 1), :], U23[:, 6144:8192],
                 waits=[(cVE, mqr)])
            mv1 = ve(lambda E, cb=cb: E.tensor_scalar_min(
                MSC[:, 2048:4096], XC2[:, cb:cb + 2048], 0.0),
                waits=[(cSC, mtr_ev)])
            ms1 = sca(lambda E: E.activation(MSC[:, 2048:4096], MSC[:, 2048:4096],
                                             AF.Exp), waits=[(cVE, mv1)])
            mv2 = ve(lambda E, cb=cb: E.tensor_relu(MSC[:, 0:2048],
                                                    XC2[:, cb:cb + 2048]))
            mkf = ve(lambda E: E.tensor_tensor(MSC[:, 0:2048], MSC[:, 0:2048],
                                               MSC[:, 2048:4096], op=ALU.add),
                     waits=[(cSC, ms1)])
            for j in range(NT):
                w = []
                if j == 0:
                    w = [(cVE, mkf), (dSY, d_vh), (cSC, cSC.n)]
                te(lambda E, j=j, cb=cb: E.matmul(
                    pb[0][:, 0:128], MSC[:, 128 * j:128 * (j + 1)],
                    XC2[:, 4096 + cb + 128 * j:4096 + cb + 128 * (j + 1)],
                    start=(j == 0), stop=(j == NT - 1)), waits=w)
                te(lambda E, j=j: E.matmul(
                    pb[1][0:1, 0:128], PARB[:, B_ones:B_ones + 1],
                    MSC[:, 128 * j:128 * (j + 1)],
                    start=(j == 0), stop=(j == NT - 1)))
            m_mem = cTE.n
            sca(lambda E, h=h: E.copy(U1[:, 128 * h:128 * (h + 1)],
                                      pb[0][:, 0:128]),
                waits=[(cTE, m_mem)] + ([(cVE, cVE.n)] if h == 0 else []))
            m_e2 = sca(lambda E, h=h: E.copy(ROWS[0:1, 128 * h:128 * (h + 1)],
                                             pb[1][0:1, 0:128]),
                       waits=[(cTE, m_mem)])
        d_nw = gdma(s_row[3:4, 0:2048], ROWS[0:1, 0:2048], waits=[(cSC, cSC.n)])
        d_norm = gdma(PARB[:, B_norm:B_norm + 16],
                      s_row[3, 0:2048].rearrange("(h p) -> p h", p=128),
                      waits=[(dGP, d_nw)], slow=True)

        tap("memB", U1[:, 0:2048], [128, 2048], BF16)
        if stage < 5:
            _finish(P, cGP, dGP, dSY)
            return _close(nc, P)

        for qcc in range(2):
            qb0 = 1024 * qcc
            wR = [(cVE, cVE.n), (cTE, cTE.n)] if qcc > 0 else [(dSY, d_rspill)]
            d_rload = sdma(R.rearrange("p (i q) -> p i q", q=CH),
                           s_R[:, qb0:qb0 + CH]
                           .rearrange("(i p) q -> p i q", p=128), waits=wR)
            hq = [(qc2, h) for qc2 in range(2) for h in range(NH)]
            lds = {}
            mk = {}

            def issue_loads(j):
                qc2, h = hq[j]
                qb = qb0 + 512 * qc2
                cb = (h % 2) * 2048
                pm = mk.get(j - 2, {})
                d_kt = sdma(KB2[:, cb:cb + 2048],
                            s_kTr[128 * h:128 * (h + 1), :],
                            waits=[(cTE, pm.get("av", 0))] if pm else
                                  [(cTE, cTE.n)])
                d_vh = sdma(KB2[:, 4096 + cb:4096 + cb + 2048]
                            .rearrange("p (j q) -> p j q", q=128),
                            s_v[:, 128 * h:128 * (h + 1)]
                            .rearrange("(j p) q -> p j q", p=128),
                            waits=([(cTE, pm.get("av", 0))] if pm else
                                   [(cTE, cTE.n), (dGP, dGP.n)]))
                d_qt = sdma(MSC[:, 512 * (h % 2):512 * (h % 2) + 512],
                            s_qTr[128 * h:128 * (h + 1), qb:qb + 512],
                            waits=([(cTE, pm.get("mo", 0)),
                                    (cVE, pm.get("qdve", 0))] if pm else
                                   [(cTE, cTE.n), (cVE, cVE.n)]))
                lds[j] = (d_kt, d_vh, d_qt)

            issue_loads(0)
            issue_loads(1)
            for j, (qc2, h) in enumerate(hq):
                cb = (h % 2) * 2048
                qsl = MSC[:, 512 * (h % 2):512 * (h % 2) + 512]
                qf = MSC[:, 1536 + 512 * (h % 2):1536 + 512 * (h % 2) + 512]
                d_kt, d_vh, d_qt = lds[j]
                pm = mk.get(j - 1, {})
                m = {}
                mv1 = ve(lambda E, q=qsl: E.tensor_scalar_min(
                    MSC[:, 1024:1536], q, 0.0),
                    waits=[(dSY, d_qt), (cSC, pm.get("me2", 0))])
                ms1 = sca(lambda E: E.activation(
                    MSC[:, 1024:1536], MSC[:, 1024:1536], AF.Exp),
                    waits=[(cVE, mv1)])
                mv2 = ve(lambda E, q=qsl, f=qf: E.tensor_relu(f, q))
                m["qdve"] = mv2
                mqf = ve(lambda E, f=qf: E.tensor_tensor(
                    f, f, MSC[:, 1024:1536], op=ALU.add), waits=[(cSC, ms1)])
                m_exps = []
                mmks = []

                def emit_S(kt):
                    bank = pb[kt % 3]
                    w = []
                    if kt == 0:
                        w = [(dSY, d_kt), (cSC, pm.get("exp15", 0)),
                             (cVE, pm.get("comb", 0))]
                    elif kt in (1, 2):
                        w = [(cSC, pm.get("exp15", 0)),
                             (cVE, pm.get("comb", 0))]
                    else:
                        w = [(cSC, m_exps[kt - 3])]
                    mmk = te(lambda E, kt=kt, b=bank, cb=cb, q=qsl: E.matmul(
                        b[:, :], KB2[:, cb + 128 * kt:cb + 128 * (kt + 1)],
                        q, start=True, stop=True), waits=w)
                    if kt % 3 == 2 or kt == NT - 1:
                        k0 = kt - (kt % 3)
                        nb = kt - k0 + 1
                        me = sca(lambda E, k0=k0, nb=nb: E.activation(
                            XC2[:, 512 * k0:512 * (k0 + nb)],
                            PS3[:, 0:512 * nb], AF.Exp,
                            bias=CAPS[:, 4:5], scale=CAPS[:, 5:6]),
                            waits=[(cTE, mmk)])
                        for _ in range(nb):
                            m_exps.append(me)

                for kt in range(3):
                    emit_S(kt)
                for kt in range(NT):
                    if kt + 3 < NT:
                        emit_S(kt + 3)
                    w = [(cSC, m_exps[kt])]
                    if kt == 0:
                        w += [(dSY, d_vh), (cSC, pm.get("me1", 0)),
                              (cVE, pm.get("mr1", 0))]
                    te(lambda E, kt=kt, cb=cb: E.matmul(
                        pb[3][:, :],
                        KB2[:, 4096 + cb + 128 * kt:4096 + cb + 128 * (kt + 1)],
                        XC2[:, 512 * kt:512 * (kt + 1)],
                        start=(kt == 0), stop=(kt == NT - 1)), waits=w)
                    te(lambda E, kt=kt: E.matmul(
                        pb[4][0:1, :], PARB[:, B_ones:B_ones + 1],
                        XC2[:, 512 * kt:512 * (kt + 1)],
                        start=(kt == 0), stop=(kt == NT - 1)))
                m["exp15"] = m_exps[15]
                m["exp14"] = m_exps[14]
                m["av"] = cTE.n
                te(lambda E, h=h, f=qf: E.matmul(
                    pb[5][:, :], U1[:, 128 * h:128 * (h + 1)], f,
                    start=True, stop=True),
                   waits=[(cVE, mqf), (dGP, d_norm), (cSC, pm.get("me2", 0)),
                          (cVE, pm.get("mr2", 0))])
                m_mo = te(lambda E, h=h, f=qf: E.matmul(
                    pb[6][0:1, :], PARB[:, B_norm + h:B_norm + h + 1], f,
                    start=True, stop=True))
                m["mo"] = m_mo
                if j + 2 < len(hq):
                    issue_loads(j + 2)
                mr1 = ve(lambda E: E.reciprocal(ROWS[0:1, 0:512],
                                                pb[4][0:1, :]),
                         waits=[(cTE, m_mo)])
                m["mr1"] = mr1
                mr2 = ve(lambda E: E.reciprocal(ROWS[0:1, 512:1024],
                                                pb[6][0:1, :]))
                m["mr2"] = mr2
                mrb = ve(lambda E, h=h: E.tensor_scalar(
                    ROWS[0:1, 1024:1536], ROWS[0:1, 512:1024],
                    GS[0:1, h:h + 1], None, op0=ALU.mult),
                    waits=[(cTE, pm.get("rep", 0))])
                mrb = ve(lambda E, h=h: E.tensor_scalar(
                    ROWS[0:1, 1536:2048], ROWS[0:1, 0:512],
                    GS[0:1, 16 + h:16 + h + 1], None, op0=ALU.mult),
                    waits=[(cVE, m_g1m)])
                te(lambda E: E.matmul(
                    pb[0][:, :], ONER[0:1, :], ROWS[0:1, 1024:1536],
                    start=True, stop=True),
                   waits=[(cVE, mrb), (cSC, m_exps[15])])
                m["rep"] = te(lambda E: E.matmul(
                    pb[1][:, :], ONER[0:1, :], ROWS[0:1, 1536:2048],
                    start=True, stop=True), waits=[(cSC, m_exps[13])])
                m["sc"] = m["rep"]
                me1 = sca(lambda E: E.copy(MSC[:, 2560:3072], pb[3][:, :]),
                          waits=[(cTE, m["rep"])])
                m["me1"] = me1
                me2 = sca(lambda E: E.copy(MSC[:, 3072:3584], pb[5][:, :]))
                m["me2"] = me2
                mt1 = ve(lambda E: E.tensor_tensor(
                    MSC[:, 3584:4096], MSC[:, 2560:3072], pb[1][:, :],
                    op=ALU.mult), waits=[(cSC, me2), (cTE, m["rep"])])
                ve(lambda E: E.tensor_tensor(
                    MSC[:, 2560:3072], MSC[:, 3072:3584], pb[0][:, :],
                    op=ALU.mult))
                m["comb"] = ve(lambda E, h=h, q=qc2: E.tensor_tensor(
                    BC4[:, 1024 * h + 512 * q:1024 * h + 512 * (q + 1)],
                    MSC[:, 3584:4096], MSC[:, 2560:3072], op=ALU.add))
                mk[j] = m
            m_comb = cVE.n
            lastv = 0
            for o in range(NT):
                kb = KB2[:, (o % 2) * 2048:(o % 2) * 2048 + 2048]
                dW = sdma(kb.rearrange("p (i q) -> p i q", q=128),
                          s_woT[:, 128 * o:128 * (o + 1)]
                          .rearrange("(i p) q -> p i q", p=128),
                          waits=[(cTE, cTE.n)])
                for half in range(2):
                    bank = pb[2 + half]
                    for i in range(NT):
                        w = []
                        if i == 0:
                            w = [(dSY, dW), (cVE, max(m_comb, lastv)),
                                 (cSC, cSC.n)]
                        te(lambda E, i=i, h=half, b=bank, kb=kb: E.matmul(
                            b[:, :], kb[:, 128 * i:128 * (i + 1)],
                            BC4[:, 1024 * i + 512 * h:1024 * i + 512 * (h + 1)],
                            start=(i == 0), stop=(i == NT - 1)), waits=w)
                    m_mm = cTE.n
                    rsl = R[:, CH * o + 512 * half:CH * o + 512 * half + 512]
                    lastv = ve(lambda E, r=rsl, b=bank: E.tensor_tensor(
                        r, r, b[:, :], op=ALU.add),
                        waits=[(cTE, m_mm), (dSY, d_rload)])
            m_wo = lastv
            m_u4 = 0
            for half in range(2):
                m_u4 = ln_feat(lambda i, h=half: U23[:, CH * i + 512 * h:
                                                     CH * i + 512 * h + 512],
                               512 * half, 512,
                               wait_ve=m_wo, wait_x=[(cTE, cTE.n)])
            mx = ve(lambda E: E.tensor_tensor(F32A[:, 2048:3072], U23[:, 0:CH],
                                              U23[:, 0:CH], op=ALU.mult),
                    waits=[(cVE, m_u4)])
            for i in range(1, NT):
                ve(lambda E, i=i: E.tensor_tensor(
                    F32B[:, 1024:2048], U23[:, CH * i:CH * i + CH],
                    U23[:, CH * i:CH * i + CH], op=ALU.mult))
                mx = ve(lambda E: E.tensor_tensor(
                    F32A[:, 2048:3072], F32A[:, 2048:3072], F32B[:, 1024:2048],
                    op=ALU.max))
            width = 64
            while width >= 1:
                dsh = gdma(F32B[0:width, 0:1024], F32A[width:2 * width, 2048:3072],
                           waits=[(cVE, mx)])
                mx = ve(lambda E, w=width: E.tensor_tensor(
                    F32A[0:w, 2048:3072], F32A[0:w, 2048:3072],
                    F32B[0:w, 0:1024], op=ALU.max), waits=[(dGP, dsh)])
                width //= 2
            ms = sca(lambda E: E.activation(ROWS[0:1, 0:1024],
                                            F32A[0:1, 2048:3072], AF.Sqrt),
                     waits=[(cVE, mx), (cTE, cTE.n)])
            mv = ve(lambda E: E.tensor_scalar(ROWS[0:1, 1024:2048],
                                              ROWS[0:1, 0:1024], 1.0 / 127.0,
                                              1e-12, op0=ALU.mult, op1=ALU.max),
                    waits=[(cSC, ms)])
            mvr = ve(lambda E: E.reciprocal(ROWS[0:1, 2048:3072],
                                            ROWS[0:1, 1024:2048]))
            for j in range(2):
                te(lambda E, j=j: E.matmul(
                    pb[3 + j][:, :], ONER[0:1, :],
                    ROWS[0:1, 1024 + 512 * j:1024 + 512 * (j + 1)],
                    start=True, stop=True),
                   waits=[(cVE, mvr), (cSC, cSC.n)] if j == 0 else [])
                te(lambda E, j=j: E.matmul(
                    pb[5 + j][:, :], ONER[0:1, :],
                    ROWS[0:1, 2048 + 512 * j:2048 + 512 * (j + 1)],
                    start=True, stop=True))
            m_reps = cTE.n
            m_xsr = 0
            for j in range(2):
                ve(lambda E, j=j: E.tensor_copy(F32A[:, 512 * j:512 * (j + 1)],
                                                pb[3 + j][:, :]),
                   waits=[(cTE, m_reps)])
                m_xsr = ve(lambda E, j=j: E.tensor_copy(
                    F32A[:, 1024 + 512 * j:1024 + 512 * (j + 1)],
                    pb[5 + j][:, :]))
            m_xq = 0
            for i in range(NT):
                ve(lambda E, i=i: E.scalar_tensor_tensor(
                    F32B[:, 0:1024], U23[:, CH * i:CH * i + CH], 1.0,
                    F32A[:, 1024:2048], op0=ALU.mult, op1=ALU.mult),
                   waits=[(cVE, m_xsr)] if i == 0 else [])
                m_xq = ve(lambda E, i=i: E.tensor_scalar(
                    U23[:, CH * i:CH * i + CH], F32B[:, 0:1024], MAGIC, MAGIC,
                    op0=ALU.add, op1=ALU.subtract))
            tr_marks = {}
            for o in range(NT):
                kb = KB2[:, (o % 2) * 2048:(o % 2) * 2048 + 2048]
                dW = sdma(kb.rearrange("p (i q) -> p i q", q=128),
                          s_wq1[:, 128 * o:128 * (o + 1)]
                          .rearrange("(i p) q -> p i q", p=128),
                          waits=[(cTE, cTE.n)])
                for half in range(2):
                    bank = pb[half]
                    for i in range(NT):
                        w = []
                        if i == 0:
                            w = [(dSY, dW), (cVE, max(m_xq, cVE.n)), (cSC, cSC.n)]
                        te(lambda E, i=i, h=half, b=bank, kb=kb: E.matmul(
                            b[:, :], kb[:, 128 * i:128 * (i + 1)],
                            U23[:, CH * i + 512 * h:CH * i + 512 * (h + 1)],
                            start=(i == 0), stop=(i == NT - 1)), waits=w)
                    m_mm = cTE.n
                    ve(lambda E, o=o, b=bank, h=half: E.scalar_tensor_tensor(
                        F32B[:, 0:512], b[:, :],
                        PAR[:, P_ws1 + o:P_ws1 + o + 1],
                        F32A[:, 512 * h:512 * (h + 1)],
                        op0=ALU.mult, op1=ALU.mult), waits=[(cTE, m_mm)])
                    rsl = R[:, CH * o + 512 * half:CH * o + 512 * half + 512]
                    ve(lambda E, o=o, r=rsl, h=half: E.scalar_tensor_tensor(
                        OUTS[:, 512 * h:512 * (h + 1)], F32B[:, 0:512],
                        PAR[:, P_b1 + o:P_b1 + o + 1], r,
                        op0=ALU.add, op1=ALU.add))
                m_row = cVE.n
                for tt in range(8):
                    bank = pb[3 + tt % 2]
                    w = [(cVE, m_row)]
                    if (o, tt) != (0, 0):
                        pass
                    if tt >= 2 or o > 0:
                        prev = tr_marks.get((o, tt - 2)) or \
                            tr_marks.get((o - 1, tt + 6))
                        if prev:
                            w.append((cGP, prev))
                    mtr = te(lambda E, tt=tt, b=bank: E.transpose(
                        b[:, 0:128], OUTS[:, 128 * tt:128 * (tt + 1)],
                        IDEN[:, :]), waits=w)
                    msc = sca(lambda E, tt=tt, b=bank: E.copy(
                        F32A[:, 3072 + 128 * (tt % 2):3072 + 128 * (tt % 2) + 128],
                        b[:, 0:128]), waits=[(cTE, mtr)])
                    P.op("gp", lambda E, tt=tt, o=o, q=qb0: E.dma_start(
                        out=out[q + 128 * tt:q + 128 * (tt + 1),
                                128 * o:128 * (o + 1)],
                        in_=F32A[:, 3072 + 128 * (tt % 2):
                                 3072 + 128 * (tt % 2) + 128]),
                         waits=[(cSC, msc)], inc=cGP, k=16)
                    tr_marks[(o, tt)] = cGP.n
        _finish(P, cGP, dGP, dSY)
        return _close(nc, P)


def _cols16(v):
    return np.ascontiguousarray(np.asarray(v).reshape(16, 128).T)


def _host_inputs(inp):
    f32 = np.float32
    m = {}
    m["convw"] = np.ascontiguousarray(
        np.asarray(inp["L0_attn_conv_w"]).reshape(16, 128, 4)
        .transpose(1, 0, 2).reshape(128, 64)).astype(f32)
    m["convb"] = _cols16(inp["L0_attn_conv_b"]).astype(f32)
    m["xprojT"] = np.ascontiguousarray(np.asarray(inp["L0_attn_xproj_w"]).T).astype(f32)
    m["dtprojT"] = np.ascontiguousarray(np.asarray(inp["L0_attn_dtproj_w"]).T).astype(f32)
    m["dtprojb"] = _cols16(inp["L0_attn_dtproj_b"]).astype(f32)
    m["alog"] = np.ascontiguousarray(
        np.asarray(inp["L0_attn_A_log"]).reshape(16, 128, 16)
        .transpose(1, 0, 2).reshape(128, 256)).astype(f32)
    m["dpar"] = _cols16(inp["L0_attn_D_param"]).astype(f32)
    m["w0T"] = np.ascontiguousarray(np.asarray(inp["L0_ffn_weight"]).T).astype(f32)
    m["b0c"] = _cols16(inp["L0_ffn_bias"]).astype(f32)
    m["wqT"] = np.ascontiguousarray(np.asarray(inp["L1_attn_Wq"]).T).astype(f32)
    m["wkT"] = np.ascontiguousarray(np.asarray(inp["L1_attn_Wk"]).T).astype(f32)
    m["wvT"] = np.ascontiguousarray(np.asarray(inp["L1_attn_Wv"]).T).astype(f32)
    m["woT"] = np.ascontiguousarray(np.asarray(inp["L1_attn_Wo"]).T).astype(f32)
    m["gate"] = np.ascontiguousarray(np.asarray(inp["L1_attn_gate"]).reshape(1, 16)).astype(f32)
    m["w1T"] = np.ascontiguousarray(np.asarray(inp["L1_ffn_weight"]).T).astype(f32)
    m["b1c"] = _cols16(inp["L1_ffn_bias"]).astype(f32)
    t = np.arange(T, dtype=np.float64)
    invf = 1.0 / (10000.0 ** (np.arange(0, D, 2, dtype=np.float64) / D))
    ang = invf[:, None] * t[None, :]
    m["cosT"] = np.concatenate([np.cos(ang), np.cos(ang)], 0).astype(f32)
    m["sinT"] = np.concatenate([-np.sin(ang), np.sin(ang)], 0).astype(f32)
    m["ident"] = np.eye(128, dtype=f32)
    return m


def kernel(**inputs):
    shared = _host_inputs(inputs)
    x = np.asarray(inputs["L0_attn_x"])
    nc = build_program()
    in_maps = []
    for b in range(8):
        mm = dict(shared)
        mm["xT"] = np.ascontiguousarray(x[b].T).astype(np.float32)
        in_maps.append(mm)
    res = run_bass_kernel_spmd(nc, in_maps, core_ids=list(range(8)))
    return np.stack([np.asarray(r["y"], np.float32) for r in res.results], 0)


# version 10
# speedup vs baseline: 1.0538x; 1.0538x over previous
import numpy as np
import ml_dtypes
import concourse.bass as bass
import concourse.mybir as mybir
from concourse.bass_utils import run_bass_kernel_spmd
from contextlib import ExitStack

F32 = mybir.dt.float32
BF16 = mybir.dt.bfloat16
AF = mybir.ActivationFunctionType
ALU = mybir.AluOpType

T = 2048; D = 2048; NT = 16; NST = 16; NH = 16
CH = 1024; NCH = T // CH
SC = 512; NSC = T // SC
SCPC = CH // SC
MAGIC = 12582912.0
INVSQ = float(1.0 / np.sqrt(128.0))
SOFTBIAS = -20.0
UPAD = 8
UW = SC + UPAD


class Ctr:
    def __init__(self, name):
        self.name = name
        self.n = 0


class Prog:
    def __init__(self):
        self.items = {e: [] for e in ("gp", "sy", "te", "ve", "sc")}

    def op(self, eng, fn, waits=(), inc=None, k=1):
        wl = [(c.name, v) for (c, v) in waits if v and v > 0]
        tgt = None
        if inc is not None:
            inc.n += k
            tgt = (inc.name, k)

        def run(E, sems):
            for nm, v in wl:
                E.wait_ge(sems[nm], v)
            ins = fn(E)
            if tgt is not None:
                ins.then_inc(sems[tgt[0]], tgt[1])
        self.items[eng].append(run)


def _finish(P, cGP, dGP, dSY):
    P.op("gp", lambda E: None, waits=[(cGP, cGP.n), (dGP, dGP.n)])
    P.op("sy", lambda E: None, waits=[(cGP, cGP.n), (dSY, dSY.n)])


def _close(nc, P):
    with (nc.semaphore("sdGP") as s0, nc.semaphore("sdSY") as s1,
          nc.semaphore("scTE") as s2, nc.semaphore("scVE") as s3,
          nc.semaphore("scSC") as s4, nc.semaphore("scGP") as s5,
          nc.Block() as block):
        sems = {"dGP": s0, "dSY": s1, "cTE": s2, "cVE": s3,
                "cSC": s4, "cGP": s5}

        @block.gpsimd
        def _(E):
            for fn in P.items["gp"]:
                fn(E, sems)

        @block.sync
        def _(E):
            for fn in P.items["sy"]:
                fn(E, sems)

        @block.tensor
        def _(E):
            for fn in P.items["te"]:
                fn(E, sems)

        @block.vector
        def _(E):
            for fn in P.items["ve"]:
                fn(E, sems)

        @block.scalar
        def _(E):
            for fn in P.items["sc"]:
                fn(E, sems)

    return nc


def build_program(taps=(), stage=99):
    nc = bass.Bass("TRN2", target_bir_lowering=False,
                   detect_race_conditions=False)
    P = Prog()
    taps = set(taps)

    def din(name, shape, dt=F32):
        return nc.dram_tensor(name, shape, dt, kind="ExternalInput").ap()

    xT = din("xT", [D, T])
    convw = din("convw", [128, 4 * NT])
    convb = din("convb", [128, NT])
    xprojT_d = din("xprojT", [D, 160])
    dtprojT_d = din("dtprojT", [128, D])
    dtprojb = din("dtprojb", [128, NT])
    alog = din("alog", [128, NST * NT])
    dpar = din("dpar", [128, NT])
    w0T = din("w0T", [D, D])
    b0c = din("b0c", [128, NT])
    wqT = din("wqT", [D, D])
    wkT = din("wkT", [D, D])
    wvT = din("wvT", [D, D])
    woT = din("woT", [D, D])
    gate = din("gate", [1, NH])
    w1T = din("w1T", [D, D])
    b1c = din("b1c", [128, NT])
    cosTd = din("cosT", [D, T])
    sinTd = din("sinT", [D, T])
    identd = din("ident", [128, 128])
    out = nc.dram_tensor("y", [T, D], F32, kind="ExternalOutput").ap()

    def scr(name, shape, dt):
        return nc.dram_tensor(name, shape, dt, kind="Internal").ap()

    s_bc = scr("s_bc", [NSC, 32, SC], BF16)
    s_sign = scr("s_sign", [D, D], BF16)
    s_wq1 = scr("s_wq1", [D, D], BF16)
    s_woT = scr("s_woT", [D, D], BF16)
    s_qT = scr("s_qT", [D, T], BF16)
    s_kT = scr("s_kT", [D, T], BF16)
    s_v = scr("s_v", [T, D], BF16)
    s_row = scr("s_row", [8, 2048], F32)
    s_R = scr("s_R", [D, T], BF16)
    s_qTr = scr("s_qTr", [D, T], BF16)
    s_kTr = scr("s_kTr", [D, T], BF16)

    dGP = Ctr("dGP"); dSY = Ctr("dSY")
    cTE = Ctr("cTE"); cVE = Ctr("cVE"); cSC = Ctr("cSC"); cGP = Ctr("cGP")

    tap_info = {}

    with ExitStack() as ctx:
        def sbuf(name, shape, dt):
            return ctx.enter_context(nc.sbuf_tensor(name, shape, dt))

        R = sbuf("R", [128, NT * CH], BF16)
        U1 = sbuf("U1", [128, NT * UW], BF16)
        U23 = sbuf("U23", [128, NT * CH], BF16)
        BC4 = sbuf("BC4", [128, 16384], BF16)
        XC2 = sbuf("XC2", [128, NT * SC], BF16)
        KB2 = sbuf("KB2", [128, 8192], BF16)
        MSC = sbuf("MSC", [128, 4096], BF16)
        F32A = sbuf("F32A", [128, 3584], F32)
        F32B = sbuf("F32B", [128, 2048], F32)
        PAR = sbuf("PAR", [128, 768], F32)
        PARB = sbuf("PARB", [128, 4640], BF16)
        ROWS = sbuf("ROWS", [1, 4096], F32)
        OUTS = sbuf("OUTS", [128, 1024], F32)
        IDEN = sbuf("IDEN", [128, 128], F32)
        ONER = sbuf("ONER", [1, 128], F32)
        IDENB = sbuf("IDENB", [128, 128], BF16)
        GS = sbuf("GS", [1, 32], F32)
        CAPS = sbuf("CAPS", [128, 8], F32)

        PS3 = ctx.enter_context(nc.psum_tensor("PS3", [128, 1536], F32))
        pb = [PS3[:, 512 * i:512 * (i + 1)] for i in range(3)]
        pb += [ctx.enter_context(nc.psum_tensor(f"pb{i}", [128, 512], F32))
               for i in range(3, 7)]
        pbt = ctx.enter_context(nc.psum_tensor("pbt", [128, 1024], BF16))

        P_convw = 0; P_convb = 64; P_dtb = 80; P_dpar = 96
        P_b0 = 112; P_b1 = 128; P_alpha = 144; P_ws1 = 160
        P_aneg = 176
        P_hc = 432
        P_meanf = 688
        B_dtp = 0
        B_xp = 2048
        B_ones = 4608
        B_mean = 4610
        B_norm = 4612


        def gdma(dst, src, waits=(), slow=False):
            P.op("gp", lambda E, d=dst, s=src, sl=slow: E.dma_start(
                out=d, in_=s, allow_slow_non_contiguous=sl),
                 waits=waits, inc=dGP, k=16)
            return dGP.n

        def sdma(dst, src, waits=(), transpose=False):
            P.op("sy", lambda E, d=dst, s=src, t=transpose:
                 E.dma_start(out=d, in_=s, transpose=t),
                 waits=waits, inc=dSY, k=16)
            return dSY.n

        def te(fn, waits=()):
            P.op("te", fn, waits=waits, inc=cTE)
            return cTE.n

        def ve(fn, waits=()):
            P.op("ve", fn, waits=waits, inc=cVE)
            return cVE.n

        def sca(fn, waits=()):
            P.op("sc", fn, waits=waits, inc=cSC)
            return cSC.n

        def tap(name, ap, shape, dt=F32):
            if name not in taps:
                return
            t = nc.dram_tensor("tap_" + name, shape, dt, kind="ExternalOutput").ap()
            tap_info[name] = True
            w = [(cTE, cTE.n), (cVE, cVE.n), (cSC, cSC.n), (dSY, dSY.n)]
            gdma(t, ap, waits=w)

        gdma(PARB[:, B_dtp:B_dtp + 2048], dtprojT_d[:, :])
        gdma(PARB[:, B_xp:B_xp + 2560].rearrange("p (i q) -> p i q", q=160),
             xprojT_d.rearrange("(i p) q -> p i q", p=128))
        gdma(PAR[:, P_convw:P_convw + 64], convw[:, :])
        gdma(PAR[:, P_convb:P_convb + 16], convb[:, :])
        gdma(PAR[:, P_dtb:P_dtb + 16], dtprojb[:, :])
        gdma(PAR[:, P_dpar:P_dpar + 16], dpar[:, :])
        gdma(PAR[:, P_b0:P_b0 + 16], b0c[:, :])
        gdma(PAR[:, P_b1:P_b1 + 16], b1c[:, :])
        gdma(PAR[:, P_aneg:P_aneg + 256], alog[:, :])
        gdma(IDEN[:, :], identd[:, :])
        gdma(IDENB[:, :], identd[:, :])
        gdma(ROWS[0:1, 0:16], gate[:, :])
        d_par = dGP.n
        sca(lambda E: E.activation(PAR[:, P_aneg:P_aneg + 256],
                                   PAR[:, P_aneg:P_aneg + 256], AF.Exp),
            waits=[(dGP, d_par)])
        ve(lambda E: E.tensor_scalar_mul(PAR[:, P_aneg:P_aneg + 256],
                                         PAR[:, P_aneg:P_aneg + 256], -1.0),
           waits=[(cSC, cSC.n)])
        ve(lambda E: E.memset(PARB[:, B_ones:B_ones + 1], 1.0))
        ve(lambda E: E.memset(ONER[0:1, :], 1.0))
        ve(lambda E: E.memset(PAR[:, P_meanf:P_meanf + 1], 1.0 / 2048.0))
        ve(lambda E: E.memset(CAPS[:, 0:1], 1e-5))
        ve(lambda E: E.memset(CAPS[:, 1:2], -0.5))
        ve(lambda E: E.memset(CAPS[:, 2:3], -1.0))
        ve(lambda E: E.memset(CAPS[:, 3:4], 1.0))
        ve(lambda E: E.memset(CAPS[:, 4:5], SOFTBIAS))
        ve(lambda E: E.memset(CAPS[:, 5:6], INVSQ))
        m_ones = ve(lambda E: E.memset(PARB[:, B_mean:B_mean + 1], 1.0 / 2048.0))
        ve(lambda E: E.memset(PAR[:, P_hc:P_hc + 256], 0.0))
        m_ut0 = ve(lambda E: E.memset(U1[:, :], 0.0))
        m_gs = sca(lambda E: E.activation(GS[0:1, 0:16], ROWS[0:1, 0:16],
                                          AF.Sigmoid), waits=[(dGP, d_par)])
        m_g1m = ve(lambda E: E.tensor_scalar(GS[0:1, 16:32], GS[0:1, 0:16],
                                             -1.0, 1.0, op0=ALU.mult, op1=ALU.add),
                   waits=[(cSC, m_gs)])

        for i in range(NT):
            kb = KB2[:, (i % 2) * 2048:(i % 2) * 2048 + 2048]
            dW = gdma(kb, w0T[128 * i:128 * (i + 1), :],
                      waits=[(cVE, cVE.n), (cSC, cSC.n)])
            m1 = ve(lambda E, b=kb: E.tensor_scalar(
                MSC[:, 0:2048], b, 0.0, None, op0=ALU.is_ge),
                waits=[(dGP, dW), (dSY, dSY.n)])
            m2 = ve(lambda E: E.tensor_scalar(
                MSC[:, 0:2048], MSC[:, 0:2048], 2.0, -1.0,
                op0=ALU.mult, op1=ALU.add))
            sdma(s_sign[128 * i:128 * (i + 1), :], MSC[:, 0:2048],
                 waits=[(cVE, m2)])
            ma = sca(lambda E, b=kb: E.activation(MSC[:, 2048:4096], b, AF.Abs),
                     waits=[(dGP, dW), (cTE, cTE.n)])
            for j in range(4):
                te(lambda E, j=j, i=i: E.matmul(
                    pb[3 + j][0:1, :], PARB[:, B_mean:B_mean + 1],
                    MSC[:, 2048 + 512 * j:2048 + 512 * (j + 1)],
                    start=(i == 0), stop=(i == NT - 1)),
                   waits=[(cSC, ma), (cVE, m_ones)])
        m_amm = cTE.n
        m_acp = 0
        for j in range(4):
            m_acp = sca(lambda E, j=j: E.copy(ROWS[0:1, 512 * j:512 * (j + 1)],
                                              pb[3 + j][0:1, :]),
                        waits=[(cTE, m_amm)])
        d_aw = gdma(s_row[1:2, 0:2048], ROWS[0:1, 0:2048], waits=[(cSC, m_acp)])
        gdma(PAR[:, P_alpha:P_alpha + 16],
             s_row[1, 0:2048].rearrange("(i p) -> p i", p=128),
             waits=[(dGP, d_aw)], slow=True)

        def w1_prep():
            mxm = 0
            for i in range(NT):
                kb = KB2[:, (i % 2) * 2048:(i % 2) * 2048 + 2048]
                dW = gdma(kb, w1T[128 * i:128 * (i + 1), :],
                          waits=[(cVE, cVE.n), (cSC, cSC.n)])
                ma = sca(lambda E, b=kb: E.activation(MSC[:, 0:2048], b, AF.Abs),
                         waits=[(dGP, dW), (cVE, mxm)])
                if i == 0:
                    mxm = ve(lambda E: E.tensor_copy(F32A[:, 0:2048], MSC[:, 0:2048]),
                             waits=[(cSC, ma)])
                else:
                    mxm = ve(lambda E: E.tensor_tensor(
                        F32A[:, 0:2048], F32A[:, 0:2048], MSC[:, 0:2048], op=ALU.max),
                        waits=[(cSC, ma)])
            width = 64
            while width >= 1:
                dsh = gdma(F32B[0:width, 0:2048], F32A[width:2 * width, 0:2048],
                           waits=[(cVE, mxm)])
                mxm = ve(lambda E, w=width: E.tensor_tensor(
                    F32A[0:w, 0:2048], F32A[0:w, 0:2048], F32B[0:w, 0:2048],
                    op=ALU.max), waits=[(dGP, dsh)])
                width //= 2
            m_ws = ve(lambda E: E.tensor_scalar(ROWS[0:1, 0:2048],
                                                F32A[0:1, 0:2048], 1e-8, 1.0 / 127.0,
                                                op0=ALU.max, op1=ALU.mult),
                      waits=[(dGP, dGP.n)])
            m_wsr = ve(lambda E: E.reciprocal(ROWS[0:1, 2048:4096],
                                              ROWS[0:1, 0:2048]),
                       waits=[(cVE, m_ws)])
            d_ww = gdma(s_row[2:3, 0:2048], ROWS[0:1, 0:2048], waits=[(cVE, m_ws)])
            gdma(PAR[:, P_ws1:P_ws1 + 16],
                 s_row[2, 0:2048].rearrange("(i p) -> p i", p=128),
                 waits=[(dGP, d_ww)], slow=True)
            for j in range(4):
                te(lambda E, j=j: E.matmul(
                    pb[j][:, :], ONER[0:1, :],
                    ROWS[0:1, 2048 + 512 * j:2048 + 512 * (j + 1)],
                    start=True, stop=True),
                   waits=[(cVE, m_wsr), (cVE, m_ones), (cSC, cSC.n)])
            m_wmm = cTE.n
            m_wcp = 0
            for j in range(4):
                m_wcp = ve(lambda E, j=j: E.tensor_copy(
                    F32B[:, 512 * j:512 * (j + 1)], pb[j][:, :]),
                    waits=[(cTE, m_wmm)])
            for i in range(NT):
                kb = KB2[:, (i % 2) * 2048:(i % 2) * 2048 + 2048]
                dW = gdma(kb, w1T[128 * i:128 * (i + 1), :],
                          waits=[(cVE, cVE.n), (dSY, dSY.n)])
                mq1 = ve(lambda E, b=kb: E.tensor_tensor(
                    F32A[:, 0:2048], b, F32B[:, 0:2048], op=ALU.mult),
                    waits=[(dGP, dW), (cVE, m_wcp)])
                mq2 = ve(lambda E: E.tensor_scalar(
                    MSC[:, 0:2048], F32A[:, 0:2048], MAGIC, MAGIC,
                    op0=ALU.add, op1=ALU.subtract))
                sdma(s_wq1[128 * i:128 * (i + 1), :], MSC[:, 0:2048],
                     waits=[(cVE, mq2)])

        for i in range(NT):
            kb = KB2[:, (i % 2) * 2048:(i % 2) * 2048 + 2048]
            dW = gdma(kb, woT[128 * i:128 * (i + 1), :], waits=[(dSY, dSY.n)])
            sdma(s_woT[128 * i:128 * (i + 1), :], kb, waits=[(dGP, dW)])
        m_prep_ve = cVE.n
        m_prep_sc = cSC.n
        m_prep_sy = dSY.n

        sq_marks = []

        def ln_feat(dst_cols, src_col, width, wait_ve=0, wait_x=()):
            sqs = [0, 512]
            mm_sq = {}
            for i in range(NT):
                rsl = R[:, CH * i + src_col:CH * i + src_col + width]
                w_sq = []
                if i >= 2:
                    w_sq = [(cTE, mm_sq[i - 2])]
                msq = ve(lambda E, r=rsl, s=sqs[i % 2]: E.tensor_tensor(
                    F32B[:, s:s + width], r, r, op=ALU.mult),
                    waits=w_sq + (list(wait_x) if i == 0 else []))
                w_mm = []
                if i == 0:
                    w_mm = [(cVE, max(wait_ve, cVE.n)), (cSC, cSC.n)] + list(wait_x)
                te(lambda E, r=rsl, i=i: E.matmul(
                    pb[3][0:1, 0:width], PARB[:, B_mean:B_mean + 1], r,
                    start=(i == 0), stop=(i == NT - 1)), waits=w_mm)
                mm_sq[i] = te(lambda E, s=sqs[i % 2], i=i: E.matmul(
                    pb[4][0:1, 0:width], PAR[:, P_meanf:P_meanf + 1],
                    F32B[:, s:s + width],
                    start=(i == 0), stop=(i == NT - 1)), waits=[(cVE, msq)])
            m_stat = cTE.n
            mvc = ve(lambda E: E.tensor_copy(ROWS[0:1, 1024:1024 + width],
                                             pb[3][0:1, 0:width]),
                     waits=[(cTE, cTE.n)])
            mv = ve(lambda E: E.tensor_tensor(
                ROWS[0:1, 0:width], ROWS[0:1, 1024:1024 + width],
                ROWS[0:1, 1024:1024 + width], op=ALU.mult))
            mv = ve(lambda E: E.tensor_tensor(
                ROWS[0:1, 0:width], pb[4][0:1, 0:width], ROWS[0:1, 0:width],
                op=ALU.subtract))
            ms = sca(lambda E: E.activation(ROWS[0:1, 512:512 + width],
                                            ROWS[0:1, 0:width],
                                            AF.Ln, bias=CAPS[0:1, 0:1]),
                     waits=[(cVE, mv), (cTE, cTE.n)])
            ms = sca(lambda E: E.activation(ROWS[0:1, 512:512 + width],
                                            ROWS[0:1, 512:512 + width],
                                            AF.Exp, scale=CAPS[0:1, 1:2]))
            mn = ve(lambda E: E.tensor_scalar(ROWS[0:1, 1536:1536 + width],
                                              ROWS[0:1, 0:width], 1e-5, None,
                                              op0=ALU.add), waits=[(cSC, ms)])
            mn = ve(lambda E: E.tensor_tensor(
                ROWS[0:1, 1536 + width:1536 + 2 * width] if width <= 256 else
                ROWS[0:1, 0:width],
                ROWS[0:1, 512:512 + width], ROWS[0:1, 512:512 + width],
                op=ALU.mult))
            mn = ve(lambda E: E.tensor_tensor(
                ROWS[0:1, 0:width], ROWS[0:1, 0:width],
                ROWS[0:1, 1536:1536 + width], op=ALU.mult))
            mn = ve(lambda E: E.tensor_scalar(
                ROWS[0:1, 0:width], ROWS[0:1, 0:width], -0.5, 1.5,
                op0=ALU.mult, op1=ALU.add))
            mn = ve(lambda E: E.tensor_tensor(
                ROWS[0:1, 512:512 + width], ROWS[0:1, 512:512 + width],
                ROWS[0:1, 0:width], op=ALU.mult))
            te(lambda E: E.matmul(pb[5][:, 0:width], ONER[0:1, :],
                                  ROWS[0:1, 1024:1024 + width],
                                  start=True, stop=True), waits=[(cVE, mvc)])
            m_bc = te(lambda E: E.matmul(
                pb[6][:, 0:width], ONER[0:1, :],
                ROWS[0:1, 512:512 + width], start=True, stop=True),
                waits=[(cVE, mn)])
            last = 0
            for i in range(NT):
                rsl = R[:, CH * i + src_col:CH * i + src_col + width]
                dsl = dst_cols(i)
                ve(lambda E, r=rsl, d=dsl: E.tensor_tensor(
                    d, r, pb[5][:, 0:width], op=ALU.subtract),
                   waits=[(cTE, m_bc)] if i == 0 else [])
                last = ve(lambda E, d=dsl: E.tensor_tensor(
                    d, d, pb[6][:, 0:width], op=ALU.mult))
            return last

        def mamba_sc(s, first_wait=(), bc_free=0, feeder=None):
            base = SC * s
            rbase = SC * (s % SCPC)
            if s > 0:
                ve(lambda E: E.tensor_copy(
                    U1.rearrange("p (i c) -> p i c", c=UW)[:, :, 5:8],
                    U1.rearrange("p (i c) -> p i c", c=UW)
                    [:, :, UPAD + SC - 3:UPAD + SC]), waits=list(first_wait))
                first_wait = ()
            wx = list(first_wait)
            if s == 0:
                wx += [(cVE, m_ut0)]
            m_u = ln_feat(lambda i: U1[:, i * UW + UPAD:i * UW + UPAD + SC],
                          rbase, SC, wait_x=wx + [(cTE, cTE.n)])
            if stage == 15:
                tap("rows", ROWS[0:1, :], [1, 4096], F32)
                return m_u
            silu_marks = {}
            d_bcw = dGP.n
            for i in range(NT):
                ub = U1[:, i * UW:(i + 1) * UW]
                ac = 2560 + 512 * (i % 2)
                w0_ = []
                if i < 2:
                    w0_ = [(cSC, cSC.n)]
                else:
                    w0_ = [(cSC, silu_marks[i - 2])]
                ve(lambda E, u=ub, a=ac, i=i: E.tensor_scalar(
                    F32A[:, a:a + SC], u[:, 5:5 + SC],
                    PAR[:, P_convw + 4 * i:P_convw + 4 * i + 1], None,
                    op0=ALU.mult), waits=w0_)
                for j in range(1, 4):
                    ve(lambda E, u=ub, a=ac, i=i, j=j: E.scalar_tensor_tensor(
                        F32A[:, a:a + SC], u[:, 5 + j:5 + j + SC],
                        PAR[:, P_convw + 4 * i + j:P_convw + 4 * i + j + 1],
                        F32A[:, a:a + SC], op0=ALU.mult, op1=ALU.add))
                m_cv = cVE.n
                silu_marks[i] = sca(lambda E, i=i, a=ac: E.activation(
                    XC2[:, i * SC:(i + 1) * SC], F32A[:, a:a + SC], AF.Silu,
                    bias=PAR[:, P_convb + i:P_convb + i + 1]),
                    waits=[(cVE, m_cv)])
            m_xc = cSC.n
            if stage == 17:
                return m_xc
            for i in range(NT):
                w = [(cSC, m_xc if i == 0 else 0), (cVE, cVE.n if i == 0 else 0)]
                w = [x for x in w if x[1]]
                te(lambda E, i=i: E.matmul(
                    pb[3][:, :], PARB[:, B_xp + 160 * i:B_xp + 160 * i + 128],
                    XC2[:, i * SC:(i + 1) * SC],
                    start=(i == 0), stop=(i == NT - 1)), waits=w)
                te(lambda E, i=i: E.matmul(
                    pb[4][0:32, :],
                    PARB[:, B_xp + 160 * i + 128:B_xp + 160 * i + 160],
                    XC2[:, i * SC:(i + 1) * SC],
                    start=(i == 0), stop=(i == NT - 1)))
            m_proj = cTE.n
            m_dtin = sca(lambda E: E.copy(MSC[:, 2048:2560], pb[3][:, :]),
                         waits=[(cTE, m_proj), (cVE, cVE.n)])
            m_bcv = sca(lambda E: E.copy(F32A[0:32, 2560:2560 + SC],
                                         pb[4][0:32, :]),
                        waits=[(cTE, m_proj), (cVE, cVE.n)])
            d_bcw = gdma(s_bc[s, :, :], F32A[0:32, 2560:2560 + SC],
                         waits=[(cSC, m_bcv)])
            d_ball = sdma(BC4[:, 0:8192].rearrange("p (n c) -> p n c", c=SC),
                          s_bc[s:s + 1, 0:16, :].broadcast_to([128, 16, SC]),
                          waits=[(dGP, d_bcw), (cVE, bc_free)])
            d_call = sdma(BC4[:, 8192:16384].rearrange("p (n c) -> p n c", c=SC),
                          s_bc[s:s + 1, 16:32, :].broadcast_to([128, 16, SC]),
                          waits=[])
            if stage == 18:
                return cVE.n
            prev_sp = (0, 0)
            m_last = cVE.n
            for i in range(NT):
                dcol = 512 * (i % 2)
                wmt = [(cSC, m_dtin)]
                if i == 0:
                    wmt.append((cVE, m_u))
                if prev_sp[0]:
                    wmt += [(cSC, prev_sp[0]), (cVE, prev_sp[1])]
                mt = te(lambda E, i=i: E.matmul(
                    pb[5][:, :], PARB[:, B_dtp + 128 * i:B_dtp + 128 * (i + 1)],
                    MSC[:, 2048:2560], start=True, stop=True), waits=wmt)
                if stage == 1801:
                    return mt
                ms1 = sca(lambda E, i=i, d=dcol: E.activation(
                    F32A[:, d:d + SC], pb[5][:, :], AF.Abs,
                    bias=PAR[:, P_dtb + i:P_dtb + i + 1]), waits=[(cTE, mt)])
                if stage == 1802:
                    return cVE.n
                sca(lambda E, d=dcol: E.activation(
                    F32A[:, d:d + SC], F32A[:, d:d + SC], AF.Exp, scale=CAPS[:, 2:3]))
                if stage == 1803:
                    return cVE.n
                ms3 = sca(lambda E, d=dcol: E.activation(
                    F32A[:, d:d + SC], F32A[:, d:d + SC], AF.Ln, bias=CAPS[:, 3:4]))
                if stage == 1804:
                    return cVE.n
                mr = ve(lambda E, i=i: E.tensor_scalar(
                    F32B[:, 0:SC], pb[5][:, :],
                    PAR[:, P_dtb + i:P_dtb + i + 1], 0.0,
                    op0=ALU.add, op1=ALU.max), waits=[(cTE, mt), (cSC, ms1)])
                prev_sp = (ms1, mr)
                if stage == 1805:
                    return cVE.n
                mdelta = ve(lambda E, d=dcol: E.tensor_tensor(
                    F32A[:, d:d + SC], F32A[:, d:d + SC], F32B[:, 0:SC],
                    op=ALU.add), waits=[(cSC, ms3)])
                if stage == 181:
                    return mdelta
                mw_w = [(dSY, dSY.n)] if (s == 0 and i == 0) else []
                m_w = ve(lambda E, i=i, d=dcol: E.tensor_tensor(
                    MSC[:, 0:512], F32A[:, d:d + SC],
                    XC2[:, i * SC:(i + 1) * SC], op=ALU.mult), waits=mw_w)
                ve(lambda E, i=i: E.tensor_scalar(
                    MSC[:, 2560:3072],
                    U1[:, i * UW + UPAD:i * UW + UPAD + SC],
                    PAR[:, P_dpar + i:P_dpar + i + 1], None, op0=ALU.mult))
                if stage == 182:
                    return cVE.n
                h_marks = {}
                for n in range(NST):
                    acol = 1024 + 512 * (n % 2)
                    wsa = [(cVE, mdelta)]
                    if n >= 2:
                        wsa.append((cVE, h_marks[n - 2]))
                    sa_ = sca(lambda E, i=i, n=n, a=acol, d=dcol: E.activation(
                        F32A[:, a:a + SC], F32A[:, d:d + SC], AF.Exp,
                        scale=PAR[:, P_aneg + 16 * i + n:P_aneg + 16 * i + n + 1]),
                        waits=wsa)
                    wb = [(dSY, d_ball)] if n == 0 else []
                    ve(lambda E, n=n: E.tensor_tensor(
                        MSC[:, 512:1024], MSC[:, 0:512],
                        BC4[:, 512 * n:512 * (n + 1)], op=ALU.mult), waits=wb)
                    m_h = ve(lambda E, i=i, n=n, a=acol: E.tensor_tensor_scan(
                        MSC[:, 1024:1536], F32A[:, a:a + SC], MSC[:, 512:1024],
                        PAR[:, P_hc + 16 * i + n:P_hc + 16 * i + n + 1],
                        op0=ALU.mult, op1=ALU.add), waits=[(cSC, sa_)])
                    h_marks[n] = m_h
                    if stage == 183:
                        return m_h
                    wc = [(dSY, d_call)] if n == 0 else []
                    ve(lambda E, n=n: E.tensor_tensor(
                        MSC[:, 1536:2048], MSC[:, 1024:1536],
                        BC4[:, 8192 + 512 * n:8192 + 512 * (n + 1)],
                        op=ALU.mult), waits=wc)
                    m_ya = ve(lambda E: E.tensor_tensor(
                        MSC[:, 2560:3072], MSC[:, 2560:3072],
                        MSC[:, 1536:2048], op=ALU.add))
                    ve(lambda E, i=i, n=n: E.tensor_copy(
                        PAR[:, P_hc + 16 * i + n:P_hc + 16 * i + n + 1],
                        MSC[:, 1535:1536]))
                    if stage == 184:
                        return m_ya
                m_last = ve(lambda E, i=i, b=rbase: E.tensor_tensor(
                    R[:, CH * i + b:CH * i + b + SC],
                    R[:, CH * i + b:CH * i + b + SC],
                    MSC[:, 2560:3072], op=ALU.add))
                if stage == 19:
                    return m_last
                if feeder:
                    feeder.pop(0)()
                    if feeder:
                        feeder.pop(0)()
            return m_last

        def ffn0(c, m_u2):
            lastv = 0
            for o in range(NT):
                kb = KB2[:, (o % 2) * 2048:(o % 2) * 2048 + 2048]
                wkb = [(cTE, cTE.n)]
                if c == 0 and o < 2:
                    wkb += [(cVE, m_prep_ve), (cSC, m_prep_sc)]
                dW = sdma(kb.rearrange("p (i q) -> p i q", q=128),
                          s_sign[:, 128 * o:128 * (o + 1)]
                          .rearrange("(i p) q -> p i q", p=128), waits=wkb)
                for half in range(2):
                    bank = pb[half]
                    for i in range(NT):
                        w = []
                        if i == 0:
                            w = [(dSY, dW), (cVE, max(m_u2, lastv)),
                                 (cSC, cSC.n)]
                        te(lambda E, i=i, h=half, b=bank, kb=kb: E.matmul(
                            b[:, :], kb[:, 128 * i:128 * (i + 1)],
                            U23[:, CH * i + 512 * h:CH * i + 512 * (h + 1)],
                            start=(i == 0), stop=(i == NT - 1)), waits=w)
                    m_mm = cTE.n
                    ve(lambda E, o=o, b=bank: E.tensor_scalar(
                        F32B[:, 1024:1536], b[:, :],
                        PAR[:, P_alpha + o:P_alpha + o + 1],
                        PAR[:, P_b0 + o:P_b0 + o + 1],
                        op0=ALU.mult, op1=ALU.add), waits=[(cTE, m_mm)])
                    rsl = R[:, CH * o + 512 * half:CH * o + 512 * half + 512]
                    lastv = ve(lambda E, r=rsl: E.tensor_tensor(
                        r, r, F32B[:, 1024:1536], op=ALU.add))
            return lastv

        def qkv_feeder(c, m_u3):
            base = CH * c
            st = {"ev": {}, "sp": {}, "pend": None,
                  "mmdone": {0: cTE.n, 1: cTE.n}}
            items = []

            def qk_load(wT, o, slot):
                def emit():
                    kb = KB2[:, (o % 2) * 2048:(o % 2) * 2048 + 2048]
                    slot[0] = gdma(kb.rearrange("p (i q) -> p i q", q=128),
                                   wT[:, 128 * o:128 * (o + 1)]
                                   .rearrange("(i p) q -> p i q", p=128),
                                   waits=[(cTE, st["mmdone"].get(o % 2, 0))])
                return emit

            def qk_mm(wT, o, slot):
                def emit():
                    kb = KB2[:, (o % 2) * 2048:(o % 2) * 2048 + 2048]
                    dW = slot[0]
                    mms = []
                    for half in range(2):
                        bk = (2 * o + half) % 3 if True else 0
                        bank = pb[bk]
                        for i in range(NT):
                            w = []
                            if i == 0:
                                w = [(dGP, dW), (cVE, m_u3),
                                     (cSC, st["ev"].get(bk, 0))]
                            te(lambda E, i=i, b=bank, kb=kb, h=half: E.matmul(
                                b[:, :], kb[:, 128 * i:128 * (i + 1)],
                                U23[:, CH * i + 512 * h:CH * i + 512 * (h + 1)],
                                start=(i == 0), stop=(i == NT - 1)), waits=w)
                        mms.append((bk, cTE.n))
                    st["mmdone"][o % 2] = cTE.n
                    return mms

                return emit

            def qk_ev(sdst, o, mms):
                def emit():
                    marks = []
                    for half, (bk, m_mm) in enumerate(mms()):
                        m_ev = sca(lambda E, b=pb[mms()[half][0]], h=half: E.copy(
                            KB2[:, 4096 + 512 * h:4096 + 512 * (h + 1)], b[:, :]),
                            waits=[(cTE, m_mm), (dSY, st["sp"].get("qk", 0))])
                        st["ev"][bk] = m_ev
                        marks.append(m_ev)
                    st["sp"]["qk"] = sdma(
                        sdst[128 * o:128 * (o + 1), base:base + CH],
                        KB2[:, 4096:5120], waits=[(cSC, marks[-1])])
                return emit

            def v_mm(ob, tt, dwv):
                def emit():
                    if tt == 0:
                        dwv[0] = gdma(KB2.rearrange("p (i q) -> p i q", q=512),
                                      wvT[:, 512 * ob:512 * (ob + 1)]
                                      .rearrange("(i p) q -> p i q", p=128),
                                      waits=[(cTE, cTE.n), (dSY, dSY.n),
                                             (cSC, st["ev"].get(tt % 2, 0))])
                    bank = pb[tt % 2]
                    for i in range(NT):
                        w = []
                        if i == 0:
                            w = [(dGP, dwv[0]), (cVE, m_u3),
                                 (cSC, st["ev"].get(tt % 2, 0))]
                        te(lambda E, i=i, tt=tt, b=bank: E.matmul(
                            b[:, :],
                            U23[:, CH * i + 128 * tt:CH * i + 128 * (tt + 1)],
                            KB2[:, 512 * i:512 * (i + 1)],
                            start=(i == 0), stop=(i == NT - 1)), waits=w)
                    return [(tt % 2, cTE.n)]
                return emit

            def v_ev(ob, tt, mms):
                def emit():
                    bk, m_mm = mms()[0]
                    m_ev = sca(lambda E, b=pb[bk]: E.copy(F32B[:, 1536:2048],
                                                          b[:, :]),
                               waits=[(cTE, m_mm), (dGP, st["sp"].get("v", 0))])
                    st["ev"][bk] = m_ev
                    st["sp"]["v"] = gdma(
                        s_v[base + 128 * tt:base + 128 * (tt + 1),
                            512 * ob:512 * (ob + 1)], F32B[:, 1536:2048],
                        waits=[(cSC, m_ev)])
                return emit

            units = []
            for o in range(NT):
                units.append(("qk", wqT, s_qT, o))
            for o in range(NT):
                units.append(("qk", wkT, s_kT, o))
            for ob in range(4):
                dwv = [0]
                for tt in range(8):
                    units.append(("v", ob, tt, dwv))
            pend = [None]
            pre = []
            for u in units:
                if u[0] == "qk":
                    _, wT, sdst, o = u
                    res = {}
                    slot = [0]
                    ld = qk_load(wT, o, slot)
                    mm = qk_mm(wT, o, slot)
                    def item(mm=mm, res=res):
                        res["mms"] = mm()
                    ev = qk_ev(sdst, o, lambda res=res: res["mms"])
                else:
                    _, ob, tt, dwv = u
                    res = {}
                    ld = None
                    mm = v_mm(ob, tt, dwv)
                    def item(mm=mm, res=res):
                        res["mms"] = mm()
                    ev = v_ev(ob, tt, lambda res=res: res["mms"])
                pre.append(ld)
                units[len(pre) - 1] = (item, ev)
            for k, (item, ev) in enumerate(units):
                def combined(item=item, prev_ev=pend[0],
                             nld=(pre[k + 1] if k + 1 < len(pre) else None),
                             first=(pre[k] if k == 0 else None)):
                    if first is not None:
                        first()
                    if nld is not None:
                        nld()
                    if prev_ev is not None:
                        prev_ev()
                    item()
                items.append(combined)
                pend[0] = ev
            def final(prev_ev=pend[0]):
                if prev_ev is not None:
                    prev_ev()
            items.append(final)
            return items

        m_u3_prev = 0
        if stage < 2:
            _finish(P, cGP, dGP, dSY)
            return _close(nc, P)
        m_mamba = 0
        d_rspill = 0
        feeder = []
        for c in range(NCH):
            wR = []
            if c > 0:
                wR = [(dSY, d_rspill), (cVE, m_u3_prev)]
            d_xload = gdma(R.rearrange("p (i q) -> p i q", q=CH),
                           xT[:, CH * c:CH * (c + 1)]
                           .rearrange("(i p) q -> p i q", p=128), waits=wR)
            for sci in range(SCPC):
                s = SCPC * c + sci
                fw = []
                if sci == 0:
                    fw = [(dGP, d_xload)]
                m_mamba = mamba_sc(s, first_wait=fw, bc_free=m_mamba,
                                   feeder=feeder)
                if stage == 22 and s == 0:
                    tap("hc", PAR[:, :], [128, 768], F32)
                    _finish(P, cGP, dGP, dSY)
                    return _close(nc, P)
                if stage == 23 and s == 1:
                    tap("R_c0", R[:, :], [128, NT * CH], BF16)
                    _finish(P, cGP, dGP, dSY)
                    return _close(nc, P)
                if stage in (15, 17, 18, 1801, 1802, 1803, 1804, 1805, 181, 182, 183, 184, 19, 2):
                    tap("R_sc0", R[:, :], [128, NT * CH], BF16)
                    tap("u_sc0", U1[:, :], [128, NT * UW], BF16)
                    tap("xc_sc0", XC2[:, :], [128, NT * SC], BF16)
                    _finish(P, cGP, dGP, dSY)
                    return _close(nc, P)
            for fn in feeder:
                fn()
            feeder = []
            m_u2 = 0
            for half in range(2):
                m_u2 = ln_feat(lambda i, h=half: U23[:, CH * i + 512 * h:
                                                     CH * i + 512 * h + 512],
                               512 * half, 512,
                               wait_ve=m_mamba, wait_x=[(cTE, cTE.n)])
            m_ffn = ffn0(c, m_u2)
            if stage == 25:
                tap("R_ffn0", R[:, :], [128, NT * CH], BF16)
                _finish(P, cGP, dGP, dSY)
                return _close(nc, P)
            if stage == 24:
                tap("u2", U23[:, :], [128, NT * CH], BF16)
                _finish(P, cGP, dGP, dSY)
                return _close(nc, P)
            m_u3 = 0
            for half in range(2):
                m_u3 = ln_feat(lambda i, h=half: U23[:, CH * i + 512 * h:
                                                     CH * i + 512 * h + 512],
                               512 * half, 512,
                               wait_ve=m_ffn, wait_x=[(cTE, cTE.n)])
            d_rspill = sdma(s_R[:, CH * c:CH * (c + 1)]
                            .rearrange("(i p) q -> p i q", p=128),
                            R.rearrange("p (i q) -> p i q", q=CH),
                            waits=[(cTE, cTE.n), (cVE, m_u3)])
            feeder = qkv_feeder(c, m_u3)
            m_u3_prev = m_u3
        for fn in feeder:
            fn()
        feeder = []
        w1_prep()
        tap("R_A", R[:, :], [128, NT * CH], BF16)
        if stage < 4:
            _finish(P, cGP, dGP, dSY)
            return _close(nc, P)
        tap("qT", None, None) if False else None

        for h in range(NH):
            cb = (h % 2) * 2048
            hp = (h + 8) % 16
            d_kt = sdma(KB2[:, cb:cb + 2048], s_kT[128 * h:128 * (h + 1), :],
                        waits=[(cTE, cTE.n), (cVE, cVE.n)])
            d_kp = sdma(KB2[:, 4096 + cb:4096 + cb + 2048],
                        s_kT[128 * hp:128 * (hp + 1), :], waits=[])
            d_cs = gdma(U23[:, 0:2048], cosTd[128 * h:128 * (h + 1), :],
                        waits=[(cVE, cVE.n), (cTE, cTE.n)])
            d_sn = gdma(U23[:, 2048:4096], sinTd[128 * h:128 * (h + 1), :],
                        waits=[])
            d_vh = sdma(XC2[:, 4096 + cb:4096 + cb + 2048]
                        .rearrange("p (j q) -> p j q", q=128),
                        s_v[:, 128 * h:128 * (h + 1)]
                        .rearrange("(j p) q -> p j q", p=128),
                        waits=[(cTE, cTE.n), (dGP, dGP.n)])
            ve(lambda E, cb=cb: E.tensor_tensor(
                U23[:, 4096:6144], KB2[:, cb:cb + 2048], U23[:, 0:2048],
                op=ALU.mult),
               waits=[(dSY, d_kp), (dGP, d_sn), (cSC, cSC.n), (cTE, cTE.n)])
            ve(lambda E, cb=cb: E.tensor_tensor(
                MSC[:, 2048:4096], KB2[:, 4096 + cb:4096 + cb + 2048],
                U23[:, 2048:4096], op=ALU.mult))
            mkr = ve(lambda E: E.tensor_tensor(
                U23[:, 4096:6144], U23[:, 4096:6144], MSC[:, 2048:4096],
                op=ALU.add))
            sdma(s_kTr[128 * h:128 * (h + 1), :], U23[:, 4096:6144],
                 waits=[(cVE, mkr)])
            mtr_ev = 0
            for g in range(2):
                mtr = 0
                for jj in range(8):
                    j = 8 * g + jj
                    wtr = [(cVE, mkr)]
                    if jj == 0 and mtr_ev:
                        wtr.append((cSC, mtr_ev))
                    mtr = te(lambda E, j=j, jj=jj: E.transpose(
                        pbt[:, 128 * jj:128 * (jj + 1)],
                        U23[:, 4096 + 128 * j:4096 + 128 * (j + 1)],
                        IDENB[:, :]), waits=wtr)
                mtr_ev = sca(lambda E, g=g, cb=cb: E.copy(
                    XC2[:, cb + 1024 * g:cb + 1024 * (g + 1)], pbt[:, :]),
                    waits=[(cTE, mtr)])
            d_qt = sdma(KB2[:, cb:cb + 2048], s_qT[128 * h:128 * (h + 1), :],
                        waits=[(cVE, cVE.n)])
            d_qp = sdma(KB2[:, 4096 + cb:4096 + cb + 2048],
                        s_qT[128 * hp:128 * (hp + 1), :], waits=[])
            ve(lambda E, cb=cb: E.tensor_tensor(
                U23[:, 6144:8192], KB2[:, cb:cb + 2048], U23[:, 0:2048],
                op=ALU.mult), waits=[(dSY, d_qp)])
            ve(lambda E, cb=cb: E.tensor_tensor(
                MSC[:, 2048:4096], KB2[:, 4096 + cb:4096 + cb + 2048],
                U23[:, 2048:4096], op=ALU.mult))
            mqr = ve(lambda E: E.tensor_tensor(
                U23[:, 6144:8192], U23[:, 6144:8192], MSC[:, 2048:4096],
                op=ALU.add))
            sdma(s_qTr[128 * h:128 * (h + 1), :], U23[:, 6144:8192],
                 waits=[(cVE, mqr)])
            mv1 = ve(lambda E, cb=cb: E.tensor_scalar_min(
                MSC[:, 2048:4096], XC2[:, cb:cb + 2048], 0.0),
                waits=[(cSC, mtr_ev)])
            ms1 = sca(lambda E: E.activation(MSC[:, 2048:4096], MSC[:, 2048:4096],
                                             AF.Exp), waits=[(cVE, mv1)])
            mv2 = ve(lambda E, cb=cb: E.tensor_relu(MSC[:, 0:2048],
                                                    XC2[:, cb:cb + 2048]))
            mkf = ve(lambda E: E.tensor_tensor(MSC[:, 0:2048], MSC[:, 0:2048],
                                               MSC[:, 2048:4096], op=ALU.add),
                     waits=[(cSC, ms1)])
            for j in range(NT):
                w = []
                if j == 0:
                    w = [(cVE, mkf), (dSY, d_vh), (cSC, cSC.n)]
                te(lambda E, j=j, cb=cb: E.matmul(
                    pb[0][:, 0:128], MSC[:, 128 * j:128 * (j + 1)],
                    XC2[:, 4096 + cb + 128 * j:4096 + cb + 128 * (j + 1)],
                    start=(j == 0), stop=(j == NT - 1)), waits=w)
                te(lambda E, j=j: E.matmul(
                    pb[1][0:1, 0:128], PARB[:, B_ones:B_ones + 1],
                    MSC[:, 128 * j:128 * (j + 1)],
                    start=(j == 0), stop=(j == NT - 1)))
            m_mem = cTE.n
            sca(lambda E, h=h: E.copy(U1[:, 128 * h:128 * (h + 1)],
                                      pb[0][:, 0:128]),
                waits=[(cTE, m_mem)] + ([(cVE, cVE.n)] if h == 0 else []))
            m_e2 = sca(lambda E, h=h: E.copy(ROWS[0:1, 128 * h:128 * (h + 1)],
                                             pb[1][0:1, 0:128]),
                       waits=[(cTE, m_mem)])
        d_nw = gdma(s_row[3:4, 0:2048], ROWS[0:1, 0:2048], waits=[(cSC, cSC.n)])
        d_norm = gdma(PARB[:, B_norm:B_norm + 16],
                      s_row[3, 0:2048].rearrange("(h p) -> p h", p=128),
                      waits=[(dGP, d_nw)], slow=True)

        tap("memB", U1[:, 0:2048], [128, 2048], BF16)
        if stage < 5:
            _finish(P, cGP, dGP, dSY)
            return _close(nc, P)

        for qcc in range(2):
            qb0 = 1024 * qcc
            wR = [(cVE, cVE.n), (cTE, cTE.n)] if qcc > 0 else [(dSY, d_rspill)]
            d_rload = sdma(R.rearrange("p (i q) -> p i q", q=CH),
                           s_R[:, qb0:qb0 + CH]
                           .rearrange("(i p) q -> p i q", p=128), waits=wR)
            hq = [(qc2, h) for qc2 in range(2) for h in range(NH)]
            lds = {}
            mk = {}

            def issue_loads(j):
                qc2, h = hq[j]
                qb = qb0 + 512 * qc2
                cb = (h % 2) * 2048
                pm = mk.get(j - 2, {})
                d_kt = sdma(KB2[:, cb:cb + 2048],
                            s_kTr[128 * h:128 * (h + 1), :],
                            waits=[(cTE, pm.get("av", 0))] if pm else
                                  [(cTE, cTE.n)])
                d_vh = sdma(KB2[:, 4096 + cb:4096 + cb + 2048]
                            .rearrange("p (j q) -> p j q", q=128),
                            s_v[:, 128 * h:128 * (h + 1)]
                            .rearrange("(j p) q -> p j q", p=128),
                            waits=([(cTE, pm.get("av", 0))] if pm else
                                   [(cTE, cTE.n), (dGP, dGP.n)]))
                d_qt = sdma(MSC[:, 512 * (h % 2):512 * (h % 2) + 512],
                            s_qTr[128 * h:128 * (h + 1), qb:qb + 512],
                            waits=([(cTE, pm.get("mo", 0)),
                                    (cVE, pm.get("qdve", 0))] if pm else
                                   [(cTE, cTE.n), (cVE, cVE.n)]))
                lds[j] = (d_kt, d_vh, d_qt)

            issue_loads(0)
            issue_loads(1)
            for j, (qc2, h) in enumerate(hq):
                cb = (h % 2) * 2048
                qsl = MSC[:, 512 * (h % 2):512 * (h % 2) + 512]
                qf = MSC[:, 1536 + 512 * (h % 2):1536 + 512 * (h % 2) + 512]
                d_kt, d_vh, d_qt = lds[j]
                pm = mk.get(j - 1, {})
                m = {}
                mv1 = ve(lambda E, q=qsl: E.tensor_scalar_min(
                    MSC[:, 1024:1536], q, 0.0),
                    waits=[(dSY, d_qt), (cSC, pm.get("me2", 0))])
                ms1 = sca(lambda E: E.activation(
                    MSC[:, 1024:1536], MSC[:, 1024:1536], AF.Exp),
                    waits=[(cVE, mv1)])
                mv2 = ve(lambda E, q=qsl, f=qf: E.tensor_relu(f, q))
                m["qdve"] = mv2
                mqf = ve(lambda E, f=qf: E.tensor_tensor(
                    f, f, MSC[:, 1024:1536], op=ALU.add), waits=[(cSC, ms1)])
                m_exps = []
                mmks = []

                def emit_S(kt):
                    bank = pb[kt % 3]
                    w = []
                    if kt == 0:
                        w = [(dSY, d_kt), (cSC, pm.get("exp15", 0)),
                             (cVE, pm.get("comb", 0))]
                    elif kt in (1, 2):
                        w = [(cSC, pm.get("exp15", 0)),
                             (cVE, pm.get("comb", 0))]
                    else:
                        w = [(cSC, m_exps[kt - 3])]
                    mmk = te(lambda E, kt=kt, b=bank, cb=cb, q=qsl: E.matmul(
                        b[:, :], KB2[:, cb + 128 * kt:cb + 128 * (kt + 1)],
                        q, start=True, stop=True), waits=w)
                    me = sca(lambda E, kt=kt, b=bank: E.activation(
                        XC2[:, 512 * kt:512 * (kt + 1)], b[:, :], AF.Exp,
                        bias=CAPS[:, 4:5], scale=CAPS[:, 5:6]),
                        waits=[(cTE, mmk)])
                    m_exps.append(me)

                for kt in range(3):
                    emit_S(kt)
                for kt in range(NT):
                    if kt + 3 < NT:
                        emit_S(kt + 3)
                    w = [(cSC, m_exps[kt])]
                    if kt == 0:
                        w += [(dSY, d_vh), (cSC, pm.get("me1", 0)),
                              (cVE, pm.get("mr1", 0))]
                    te(lambda E, kt=kt, cb=cb: E.matmul(
                        pb[3][:, :],
                        KB2[:, 4096 + cb + 128 * kt:4096 + cb + 128 * (kt + 1)],
                        XC2[:, 512 * kt:512 * (kt + 1)],
                        start=(kt == 0), stop=(kt == NT - 1)), waits=w)
                    te(lambda E, kt=kt: E.matmul(
                        pb[4][0:1, :], PARB[:, B_ones:B_ones + 1],
                        XC2[:, 512 * kt:512 * (kt + 1)],
                        start=(kt == 0), stop=(kt == NT - 1)))
                m["exp15"] = m_exps[15]
                m["exp14"] = m_exps[14]
                m["av"] = cTE.n
                te(lambda E, h=h, f=qf: E.matmul(
                    pb[5][:, :], U1[:, 128 * h:128 * (h + 1)], f,
                    start=True, stop=True),
                   waits=[(cVE, mqf), (dGP, d_norm), (cSC, pm.get("me2", 0)),
                          (cVE, pm.get("mr2", 0))])
                m_mo = te(lambda E, h=h, f=qf: E.matmul(
                    pb[6][0:1, :], PARB[:, B_norm + h:B_norm + h + 1], f,
                    start=True, stop=True))
                m["mo"] = m_mo
                if j + 2 < len(hq):
                    issue_loads(j + 2)
                mr1 = ve(lambda E: E.reciprocal(ROWS[0:1, 0:512],
                                                pb[4][0:1, :]),
                         waits=[(cTE, m_mo)])
                m["mr1"] = mr1
                mr2 = ve(lambda E: E.reciprocal(ROWS[0:1, 512:1024],
                                                pb[6][0:1, :]))
                m["mr2"] = mr2
                mrb = ve(lambda E, h=h: E.tensor_scalar(
                    ROWS[0:1, 1024:1536], ROWS[0:1, 512:1024],
                    GS[0:1, h:h + 1], None, op0=ALU.mult),
                    waits=[(cTE, pm.get("rep", 0))])
                mrb = ve(lambda E, h=h: E.tensor_scalar(
                    ROWS[0:1, 1536:2048], ROWS[0:1, 0:512],
                    GS[0:1, 16 + h:16 + h + 1], None, op0=ALU.mult),
                    waits=[(cVE, m_g1m)])
                te(lambda E: E.matmul(
                    pb[0][:, :], ONER[0:1, :], ROWS[0:1, 1024:1536],
                    start=True, stop=True),
                   waits=[(cVE, mrb), (cSC, m_exps[15])])
                m["rep"] = te(lambda E: E.matmul(
                    pb[1][:, :], ONER[0:1, :], ROWS[0:1, 1536:2048],
                    start=True, stop=True), waits=[(cSC, m_exps[13])])
                m["sc"] = m["rep"]
                me1 = sca(lambda E: E.copy(MSC[:, 2560:3072], pb[3][:, :]),
                          waits=[(cTE, m["rep"])])
                m["me1"] = me1
                me2 = sca(lambda E: E.copy(MSC[:, 3072:3584], pb[5][:, :]))
                m["me2"] = me2
                mt1 = ve(lambda E: E.tensor_tensor(
                    MSC[:, 3584:4096], MSC[:, 2560:3072], pb[1][:, :],
                    op=ALU.mult), waits=[(cSC, me2), (cTE, m["rep"])])
                ve(lambda E: E.tensor_tensor(
                    MSC[:, 2560:3072], MSC[:, 3072:3584], pb[0][:, :],
                    op=ALU.mult))
                m["comb"] = ve(lambda E, h=h, q=qc2: E.tensor_tensor(
                    BC4[:, 1024 * h + 512 * q:1024 * h + 512 * (q + 1)],
                    MSC[:, 3584:4096], MSC[:, 2560:3072], op=ALU.add))
                mk[j] = m
            m_comb = cVE.n
            lastv = 0
            for o in range(NT):
                kb = KB2[:, (o % 2) * 2048:(o % 2) * 2048 + 2048]
                dW = sdma(kb.rearrange("p (i q) -> p i q", q=128),
                          s_woT[:, 128 * o:128 * (o + 1)]
                          .rearrange("(i p) q -> p i q", p=128),
                          waits=[(cTE, cTE.n)])
                for half in range(2):
                    bank = pb[2 + half]
                    for i in range(NT):
                        w = []
                        if i == 0:
                            w = [(dSY, dW), (cVE, max(m_comb, lastv)),
                                 (cSC, cSC.n)]
                        te(lambda E, i=i, h=half, b=bank, kb=kb: E.matmul(
                            b[:, :], kb[:, 128 * i:128 * (i + 1)],
                            BC4[:, 1024 * i + 512 * h:1024 * i + 512 * (h + 1)],
                            start=(i == 0), stop=(i == NT - 1)), waits=w)
                    m_mm = cTE.n
                    rsl = R[:, CH * o + 512 * half:CH * o + 512 * half + 512]
                    lastv = ve(lambda E, r=rsl, b=bank: E.tensor_tensor(
                        r, r, b[:, :], op=ALU.add),
                        waits=[(cTE, m_mm), (dSY, d_rload)])
            m_wo = lastv
            m_u4 = 0
            for half in range(2):
                m_u4 = ln_feat(lambda i, h=half: U23[:, CH * i + 512 * h:
                                                     CH * i + 512 * h + 512],
                               512 * half, 512,
                               wait_ve=m_wo, wait_x=[(cTE, cTE.n)])
            mx = ve(lambda E: E.tensor_tensor(F32A[:, 2048:3072], U23[:, 0:CH],
                                              U23[:, 0:CH], op=ALU.mult),
                    waits=[(cVE, m_u4)])
            for i in range(1, NT):
                ve(lambda E, i=i: E.tensor_tensor(
                    F32B[:, 1024:2048], U23[:, CH * i:CH * i + CH],
                    U23[:, CH * i:CH * i + CH], op=ALU.mult))
                mx = ve(lambda E: E.tensor_tensor(
                    F32A[:, 2048:3072], F32A[:, 2048:3072], F32B[:, 1024:2048],
                    op=ALU.max))
            width = 64
            while width >= 1:
                dsh = gdma(F32B[0:width, 0:1024], F32A[width:2 * width, 2048:3072],
                           waits=[(cVE, mx)])
                mx = ve(lambda E, w=width: E.tensor_tensor(
                    F32A[0:w, 2048:3072], F32A[0:w, 2048:3072],
                    F32B[0:w, 0:1024], op=ALU.max), waits=[(dGP, dsh)])
                width //= 2
            ms = sca(lambda E: E.activation(ROWS[0:1, 0:1024],
                                            F32A[0:1, 2048:3072], AF.Sqrt),
                     waits=[(cVE, mx), (cTE, cTE.n)])
            mv = ve(lambda E: E.tensor_scalar(ROWS[0:1, 1024:2048],
                                              ROWS[0:1, 0:1024], 1.0 / 127.0,
                                              1e-12, op0=ALU.mult, op1=ALU.max),
                    waits=[(cSC, ms)])
            mvr = ve(lambda E: E.reciprocal(ROWS[0:1, 2048:3072],
                                            ROWS[0:1, 1024:2048]))
            for j in range(2):
                te(lambda E, j=j: E.matmul(
                    pb[3 + j][:, :], ONER[0:1, :],
                    ROWS[0:1, 1024 + 512 * j:1024 + 512 * (j + 1)],
                    start=True, stop=True),
                   waits=[(cVE, mvr), (cSC, cSC.n)] if j == 0 else [])
                te(lambda E, j=j: E.matmul(
                    pb[5 + j][:, :], ONER[0:1, :],
                    ROWS[0:1, 2048 + 512 * j:2048 + 512 * (j + 1)],
                    start=True, stop=True))
            m_reps = cTE.n
            m_xsr = 0
            for j in range(2):
                ve(lambda E, j=j: E.tensor_copy(F32A[:, 512 * j:512 * (j + 1)],
                                                pb[3 + j][:, :]),
                   waits=[(cTE, m_reps)])
                m_xsr = ve(lambda E, j=j: E.tensor_copy(
                    F32A[:, 1024 + 512 * j:1024 + 512 * (j + 1)],
                    pb[5 + j][:, :]))
            m_xq = 0
            for i in range(NT):
                ve(lambda E, i=i: E.scalar_tensor_tensor(
                    F32B[:, 0:1024], U23[:, CH * i:CH * i + CH], 1.0,
                    F32A[:, 1024:2048], op0=ALU.mult, op1=ALU.mult),
                   waits=[(cVE, m_xsr)] if i == 0 else [])
                m_xq = ve(lambda E, i=i: E.tensor_scalar(
                    U23[:, CH * i:CH * i + CH], F32B[:, 0:1024], MAGIC, MAGIC,
                    op0=ALU.add, op1=ALU.subtract))
            tr_marks = {}
            for o in range(NT):
                kb = KB2[:, (o % 2) * 2048:(o % 2) * 2048 + 2048]
                dW = sdma(kb.rearrange("p (i q) -> p i q", q=128),
                          s_wq1[:, 128 * o:128 * (o + 1)]
                          .rearrange("(i p) q -> p i q", p=128),
                          waits=[(cTE, cTE.n)])
                for half in range(2):
                    bank = pb[half]
                    for i in range(NT):
                        w = []
                        if i == 0:
                            w = [(dSY, dW), (cVE, max(m_xq, cVE.n)), (cSC, cSC.n)]
                        te(lambda E, i=i, h=half, b=bank, kb=kb: E.matmul(
                            b[:, :], kb[:, 128 * i:128 * (i + 1)],
                            U23[:, CH * i + 512 * h:CH * i + 512 * (h + 1)],
                            start=(i == 0), stop=(i == NT - 1)), waits=w)
                    m_mm = cTE.n
                    ve(lambda E, o=o, b=bank, h=half: E.scalar_tensor_tensor(
                        F32B[:, 0:512], b[:, :],
                        PAR[:, P_ws1 + o:P_ws1 + o + 1],
                        F32A[:, 512 * h:512 * (h + 1)],
                        op0=ALU.mult, op1=ALU.mult), waits=[(cTE, m_mm)])
                    rsl = R[:, CH * o + 512 * half:CH * o + 512 * half + 512]
                    ve(lambda E, o=o, r=rsl, h=half: E.scalar_tensor_tensor(
                        OUTS[:, 512 * h:512 * (h + 1)], F32B[:, 0:512],
                        PAR[:, P_b1 + o:P_b1 + o + 1], r,
                        op0=ALU.add, op1=ALU.add))
                m_row = cVE.n
                for tt in range(8):
                    bank = pb[3 + tt % 2]
                    w = [(cVE, m_row)]
                    if (o, tt) != (0, 0):
                        pass
                    if tt >= 2 or o > 0:
                        prev = tr_marks.get((o, tt - 2)) or \
                            tr_marks.get((o - 1, tt + 6))
                        if prev:
                            w.append((cGP, prev))
                    mtr = te(lambda E, tt=tt, b=bank: E.transpose(
                        b[:, 0:128], OUTS[:, 128 * tt:128 * (tt + 1)],
                        IDEN[:, :]), waits=w)
                    msc = sca(lambda E, tt=tt, b=bank: E.copy(
                        F32A[:, 3072 + 128 * (tt % 2):3072 + 128 * (tt % 2) + 128],
                        b[:, 0:128]), waits=[(cTE, mtr)])
                    P.op("gp", lambda E, tt=tt, o=o, q=qb0: E.dma_start(
                        out=out[q + 128 * tt:q + 128 * (tt + 1),
                                128 * o:128 * (o + 1)],
                        in_=F32A[:, 3072 + 128 * (tt % 2):
                                 3072 + 128 * (tt % 2) + 128]),
                         waits=[(cSC, msc)], inc=cGP, k=16)
                    tr_marks[(o, tt)] = cGP.n
        _finish(P, cGP, dGP, dSY)
        return _close(nc, P)


def _cols16(v):
    return np.ascontiguousarray(np.asarray(v).reshape(16, 128).T)


def _host_inputs(inp):
    f32 = np.float32
    m = {}
    m["convw"] = np.ascontiguousarray(
        np.asarray(inp["L0_attn_conv_w"]).reshape(16, 128, 4)
        .transpose(1, 0, 2).reshape(128, 64)).astype(f32)
    m["convb"] = _cols16(inp["L0_attn_conv_b"]).astype(f32)
    m["xprojT"] = np.ascontiguousarray(np.asarray(inp["L0_attn_xproj_w"]).T).astype(f32)
    m["dtprojT"] = np.ascontiguousarray(np.asarray(inp["L0_attn_dtproj_w"]).T).astype(f32)
    m["dtprojb"] = _cols16(inp["L0_attn_dtproj_b"]).astype(f32)
    m["alog"] = np.ascontiguousarray(
        np.asarray(inp["L0_attn_A_log"]).reshape(16, 128, 16)
        .transpose(1, 0, 2).reshape(128, 256)).astype(f32)
    m["dpar"] = _cols16(inp["L0_attn_D_param"]).astype(f32)
    m["w0T"] = np.ascontiguousarray(np.asarray(inp["L0_ffn_weight"]).T).astype(f32)
    m["b0c"] = _cols16(inp["L0_ffn_bias"]).astype(f32)
    m["wqT"] = np.ascontiguousarray(np.asarray(inp["L1_attn_Wq"]).T).astype(f32)
    m["wkT"] = np.ascontiguousarray(np.asarray(inp["L1_attn_Wk"]).T).astype(f32)
    m["wvT"] = np.ascontiguousarray(np.asarray(inp["L1_attn_Wv"]).T).astype(f32)
    m["woT"] = np.ascontiguousarray(np.asarray(inp["L1_attn_Wo"]).T).astype(f32)
    m["gate"] = np.ascontiguousarray(np.asarray(inp["L1_attn_gate"]).reshape(1, 16)).astype(f32)
    m["w1T"] = np.ascontiguousarray(np.asarray(inp["L1_ffn_weight"]).T).astype(f32)
    m["b1c"] = _cols16(inp["L1_ffn_bias"]).astype(f32)
    t = np.arange(T, dtype=np.float64)
    invf = 1.0 / (10000.0 ** (np.arange(0, D, 2, dtype=np.float64) / D))
    ang = invf[:, None] * t[None, :]
    m["cosT"] = np.concatenate([np.cos(ang), np.cos(ang)], 0).astype(f32)
    m["sinT"] = np.concatenate([-np.sin(ang), np.sin(ang)], 0).astype(f32)
    m["ident"] = np.eye(128, dtype=f32)
    return m


def kernel(**inputs):
    shared = _host_inputs(inputs)
    x = np.asarray(inputs["L0_attn_x"])
    nc = build_program()
    in_maps = []
    for b in range(8):
        mm = dict(shared)
        mm["xT"] = np.ascontiguousarray(x[b].T).astype(np.float32)
        in_maps.append(mm)
    res = run_bass_kernel_spmd(nc, in_maps, core_ids=list(range(8)))
    return np.stack([np.asarray(r["y"], np.float32) for r in res.results], 0)


# version 11
# speedup vs baseline: 1.0567x; 1.0028x over previous
import numpy as np
import ml_dtypes
import concourse.bass as bass
import concourse.mybir as mybir
from concourse.bass_utils import run_bass_kernel_spmd
from contextlib import ExitStack

F32 = mybir.dt.float32
BF16 = mybir.dt.bfloat16
AF = mybir.ActivationFunctionType
ALU = mybir.AluOpType

T = 2048; D = 2048; NT = 16; NST = 16; NH = 16
CH = 1024; NCH = T // CH
SC = 512; NSC = T // SC
SCPC = CH // SC
MAGIC = 12582912.0
INVSQ = float(1.0 / np.sqrt(128.0))
SOFTBIAS = -20.0
UPAD = 8
UW = SC + UPAD


class Ctr:
    def __init__(self, name):
        self.name = name
        self.n = 0


class Prog:
    def __init__(self):
        self.items = {e: [] for e in ("gp", "sy", "te", "ve", "sc")}

    def op(self, eng, fn, waits=(), inc=None, k=1):
        wl = [(c.name, v) for (c, v) in waits if v and v > 0]
        tgt = None
        if inc is not None:
            inc.n += k
            tgt = (inc.name, k)

        def run(E, sems):
            for nm, v in wl:
                E.wait_ge(sems[nm], v)
            ins = fn(E)
            if tgt is not None:
                ins.then_inc(sems[tgt[0]], tgt[1])
        self.items[eng].append(run)


def _finish(P, cGP, dGP, dSY):
    P.op("gp", lambda E: None, waits=[(cGP, cGP.n), (dGP, dGP.n)])
    P.op("sy", lambda E: None, waits=[(cGP, cGP.n), (dSY, dSY.n)])


def _close(nc, P):
    with (nc.semaphore("sdGP") as s0, nc.semaphore("sdSY") as s1,
          nc.semaphore("scTE") as s2, nc.semaphore("scVE") as s3,
          nc.semaphore("scSC") as s4, nc.semaphore("scGP") as s5,
          nc.Block() as block):
        sems = {"dGP": s0, "dSY": s1, "cTE": s2, "cVE": s3,
                "cSC": s4, "cGP": s5}

        @block.gpsimd
        def _(E):
            for fn in P.items["gp"]:
                fn(E, sems)

        @block.sync
        def _(E):
            for fn in P.items["sy"]:
                fn(E, sems)

        @block.tensor
        def _(E):
            for fn in P.items["te"]:
                fn(E, sems)

        @block.vector
        def _(E):
            for fn in P.items["ve"]:
                fn(E, sems)

        @block.scalar
        def _(E):
            for fn in P.items["sc"]:
                fn(E, sems)

    return nc


def build_program(taps=(), stage=99):
    nc = bass.Bass("TRN2", target_bir_lowering=False,
                   detect_race_conditions=False)
    P = Prog()
    taps = set(taps)

    def din(name, shape, dt=F32):
        return nc.dram_tensor(name, shape, dt, kind="ExternalInput").ap()

    xT = din("xT", [D, T])
    convw = din("convw", [128, 4 * NT])
    convb = din("convb", [128, NT])
    xprojT_d = din("xprojT", [D, 160])
    dtprojT_d = din("dtprojT", [128, D])
    dtprojb = din("dtprojb", [128, NT])
    alog = din("alog", [128, NST * NT])
    dpar = din("dpar", [128, NT])
    w0T = din("w0T", [D, D])
    b0c = din("b0c", [128, NT])
    wqT = din("wqT", [D, D])
    wkT = din("wkT", [D, D])
    wvT = din("wvT", [D, D])
    woT = din("woT", [D, D])
    gate = din("gate", [1, NH])
    w1T = din("w1T", [D, D])
    b1c = din("b1c", [128, NT])
    cosTd = din("cosT", [D, T])
    sinTd = din("sinT", [D, T])
    identd = din("ident", [128, 128])
    out = nc.dram_tensor("y", [T, D], F32, kind="ExternalOutput").ap()

    def scr(name, shape, dt):
        return nc.dram_tensor(name, shape, dt, kind="Internal").ap()

    s_bc = scr("s_bc", [NSC, 32, SC], BF16)
    s_sign = scr("s_sign", [D, D], BF16)
    s_wq1 = scr("s_wq1", [D, D], BF16)
    s_woT = scr("s_woT", [D, D], BF16)
    s_qT = scr("s_qT", [D, T], BF16)
    s_kT = scr("s_kT", [D, T], BF16)
    s_v = scr("s_v", [T, D], BF16)
    s_row = scr("s_row", [8, 2048], F32)
    s_R = scr("s_R", [D, T], BF16)
    s_qTr = scr("s_qTr", [D, T], BF16)
    s_kTr = scr("s_kTr", [D, T], BF16)

    dGP = Ctr("dGP"); dSY = Ctr("dSY")
    cTE = Ctr("cTE"); cVE = Ctr("cVE"); cSC = Ctr("cSC"); cGP = Ctr("cGP")

    tap_info = {}

    with ExitStack() as ctx:
        def sbuf(name, shape, dt):
            return ctx.enter_context(nc.sbuf_tensor(name, shape, dt))

        R = sbuf("R", [128, NT * CH], BF16)
        U1 = sbuf("U1", [128, NT * UW], BF16)
        U23 = sbuf("U23", [128, NT * CH], BF16)
        BC4 = sbuf("BC4", [128, 16384], BF16)
        XC2 = sbuf("XC2", [128, NT * SC], BF16)
        KB2 = sbuf("KB2", [128, 8192], BF16)
        MSC = sbuf("MSC", [128, 4096], BF16)
        F32A = sbuf("F32A", [128, 3584], F32)
        F32B = sbuf("F32B", [128, 2048], F32)
        PAR = sbuf("PAR", [128, 768], F32)
        PARB = sbuf("PARB", [128, 4640], BF16)
        ROWS = sbuf("ROWS", [1, 4096], F32)
        OUTS = sbuf("OUTS", [128, 1024], F32)
        IDEN = sbuf("IDEN", [128, 128], F32)
        ONER = sbuf("ONER", [1, 128], F32)
        IDENB = sbuf("IDENB", [128, 128], BF16)
        GS = sbuf("GS", [1, 32], F32)
        CAPS = sbuf("CAPS", [128, 8], F32)

        PS3 = ctx.enter_context(nc.psum_tensor("PS3", [128, 1536], F32))
        pb = [PS3[:, 512 * i:512 * (i + 1)] for i in range(3)]
        pb += [ctx.enter_context(nc.psum_tensor(f"pb{i}", [128, 512], F32))
               for i in range(3, 7)]
        pbt = ctx.enter_context(nc.psum_tensor("pbt", [128, 1024], BF16))

        P_convw = 0; P_convb = 64; P_dtb = 80; P_dpar = 96
        P_b0 = 112; P_b1 = 128; P_alpha = 144; P_ws1 = 160
        P_aneg = 176
        P_hc = 432
        P_meanf = 688
        B_dtp = 0
        B_xp = 2048
        B_ones = 4608
        B_mean = 4610
        B_norm = 4612


        def gdma(dst, src, waits=(), slow=False):
            P.op("gp", lambda E, d=dst, s=src, sl=slow: E.dma_start(
                out=d, in_=s, allow_slow_non_contiguous=sl),
                 waits=waits, inc=dGP, k=16)
            return dGP.n

        def sdma(dst, src, waits=(), transpose=False):
            P.op("sy", lambda E, d=dst, s=src, t=transpose:
                 E.dma_start(out=d, in_=s, transpose=t),
                 waits=waits, inc=dSY, k=16)
            return dSY.n

        def te(fn, waits=()):
            P.op("te", fn, waits=waits, inc=cTE)
            return cTE.n

        def ve(fn, waits=()):
            P.op("ve", fn, waits=waits, inc=cVE)
            return cVE.n

        def sca(fn, waits=()):
            P.op("sc", fn, waits=waits, inc=cSC)
            return cSC.n

        def tap(name, ap, shape, dt=F32):
            if name not in taps:
                return
            t = nc.dram_tensor("tap_" + name, shape, dt, kind="ExternalOutput").ap()
            tap_info[name] = True
            w = [(cTE, cTE.n), (cVE, cVE.n), (cSC, cSC.n), (dSY, dSY.n)]
            gdma(t, ap, waits=w)

        gdma(PARB[:, B_dtp:B_dtp + 2048], dtprojT_d[:, :])
        gdma(PARB[:, B_xp:B_xp + 2560].rearrange("p (i q) -> p i q", q=160),
             xprojT_d.rearrange("(i p) q -> p i q", p=128))
        gdma(PAR[:, P_convw:P_convw + 64], convw[:, :])
        gdma(PAR[:, P_convb:P_convb + 16], convb[:, :])
        gdma(PAR[:, P_dtb:P_dtb + 16], dtprojb[:, :])
        gdma(PAR[:, P_dpar:P_dpar + 16], dpar[:, :])
        gdma(PAR[:, P_b0:P_b0 + 16], b0c[:, :])
        gdma(PAR[:, P_b1:P_b1 + 16], b1c[:, :])
        gdma(PAR[:, P_aneg:P_aneg + 256], alog[:, :])
        gdma(IDEN[:, :], identd[:, :])
        gdma(IDENB[:, :], identd[:, :])
        gdma(ROWS[0:1, 0:16], gate[:, :])
        d_par = dGP.n
        sca(lambda E: E.activation(PAR[:, P_aneg:P_aneg + 256],
                                   PAR[:, P_aneg:P_aneg + 256], AF.Exp),
            waits=[(dGP, d_par)])
        ve(lambda E: E.tensor_scalar_mul(PAR[:, P_aneg:P_aneg + 256],
                                         PAR[:, P_aneg:P_aneg + 256], -1.0),
           waits=[(cSC, cSC.n)])
        ve(lambda E: E.memset(PARB[:, B_ones:B_ones + 1], 1.0))
        ve(lambda E: E.memset(ONER[0:1, :], 1.0))
        ve(lambda E: E.memset(PAR[:, P_meanf:P_meanf + 1], 1.0 / 2048.0))
        ve(lambda E: E.memset(CAPS[:, 0:1], 1e-5))
        ve(lambda E: E.memset(CAPS[:, 1:2], -0.5))
        ve(lambda E: E.memset(CAPS[:, 2:3], -1.0))
        ve(lambda E: E.memset(CAPS[:, 3:4], 1.0))
        ve(lambda E: E.memset(CAPS[:, 4:5], SOFTBIAS))
        ve(lambda E: E.memset(CAPS[:, 5:6], INVSQ))
        m_ones = ve(lambda E: E.memset(PARB[:, B_mean:B_mean + 1], 1.0 / 2048.0))
        ve(lambda E: E.memset(PAR[:, P_hc:P_hc + 256], 0.0))
        m_ut0 = ve(lambda E: E.memset(U1[:, :], 0.0))
        m_gs = sca(lambda E: E.activation(GS[0:1, 0:16], ROWS[0:1, 0:16],
                                          AF.Sigmoid), waits=[(dGP, d_par)])
        m_g1m = ve(lambda E: E.tensor_scalar(GS[0:1, 16:32], GS[0:1, 0:16],
                                             -1.0, 1.0, op0=ALU.mult, op1=ALU.add),
                   waits=[(cSC, m_gs)])

        for i in range(NT):
            kb = KB2[:, (i % 2) * 2048:(i % 2) * 2048 + 2048]
            dW = gdma(kb, w0T[128 * i:128 * (i + 1), :],
                      waits=[(cVE, cVE.n), (cSC, cSC.n)])
            m1 = ve(lambda E, b=kb: E.tensor_scalar(
                MSC[:, 0:2048], b, 0.0, None, op0=ALU.is_ge),
                waits=[(dGP, dW), (dSY, dSY.n)])
            m2 = ve(lambda E: E.tensor_scalar(
                MSC[:, 0:2048], MSC[:, 0:2048], 2.0, -1.0,
                op0=ALU.mult, op1=ALU.add))
            sdma(s_sign[128 * i:128 * (i + 1), :], MSC[:, 0:2048],
                 waits=[(cVE, m2)])
            ma = sca(lambda E, b=kb: E.activation(MSC[:, 2048:4096], b, AF.Abs),
                     waits=[(dGP, dW), (cTE, cTE.n)])
            for j in range(4):
                te(lambda E, j=j, i=i: E.matmul(
                    pb[3 + j][0:1, :], PARB[:, B_mean:B_mean + 1],
                    MSC[:, 2048 + 512 * j:2048 + 512 * (j + 1)],
                    start=(i == 0), stop=(i == NT - 1)),
                   waits=[(cSC, ma), (cVE, m_ones)])
        m_amm = cTE.n
        m_acp = 0
        for j in range(4):
            m_acp = sca(lambda E, j=j: E.copy(ROWS[0:1, 512 * j:512 * (j + 1)],
                                              pb[3 + j][0:1, :]),
                        waits=[(cTE, m_amm)])
        d_aw = gdma(s_row[1:2, 0:2048], ROWS[0:1, 0:2048], waits=[(cSC, m_acp)])
        gdma(PAR[:, P_alpha:P_alpha + 16],
             s_row[1, 0:2048].rearrange("(i p) -> p i", p=128),
             waits=[(dGP, d_aw)], slow=True)

        def w1_prep():
            mxm = 0
            for i in range(NT):
                kb = KB2[:, (i % 2) * 2048:(i % 2) * 2048 + 2048]
                dW = gdma(kb, w1T[128 * i:128 * (i + 1), :],
                          waits=[(cVE, cVE.n), (cSC, cSC.n)])
                ma = sca(lambda E, b=kb: E.activation(MSC[:, 0:2048], b, AF.Abs),
                         waits=[(dGP, dW), (cVE, mxm)])
                if i == 0:
                    mxm = ve(lambda E: E.tensor_copy(F32A[:, 0:2048], MSC[:, 0:2048]),
                             waits=[(cSC, ma)])
                else:
                    mxm = ve(lambda E: E.tensor_tensor(
                        F32A[:, 0:2048], F32A[:, 0:2048], MSC[:, 0:2048], op=ALU.max),
                        waits=[(cSC, ma)])
            width = 64
            while width >= 1:
                dsh = gdma(F32B[0:width, 0:2048], F32A[width:2 * width, 0:2048],
                           waits=[(cVE, mxm)])
                mxm = ve(lambda E, w=width: E.tensor_tensor(
                    F32A[0:w, 0:2048], F32A[0:w, 0:2048], F32B[0:w, 0:2048],
                    op=ALU.max), waits=[(dGP, dsh)])
                width //= 2
            m_ws = ve(lambda E: E.tensor_scalar(ROWS[0:1, 0:2048],
                                                F32A[0:1, 0:2048], 1e-8, 1.0 / 127.0,
                                                op0=ALU.max, op1=ALU.mult),
                      waits=[(dGP, dGP.n)])
            m_wsr = ve(lambda E: E.reciprocal(ROWS[0:1, 2048:4096],
                                              ROWS[0:1, 0:2048]),
                       waits=[(cVE, m_ws)])
            d_ww = gdma(s_row[2:3, 0:2048], ROWS[0:1, 0:2048], waits=[(cVE, m_ws)])
            gdma(PAR[:, P_ws1:P_ws1 + 16],
                 s_row[2, 0:2048].rearrange("(i p) -> p i", p=128),
                 waits=[(dGP, d_ww)], slow=True)
            for j in range(4):
                te(lambda E, j=j: E.matmul(
                    pb[j][:, :], ONER[0:1, :],
                    ROWS[0:1, 2048 + 512 * j:2048 + 512 * (j + 1)],
                    start=True, stop=True),
                   waits=[(cVE, m_wsr), (cVE, m_ones), (cSC, cSC.n)])
            m_wmm = cTE.n
            m_wcp = 0
            for j in range(4):
                m_wcp = ve(lambda E, j=j: E.tensor_copy(
                    F32B[:, 512 * j:512 * (j + 1)], pb[j][:, :]),
                    waits=[(cTE, m_wmm)])
            for i in range(NT):
                kb = KB2[:, (i % 2) * 2048:(i % 2) * 2048 + 2048]
                dW = gdma(kb, w1T[128 * i:128 * (i + 1), :],
                          waits=[(cVE, cVE.n), (dSY, dSY.n)])
                mq1 = ve(lambda E, b=kb: E.tensor_tensor(
                    F32A[:, 0:2048], b, F32B[:, 0:2048], op=ALU.mult),
                    waits=[(dGP, dW), (cVE, m_wcp)])
                mq2 = ve(lambda E: E.tensor_scalar(
                    MSC[:, 0:2048], F32A[:, 0:2048], MAGIC, MAGIC,
                    op0=ALU.add, op1=ALU.subtract))
                sdma(s_wq1[128 * i:128 * (i + 1), :], MSC[:, 0:2048],
                     waits=[(cVE, mq2)])

        for i in range(NT):
            kb = KB2[:, (i % 2) * 2048:(i % 2) * 2048 + 2048]
            dW = gdma(kb, woT[128 * i:128 * (i + 1), :], waits=[(dSY, dSY.n)])
            sdma(s_woT[128 * i:128 * (i + 1), :], kb, waits=[(dGP, dW)])
        m_prep_ve = cVE.n
        m_prep_sc = cSC.n
        m_prep_sy = dSY.n

        sq_marks = []

        def ln_feat(dst_cols, src_col, width, wait_ve=0, wait_x=()):
            sqs = [0, 512]
            mm_sq = {}
            for i in range(NT):
                rsl = R[:, CH * i + src_col:CH * i + src_col + width]
                w_sq = []
                if i >= 2:
                    w_sq = [(cTE, mm_sq[i - 2])]
                msq = ve(lambda E, r=rsl, s=sqs[i % 2]: E.tensor_tensor(
                    F32B[:, s:s + width], r, r, op=ALU.mult),
                    waits=w_sq + (list(wait_x) if i == 0 else []))
                w_mm = []
                if i == 0:
                    w_mm = [(cVE, max(wait_ve, cVE.n)), (cSC, cSC.n)] + list(wait_x)
                te(lambda E, r=rsl, i=i: E.matmul(
                    pb[3][0:1, 0:width], PARB[:, B_mean:B_mean + 1], r,
                    start=(i == 0), stop=(i == NT - 1)), waits=w_mm)
                mm_sq[i] = te(lambda E, s=sqs[i % 2], i=i: E.matmul(
                    pb[4][0:1, 0:width], PAR[:, P_meanf:P_meanf + 1],
                    F32B[:, s:s + width],
                    start=(i == 0), stop=(i == NT - 1)), waits=[(cVE, msq)])
            m_stat = cTE.n
            mvc = ve(lambda E: E.tensor_copy(ROWS[0:1, 1024:1024 + width],
                                             pb[3][0:1, 0:width]),
                     waits=[(cTE, cTE.n)])
            mv = ve(lambda E: E.tensor_tensor(
                ROWS[0:1, 0:width], ROWS[0:1, 1024:1024 + width],
                ROWS[0:1, 1024:1024 + width], op=ALU.mult))
            mv = ve(lambda E: E.tensor_tensor(
                ROWS[0:1, 0:width], pb[4][0:1, 0:width], ROWS[0:1, 0:width],
                op=ALU.subtract))
            ms = sca(lambda E: E.activation(ROWS[0:1, 512:512 + width],
                                            ROWS[0:1, 0:width],
                                            AF.Ln, bias=CAPS[0:1, 0:1]),
                     waits=[(cVE, mv), (cTE, cTE.n)])
            ms = sca(lambda E: E.activation(ROWS[0:1, 512:512 + width],
                                            ROWS[0:1, 512:512 + width],
                                            AF.Exp, scale=CAPS[0:1, 1:2]))
            mn = ve(lambda E: E.tensor_scalar(ROWS[0:1, 1536:1536 + width],
                                              ROWS[0:1, 0:width], 1e-5, None,
                                              op0=ALU.add), waits=[(cSC, ms)])
            mn = ve(lambda E: E.tensor_tensor(
                ROWS[0:1, 1536 + width:1536 + 2 * width] if width <= 256 else
                ROWS[0:1, 0:width],
                ROWS[0:1, 512:512 + width], ROWS[0:1, 512:512 + width],
                op=ALU.mult))
            mn = ve(lambda E: E.tensor_tensor(
                ROWS[0:1, 0:width], ROWS[0:1, 0:width],
                ROWS[0:1, 1536:1536 + width], op=ALU.mult))
            mn = ve(lambda E: E.tensor_scalar(
                ROWS[0:1, 0:width], ROWS[0:1, 0:width], -0.5, 1.5,
                op0=ALU.mult, op1=ALU.add))
            mn = ve(lambda E: E.tensor_tensor(
                ROWS[0:1, 512:512 + width], ROWS[0:1, 512:512 + width],
                ROWS[0:1, 0:width], op=ALU.mult))
            te(lambda E: E.matmul(pb[5][:, 0:width], ONER[0:1, :],
                                  ROWS[0:1, 1024:1024 + width],
                                  start=True, stop=True), waits=[(cVE, mvc)])
            m_bc = te(lambda E: E.matmul(
                pb[6][:, 0:width], ONER[0:1, :],
                ROWS[0:1, 512:512 + width], start=True, stop=True),
                waits=[(cVE, mn)])
            last = 0
            for i in range(NT):
                rsl = R[:, CH * i + src_col:CH * i + src_col + width]
                dsl = dst_cols(i)
                ve(lambda E, r=rsl, d=dsl: E.tensor_tensor(
                    d, r, pb[5][:, 0:width], op=ALU.subtract),
                   waits=[(cTE, m_bc)] if i == 0 else [])
                last = ve(lambda E, d=dsl: E.tensor_tensor(
                    d, d, pb[6][:, 0:width], op=ALU.mult))
            return last

        def mamba_sc(s, first_wait=(), bc_free=0, feeder=None):
            base = SC * s
            rbase = SC * (s % SCPC)
            if s > 0:
                ve(lambda E: E.tensor_copy(
                    U1.rearrange("p (i c) -> p i c", c=UW)[:, :, 5:8],
                    U1.rearrange("p (i c) -> p i c", c=UW)
                    [:, :, UPAD + SC - 3:UPAD + SC]), waits=list(first_wait))
                first_wait = ()
            wx = list(first_wait)
            if s == 0:
                wx += [(cVE, m_ut0)]
            m_u = ln_feat(lambda i: U1[:, i * UW + UPAD:i * UW + UPAD + SC],
                          rbase, SC, wait_x=wx + [(cTE, cTE.n)])
            if stage == 15:
                tap("rows", ROWS[0:1, :], [1, 4096], F32)
                return m_u
            silu_marks = {}
            d_bcw = dGP.n
            for i in range(NT):
                ub = U1[:, i * UW:(i + 1) * UW]
                ac = 2560 + 512 * (i % 2)
                w0_ = []
                if i < 2:
                    w0_ = [(cSC, cSC.n)]
                else:
                    w0_ = [(cSC, silu_marks[i - 2])]
                ve(lambda E, u=ub, a=ac, i=i: E.tensor_scalar(
                    F32A[:, a:a + SC], u[:, 5:5 + SC],
                    PAR[:, P_convw + 4 * i:P_convw + 4 * i + 1], None,
                    op0=ALU.mult), waits=w0_)
                for j in range(1, 4):
                    ve(lambda E, u=ub, a=ac, i=i, j=j: E.scalar_tensor_tensor(
                        F32A[:, a:a + SC], u[:, 5 + j:5 + j + SC],
                        PAR[:, P_convw + 4 * i + j:P_convw + 4 * i + j + 1],
                        F32A[:, a:a + SC], op0=ALU.mult, op1=ALU.add))
                m_cv = cVE.n
                silu_marks[i] = sca(lambda E, i=i, a=ac: E.activation(
                    XC2[:, i * SC:(i + 1) * SC], F32A[:, a:a + SC], AF.Silu,
                    bias=PAR[:, P_convb + i:P_convb + i + 1]),
                    waits=[(cVE, m_cv)])
            m_xc = cSC.n
            if stage == 17:
                return m_xc
            for i in range(NT):
                w = [(cSC, m_xc if i == 0 else 0), (cVE, cVE.n if i == 0 else 0)]
                w = [x for x in w if x[1]]
                te(lambda E, i=i: E.matmul(
                    pb[3][:, :], PARB[:, B_xp + 160 * i:B_xp + 160 * i + 128],
                    XC2[:, i * SC:(i + 1) * SC],
                    start=(i == 0), stop=(i == NT - 1)), waits=w)
                te(lambda E, i=i: E.matmul(
                    pb[4][0:32, :],
                    PARB[:, B_xp + 160 * i + 128:B_xp + 160 * i + 160],
                    XC2[:, i * SC:(i + 1) * SC],
                    start=(i == 0), stop=(i == NT - 1)))
            m_proj = cTE.n
            m_dtin = sca(lambda E: E.copy(MSC[:, 2048:2560], pb[3][:, :]),
                         waits=[(cTE, m_proj), (cVE, cVE.n)])
            m_bcv = sca(lambda E: E.copy(F32A[0:32, 2560:2560 + SC],
                                         pb[4][0:32, :]),
                        waits=[(cTE, m_proj), (cVE, cVE.n)])
            d_bcw = gdma(s_bc[s, :, :], F32A[0:32, 2560:2560 + SC],
                         waits=[(cSC, m_bcv)])
            d_ball = sdma(BC4[:, 0:8192].rearrange("p (n c) -> p n c", c=SC),
                          s_bc[s:s + 1, 0:16, :].broadcast_to([128, 16, SC]),
                          waits=[(dGP, d_bcw), (cVE, bc_free)])
            d_call = sdma(BC4[:, 8192:16384].rearrange("p (n c) -> p n c", c=SC),
                          s_bc[s:s + 1, 16:32, :].broadcast_to([128, 16, SC]),
                          waits=[])
            if stage == 18:
                return cVE.n
            prev_sp = (0, 0)
            m_last = cVE.n
            for i in range(NT):
                dcol = 512 * (i % 2)
                wmt = [(cSC, m_dtin)]
                if i == 0:
                    wmt.append((cVE, m_u))
                if prev_sp[0]:
                    wmt += [(cSC, prev_sp[0]), (cVE, prev_sp[1])]
                mt = te(lambda E, i=i: E.matmul(
                    pb[5][:, :], PARB[:, B_dtp + 128 * i:B_dtp + 128 * (i + 1)],
                    MSC[:, 2048:2560], start=True, stop=True), waits=wmt)
                if stage == 1801:
                    return mt
                ms1 = sca(lambda E, i=i, d=dcol: E.activation(
                    F32A[:, d:d + SC], pb[5][:, :], AF.Abs,
                    bias=PAR[:, P_dtb + i:P_dtb + i + 1]), waits=[(cTE, mt)])
                if stage == 1802:
                    return cVE.n
                sca(lambda E, d=dcol: E.activation(
                    F32A[:, d:d + SC], F32A[:, d:d + SC], AF.Exp, scale=CAPS[:, 2:3]))
                if stage == 1803:
                    return cVE.n
                ms3 = sca(lambda E, d=dcol: E.activation(
                    F32A[:, d:d + SC], F32A[:, d:d + SC], AF.Ln, bias=CAPS[:, 3:4]))
                if stage == 1804:
                    return cVE.n
                mr = ve(lambda E, i=i: E.tensor_scalar(
                    F32B[:, 0:SC], pb[5][:, :],
                    PAR[:, P_dtb + i:P_dtb + i + 1], 0.0,
                    op0=ALU.add, op1=ALU.max), waits=[(cTE, mt), (cSC, ms1)])
                prev_sp = (ms1, mr)
                if stage == 1805:
                    return cVE.n
                mdelta = ve(lambda E, d=dcol: E.tensor_tensor(
                    F32A[:, d:d + SC], F32A[:, d:d + SC], F32B[:, 0:SC],
                    op=ALU.add), waits=[(cSC, ms3)])
                if stage == 181:
                    return mdelta
                mw_w = [(dSY, dSY.n)] if (s == 0 and i == 0) else []
                m_w = ve(lambda E, i=i, d=dcol: E.tensor_tensor(
                    MSC[:, 0:512], F32A[:, d:d + SC],
                    XC2[:, i * SC:(i + 1) * SC], op=ALU.mult), waits=mw_w)
                ve(lambda E, i=i: E.tensor_scalar(
                    MSC[:, 2560:3072],
                    U1[:, i * UW + UPAD:i * UW + UPAD + SC],
                    PAR[:, P_dpar + i:P_dpar + i + 1], None, op0=ALU.mult))
                if stage == 182:
                    return cVE.n
                h_marks = {}
                for n in range(NST):
                    acol = 1024 + 512 * (n % 2)
                    wsa = [(cVE, mdelta)]
                    if n >= 2:
                        wsa.append((cVE, h_marks[n - 2]))
                    sa_ = sca(lambda E, i=i, n=n, a=acol, d=dcol: E.activation(
                        F32A[:, a:a + SC], F32A[:, d:d + SC], AF.Exp,
                        scale=PAR[:, P_aneg + 16 * i + n:P_aneg + 16 * i + n + 1]),
                        waits=wsa)
                    wb = [(dSY, d_ball)] if n == 0 else []
                    ve(lambda E, n=n: E.tensor_tensor(
                        MSC[:, 512:1024], MSC[:, 0:512],
                        BC4[:, 512 * n:512 * (n + 1)], op=ALU.mult), waits=wb)
                    m_h = ve(lambda E, i=i, n=n, a=acol: E.tensor_tensor_scan(
                        MSC[:, 1024:1536], F32A[:, a:a + SC], MSC[:, 512:1024],
                        PAR[:, P_hc + 16 * i + n:P_hc + 16 * i + n + 1],
                        op0=ALU.mult, op1=ALU.add), waits=[(cSC, sa_)])
                    h_marks[n] = m_h
                    if stage == 183:
                        return m_h
                    wc = [(dSY, d_call)] if n == 0 else []
                    ve(lambda E, n=n: E.tensor_tensor(
                        MSC[:, 1536:2048], MSC[:, 1024:1536],
                        BC4[:, 8192 + 512 * n:8192 + 512 * (n + 1)],
                        op=ALU.mult), waits=wc)
                    m_ya = ve(lambda E: E.tensor_tensor(
                        MSC[:, 2560:3072], MSC[:, 2560:3072],
                        MSC[:, 1536:2048], op=ALU.add))
                    ve(lambda E, i=i, n=n: E.tensor_copy(
                        PAR[:, P_hc + 16 * i + n:P_hc + 16 * i + n + 1],
                        MSC[:, 1535:1536]))
                    if stage == 184:
                        return m_ya
                m_last = ve(lambda E, i=i, b=rbase: E.tensor_tensor(
                    R[:, CH * i + b:CH * i + b + SC],
                    R[:, CH * i + b:CH * i + b + SC],
                    MSC[:, 2560:3072], op=ALU.add))
                if stage == 19:
                    return m_last
                if feeder:
                    feeder.pop(0)()
                    if feeder:
                        feeder.pop(0)()
            return m_last

        def ffn0(c, m_u2):
            lastv = 0
            for o in range(NT):
                kb = KB2[:, (o % 2) * 2048:(o % 2) * 2048 + 2048]
                wkb = [(cTE, cTE.n)]
                if c == 0 and o < 2:
                    wkb += [(cVE, m_prep_ve), (cSC, m_prep_sc)]
                dW = sdma(kb.rearrange("p (i q) -> p i q", q=128),
                          s_sign[:, 128 * o:128 * (o + 1)]
                          .rearrange("(i p) q -> p i q", p=128), waits=wkb)
                for half in range(2):
                    bank = pb[half]
                    for i in range(NT):
                        w = []
                        if i == 0:
                            w = [(dSY, dW), (cVE, max(m_u2, lastv)),
                                 (cSC, cSC.n)]
                        te(lambda E, i=i, h=half, b=bank, kb=kb: E.matmul(
                            b[:, :], kb[:, 128 * i:128 * (i + 1)],
                            U23[:, CH * i + 512 * h:CH * i + 512 * (h + 1)],
                            start=(i == 0), stop=(i == NT - 1)), waits=w)
                    m_mm = cTE.n
                    ve(lambda E, o=o, b=bank: E.tensor_scalar(
                        F32B[:, 1024:1536], b[:, :],
                        PAR[:, P_alpha + o:P_alpha + o + 1],
                        PAR[:, P_b0 + o:P_b0 + o + 1],
                        op0=ALU.mult, op1=ALU.add), waits=[(cTE, m_mm)])
                    rsl = R[:, CH * o + 512 * half:CH * o + 512 * half + 512]
                    lastv = ve(lambda E, r=rsl: E.tensor_tensor(
                        r, r, F32B[:, 1024:1536], op=ALU.add))
            return lastv

        def qkv_feeder(c, m_u3):
            base = CH * c
            st = {"ev": {}, "sp": {}, "pend": None,
                  "mmdone": {0: cTE.n, 1: cTE.n}}
            items = []

            def qk_load(wT, o, slot):
                def emit():
                    kb = KB2[:, (o % 2) * 2048:(o % 2) * 2048 + 2048]
                    slot[0] = gdma(kb.rearrange("p (i q) -> p i q", q=128),
                                   wT[:, 128 * o:128 * (o + 1)]
                                   .rearrange("(i p) q -> p i q", p=128),
                                   waits=[(cTE, st["mmdone"].get(o % 2, 0))])
                return emit

            def qk_mm(wT, o, slot):
                def emit():
                    kb = KB2[:, (o % 2) * 2048:(o % 2) * 2048 + 2048]
                    dW = slot[0]
                    mms = []
                    for half in range(2):
                        bk = (2 * o + half) % 3 if True else 0
                        bank = pb[bk]
                        for i in range(NT):
                            w = []
                            if i == 0:
                                w = [(dGP, dW), (cVE, m_u3),
                                     (cSC, st["ev"].get(bk, 0))]
                            te(lambda E, i=i, b=bank, kb=kb, h=half: E.matmul(
                                b[:, :], kb[:, 128 * i:128 * (i + 1)],
                                U23[:, CH * i + 512 * h:CH * i + 512 * (h + 1)],
                                start=(i == 0), stop=(i == NT - 1)), waits=w)
                        mms.append((bk, cTE.n))
                    st["mmdone"][o % 2] = cTE.n
                    return mms

                return emit

            def qk_ev(sdst, o, mms):
                def emit():
                    marks = []
                    for half, (bk, m_mm) in enumerate(mms()):
                        m_ev = sca(lambda E, b=pb[mms()[half][0]], h=half: E.copy(
                            KB2[:, 4096 + 512 * h:4096 + 512 * (h + 1)], b[:, :]),
                            waits=[(cTE, m_mm), (dSY, st["sp"].get("qk", 0))])
                        st["ev"][bk] = m_ev
                        marks.append(m_ev)
                    st["sp"]["qk"] = sdma(
                        sdst[128 * o:128 * (o + 1), base:base + CH],
                        KB2[:, 4096:5120], waits=[(cSC, marks[-1])])
                return emit

            def v_mm(ob, tt, dwv):
                def emit():
                    if tt == 0:
                        dwv[0] = gdma(KB2.rearrange("p (i q) -> p i q", q=512),
                                      wvT[:, 512 * ob:512 * (ob + 1)]
                                      .rearrange("(i p) q -> p i q", p=128),
                                      waits=[(cTE, cTE.n), (dSY, dSY.n),
                                             (cSC, st["ev"].get(tt % 2, 0))])
                    bank = pb[tt % 2]
                    for i in range(NT):
                        w = []
                        if i == 0:
                            w = [(dGP, dwv[0]), (cVE, m_u3),
                                 (cSC, st["ev"].get(tt % 2, 0))]
                        te(lambda E, i=i, tt=tt, b=bank: E.matmul(
                            b[:, :],
                            U23[:, CH * i + 128 * tt:CH * i + 128 * (tt + 1)],
                            KB2[:, 512 * i:512 * (i + 1)],
                            start=(i == 0), stop=(i == NT - 1)), waits=w)
                    return [(tt % 2, cTE.n)]
                return emit

            def v_ev(ob, tt, mms):
                def emit():
                    bk, m_mm = mms()[0]
                    m_ev = sca(lambda E, b=pb[bk]: E.copy(F32B[:, 1536:2048],
                                                          b[:, :]),
                               waits=[(cTE, m_mm), (dGP, st["sp"].get("v", 0))])
                    st["ev"][bk] = m_ev
                    st["sp"]["v"] = gdma(
                        s_v[base + 128 * tt:base + 128 * (tt + 1),
                            512 * ob:512 * (ob + 1)], F32B[:, 1536:2048],
                        waits=[(cSC, m_ev)])
                return emit

            units = []
            for o in range(NT):
                units.append(("qk", wqT, s_qT, o))
            for o in range(NT):
                units.append(("qk", wkT, s_kT, o))
            for ob in range(4):
                dwv = [0]
                for tt in range(8):
                    units.append(("v", ob, tt, dwv))
            pend = [None]
            pre = []
            for u in units:
                if u[0] == "qk":
                    _, wT, sdst, o = u
                    res = {}
                    slot = [0]
                    ld = qk_load(wT, o, slot)
                    mm = qk_mm(wT, o, slot)
                    def item(mm=mm, res=res):
                        res["mms"] = mm()
                    ev = qk_ev(sdst, o, lambda res=res: res["mms"])
                else:
                    _, ob, tt, dwv = u
                    res = {}
                    ld = None
                    mm = v_mm(ob, tt, dwv)
                    def item(mm=mm, res=res):
                        res["mms"] = mm()
                    ev = v_ev(ob, tt, lambda res=res: res["mms"])
                pre.append(ld)
                units[len(pre) - 1] = (item, ev)
            for k, (item, ev) in enumerate(units):
                def combined(item=item, prev_ev=pend[0],
                             nld=(pre[k + 1] if k + 1 < len(pre) else None),
                             first=(pre[k] if k == 0 else None)):
                    if first is not None:
                        first()
                    if nld is not None:
                        nld()
                    if prev_ev is not None:
                        prev_ev()
                    item()
                items.append(combined)
                pend[0] = ev
            def final(prev_ev=pend[0]):
                if prev_ev is not None:
                    prev_ev()
            items.append(final)
            return items

        m_u3_prev = 0
        if stage < 2:
            _finish(P, cGP, dGP, dSY)
            return _close(nc, P)
        m_mamba = 0
        d_rspill = 0
        feeder = []
        for c in range(NCH):
            wR = []
            if c > 0:
                wR = [(dSY, d_rspill), (cVE, m_u3_prev)]
            d_xload = gdma(R.rearrange("p (i q) -> p i q", q=CH),
                           xT[:, CH * c:CH * (c + 1)]
                           .rearrange("(i p) q -> p i q", p=128), waits=wR)
            for sci in range(SCPC):
                s = SCPC * c + sci
                fw = []
                if sci == 0:
                    fw = [(dGP, d_xload)]
                m_mamba = mamba_sc(s, first_wait=fw, bc_free=m_mamba,
                                   feeder=feeder)
                if stage == 22 and s == 0:
                    tap("hc", PAR[:, :], [128, 768], F32)
                    _finish(P, cGP, dGP, dSY)
                    return _close(nc, P)
                if stage == 23 and s == 1:
                    tap("R_c0", R[:, :], [128, NT * CH], BF16)
                    _finish(P, cGP, dGP, dSY)
                    return _close(nc, P)
                if stage in (15, 17, 18, 1801, 1802, 1803, 1804, 1805, 181, 182, 183, 184, 19, 2):
                    tap("R_sc0", R[:, :], [128, NT * CH], BF16)
                    tap("u_sc0", U1[:, :], [128, NT * UW], BF16)
                    tap("xc_sc0", XC2[:, :], [128, NT * SC], BF16)
                    _finish(P, cGP, dGP, dSY)
                    return _close(nc, P)
            for fn in feeder:
                fn()
            feeder = []
            m_u2 = 0
            for half in range(2):
                m_u2 = ln_feat(lambda i, h=half: U23[:, CH * i + 512 * h:
                                                     CH * i + 512 * h + 512],
                               512 * half, 512,
                               wait_ve=m_mamba, wait_x=[(cTE, cTE.n)])
            m_ffn = ffn0(c, m_u2)
            if stage == 25:
                tap("R_ffn0", R[:, :], [128, NT * CH], BF16)
                _finish(P, cGP, dGP, dSY)
                return _close(nc, P)
            if stage == 24:
                tap("u2", U23[:, :], [128, NT * CH], BF16)
                _finish(P, cGP, dGP, dSY)
                return _close(nc, P)
            m_u3 = 0
            for half in range(2):
                m_u3 = ln_feat(lambda i, h=half: U23[:, CH * i + 512 * h:
                                                     CH * i + 512 * h + 512],
                               512 * half, 512,
                               wait_ve=m_ffn, wait_x=[(cTE, cTE.n)])
            d_rspill = sdma(s_R[:, CH * c:CH * (c + 1)]
                            .rearrange("(i p) q -> p i q", p=128),
                            R.rearrange("p (i q) -> p i q", q=CH),
                            waits=[(cTE, cTE.n), (cVE, m_u3)])
            feeder = qkv_feeder(c, m_u3)
            m_u3_prev = m_u3
        for fn in feeder:
            fn()
        feeder = []
        w1_prep()
        tap("R_A", R[:, :], [128, NT * CH], BF16)
        if stage < 4:
            _finish(P, cGP, dGP, dSY)
            return _close(nc, P)
        tap("qT", None, None) if False else None

        for h in range(NH):
            cb = (h % 2) * 2048
            hp = (h + 8) % 16
            d_kt = sdma(KB2[:, cb:cb + 2048], s_kT[128 * h:128 * (h + 1), :],
                        waits=[(cTE, cTE.n), (cVE, cVE.n)])
            d_kp = sdma(KB2[:, 4096 + cb:4096 + cb + 2048],
                        s_kT[128 * hp:128 * (hp + 1), :], waits=[])
            d_cs = gdma(U23[:, 0:2048], cosTd[128 * h:128 * (h + 1), :],
                        waits=[(cVE, cVE.n), (cTE, cTE.n)])
            d_sn = gdma(U23[:, 2048:4096], sinTd[128 * h:128 * (h + 1), :],
                        waits=[])
            d_vh = sdma(XC2[:, 4096 + cb:4096 + cb + 2048]
                        .rearrange("p (j q) -> p j q", q=128),
                        s_v[:, 128 * h:128 * (h + 1)]
                        .rearrange("(j p) q -> p j q", p=128),
                        waits=[(cTE, cTE.n), (dGP, dGP.n)])
            ve(lambda E, cb=cb: E.tensor_tensor(
                U23[:, 4096:6144], KB2[:, cb:cb + 2048], U23[:, 0:2048],
                op=ALU.mult),
               waits=[(dSY, d_kp), (dGP, d_sn), (cSC, cSC.n), (cTE, cTE.n)])
            ve(lambda E, cb=cb: E.tensor_tensor(
                MSC[:, 2048:4096], KB2[:, 4096 + cb:4096 + cb + 2048],
                U23[:, 2048:4096], op=ALU.mult))
            mkr = ve(lambda E: E.tensor_tensor(
                U23[:, 4096:6144], U23[:, 4096:6144], MSC[:, 2048:4096],
                op=ALU.add))
            sdma(s_kTr[128 * h:128 * (h + 1), :], U23[:, 4096:6144],
                 waits=[(cVE, mkr)])
            mtr_ev = 0
            for g in range(2):
                mtr = 0
                for jj in range(8):
                    j = 8 * g + jj
                    wtr = [(cVE, mkr)]
                    if jj == 0 and mtr_ev:
                        wtr.append((cSC, mtr_ev))
                    mtr = te(lambda E, j=j, jj=jj: E.transpose(
                        pbt[:, 128 * jj:128 * (jj + 1)],
                        U23[:, 4096 + 128 * j:4096 + 128 * (j + 1)],
                        IDENB[:, :]), waits=wtr)
                mtr_ev = sca(lambda E, g=g, cb=cb: E.copy(
                    XC2[:, cb + 1024 * g:cb + 1024 * (g + 1)], pbt[:, :]),
                    waits=[(cTE, mtr)])
            d_qt = sdma(KB2[:, cb:cb + 2048], s_qT[128 * h:128 * (h + 1), :],
                        waits=[(cVE, cVE.n)])
            d_qp = sdma(KB2[:, 4096 + cb:4096 + cb + 2048],
                        s_qT[128 * hp:128 * (hp + 1), :], waits=[])
            ve(lambda E, cb=cb: E.tensor_tensor(
                U23[:, 6144:8192], KB2[:, cb:cb + 2048], U23[:, 0:2048],
                op=ALU.mult), waits=[(dSY, d_qp)])
            ve(lambda E, cb=cb: E.tensor_tensor(
                MSC[:, 2048:4096], KB2[:, 4096 + cb:4096 + cb + 2048],
                U23[:, 2048:4096], op=ALU.mult))
            mqr = ve(lambda E: E.tensor_tensor(
                U23[:, 6144:8192], U23[:, 6144:8192], MSC[:, 2048:4096],
                op=ALU.add))
            sdma(s_qTr[128 * h:128 * (h + 1), :], U23[:, 6144:8192],
                 waits=[(cVE, mqr)])
            mv1 = ve(lambda E, cb=cb: E.tensor_scalar_min(
                MSC[:, 2048:4096], XC2[:, cb:cb + 2048], 0.0),
                waits=[(cSC, mtr_ev)])
            ms1 = sca(lambda E: E.activation(MSC[:, 2048:4096], MSC[:, 2048:4096],
                                             AF.Exp), waits=[(cVE, mv1)])
            mv2 = ve(lambda E, cb=cb: E.tensor_relu(MSC[:, 0:2048],
                                                    XC2[:, cb:cb + 2048]))
            mkf = ve(lambda E: E.tensor_tensor(MSC[:, 0:2048], MSC[:, 0:2048],
                                               MSC[:, 2048:4096], op=ALU.add),
                     waits=[(cSC, ms1)])
            for j in range(NT):
                w = []
                if j == 0:
                    w = [(cVE, mkf), (dSY, d_vh), (cSC, cSC.n)]
                te(lambda E, j=j, cb=cb: E.matmul(
                    pb[0][:, 0:128], MSC[:, 128 * j:128 * (j + 1)],
                    XC2[:, 4096 + cb + 128 * j:4096 + cb + 128 * (j + 1)],
                    start=(j == 0), stop=(j == NT - 1)), waits=w)
                te(lambda E, j=j: E.matmul(
                    pb[1][0:1, 0:128], PARB[:, B_ones:B_ones + 1],
                    MSC[:, 128 * j:128 * (j + 1)],
                    start=(j == 0), stop=(j == NT - 1)))
            m_mem = cTE.n
            sca(lambda E, h=h: E.copy(U1[:, 128 * h:128 * (h + 1)],
                                      pb[0][:, 0:128]),
                waits=[(cTE, m_mem)] + ([(cVE, cVE.n)] if h == 0 else []))
            m_e2 = sca(lambda E, h=h: E.copy(ROWS[0:1, 128 * h:128 * (h + 1)],
                                             pb[1][0:1, 0:128]),
                       waits=[(cTE, m_mem)])
        d_nw = gdma(s_row[3:4, 0:2048], ROWS[0:1, 0:2048], waits=[(cSC, cSC.n)])
        d_norm = gdma(PARB[:, B_norm:B_norm + 16],
                      s_row[3, 0:2048].rearrange("(h p) -> p h", p=128),
                      waits=[(dGP, d_nw)], slow=True)

        tap("memB", U1[:, 0:2048], [128, 2048], BF16)
        if stage < 5:
            _finish(P, cGP, dGP, dSY)
            return _close(nc, P)

        for qcc in range(2):
            qb0 = 1024 * qcc
            wR = [(cVE, cVE.n), (cTE, cTE.n)] if qcc > 0 else [(dSY, d_rspill)]
            d_rload = sdma(R.rearrange("p (i q) -> p i q", q=CH),
                           s_R[:, qb0:qb0 + CH]
                           .rearrange("(i p) q -> p i q", p=128), waits=wR)
            hq = [(qc2, h) for h in range(NH) for qc2 in range(2)]
            lds = {}
            mk = {}

            def issue_loads(j):
                qc2, h = hq[j]
                qb = qb0 + 512 * qc2
                cb = (h % 2) * 2048
                pm = mk.get(j - 2, {})
                if qc2 == 0:
                    pm4 = mk.get(j - 3, {})
                    wkv = ([(cTE, pm4.get("av", 0))] if pm4 else
                           [(cTE, cTE.n)])
                    d_kt = sdma(KB2[:, cb:cb + 2048],
                                s_kTr[128 * h:128 * (h + 1), :], waits=wkv)
                    d_vh = sdma(KB2[:, 4096 + cb:4096 + cb + 2048]
                                .rearrange("p (j q) -> p j q", q=128),
                                s_v[:, 128 * h:128 * (h + 1)]
                                .rearrange("(j p) q -> p j q", p=128),
                                waits=(wkv if pm4 else
                                       [(cTE, cTE.n), (dGP, dGP.n)]))
                    lds[("kv", h)] = (d_kt, d_vh)
                d_kt, d_vh = lds[("kv", h)]
                d_qt = sdma(MSC[:, 512 * (j % 2):512 * (j % 2) + 512],
                            s_qTr[128 * h:128 * (h + 1), qb:qb + 512],
                            waits=([(cTE, pm.get("mo", 0)),
                                    (cVE, pm.get("qdve", 0))] if pm else
                                   [(cTE, cTE.n), (cVE, cVE.n)]))
                lds[j] = (d_kt, d_vh, d_qt)

            issue_loads(0)
            issue_loads(1)
            for j, (qc2, h) in enumerate(hq):
                cb = (h % 2) * 2048
                qsl = MSC[:, 512 * (j % 2):512 * (j % 2) + 512]
                qf = MSC[:, 1536 + 512 * (j % 2):1536 + 512 * (j % 2) + 512]
                d_kt, d_vh, d_qt = lds[j]
                pm = mk.get(j - 1, {})
                m = {}
                mv1 = ve(lambda E, q=qsl: E.tensor_scalar_min(
                    MSC[:, 1024:1536], q, 0.0),
                    waits=[(dSY, d_qt), (cSC, pm.get("me2", 0))])
                ms1 = sca(lambda E: E.activation(
                    MSC[:, 1024:1536], MSC[:, 1024:1536], AF.Exp),
                    waits=[(cVE, mv1)])
                mv2 = ve(lambda E, q=qsl, f=qf: E.tensor_relu(f, q))
                m["qdve"] = mv2
                mqf = ve(lambda E, f=qf: E.tensor_tensor(
                    f, f, MSC[:, 1024:1536], op=ALU.add), waits=[(cSC, ms1)])
                m_exps = []
                mmks = []

                def emit_S(kt):
                    bank = pb[kt % 3]
                    w = []
                    if kt == 0:
                        w = [(dSY, d_kt), (cSC, pm.get("exp15", 0)),
                             (cVE, pm.get("comb", 0))]
                    elif kt in (1, 2):
                        w = [(cSC, pm.get("exp15", 0)),
                             (cVE, pm.get("comb", 0))]
                    else:
                        w = [(cSC, m_exps[kt - 3])]
                    mmk = te(lambda E, kt=kt, b=bank, cb=cb, q=qsl: E.matmul(
                        b[:, :], KB2[:, cb + 128 * kt:cb + 128 * (kt + 1)],
                        q, start=True, stop=True), waits=w)
                    me = sca(lambda E, kt=kt, b=bank: E.activation(
                        XC2[:, 512 * kt:512 * (kt + 1)], b[:, :], AF.Exp,
                        bias=CAPS[:, 4:5], scale=CAPS[:, 5:6]),
                        waits=[(cTE, mmk)])
                    m_exps.append(me)

                for kt in range(3):
                    emit_S(kt)
                for kt in range(NT):
                    if kt + 3 < NT:
                        emit_S(kt + 3)
                    w = [(cSC, m_exps[kt])]
                    if kt == 0:
                        w += [(dSY, d_vh), (cSC, pm.get("me1", 0)),
                              (cVE, pm.get("mr1", 0))]
                    te(lambda E, kt=kt, cb=cb: E.matmul(
                        pb[3][:, :],
                        KB2[:, 4096 + cb + 128 * kt:4096 + cb + 128 * (kt + 1)],
                        XC2[:, 512 * kt:512 * (kt + 1)],
                        start=(kt == 0), stop=(kt == NT - 1)), waits=w)
                    te(lambda E, kt=kt: E.matmul(
                        pb[4][0:1, :], PARB[:, B_ones:B_ones + 1],
                        XC2[:, 512 * kt:512 * (kt + 1)],
                        start=(kt == 0), stop=(kt == NT - 1)))
                m["exp15"] = m_exps[15]
                m["exp14"] = m_exps[14]
                m["av"] = cTE.n
                te(lambda E, h=h, f=qf: E.matmul(
                    pb[5][:, :], U1[:, 128 * h:128 * (h + 1)], f,
                    start=True, stop=True),
                   waits=[(cVE, mqf), (dGP, d_norm), (cSC, pm.get("me2", 0)),
                          (cVE, pm.get("mr2", 0))])
                m_mo = te(lambda E, h=h, f=qf: E.matmul(
                    pb[6][0:1, :], PARB[:, B_norm + h:B_norm + h + 1], f,
                    start=True, stop=True))
                m["mo"] = m_mo
                if j + 2 < len(hq):
                    issue_loads(j + 2)
                mr1 = ve(lambda E: E.reciprocal(ROWS[0:1, 0:512],
                                                pb[4][0:1, :]),
                         waits=[(cTE, m_mo)])
                m["mr1"] = mr1
                mr2 = ve(lambda E: E.reciprocal(ROWS[0:1, 512:1024],
                                                pb[6][0:1, :]))
                m["mr2"] = mr2
                mrb = ve(lambda E, h=h: E.tensor_scalar(
                    ROWS[0:1, 1024:1536], ROWS[0:1, 512:1024],
                    GS[0:1, h:h + 1], None, op0=ALU.mult),
                    waits=[(cTE, pm.get("rep", 0))])
                mrb = ve(lambda E, h=h: E.tensor_scalar(
                    ROWS[0:1, 1536:2048], ROWS[0:1, 0:512],
                    GS[0:1, 16 + h:16 + h + 1], None, op0=ALU.mult),
                    waits=[(cVE, m_g1m)])
                te(lambda E: E.matmul(
                    pb[0][:, :], ONER[0:1, :], ROWS[0:1, 1024:1536],
                    start=True, stop=True),
                   waits=[(cVE, mrb), (cSC, m_exps[15])])
                m["rep"] = te(lambda E: E.matmul(
                    pb[1][:, :], ONER[0:1, :], ROWS[0:1, 1536:2048],
                    start=True, stop=True), waits=[(cSC, m_exps[13])])
                m["sc"] = m["rep"]
                me1 = sca(lambda E: E.copy(MSC[:, 2560:3072], pb[3][:, :]),
                          waits=[(cTE, m["rep"])])
                m["me1"] = me1
                me2 = sca(lambda E: E.copy(MSC[:, 3072:3584], pb[5][:, :]))
                m["me2"] = me2
                mt1 = ve(lambda E: E.tensor_tensor(
                    MSC[:, 3584:4096], MSC[:, 2560:3072], pb[1][:, :],
                    op=ALU.mult), waits=[(cSC, me2), (cTE, m["rep"])])
                ve(lambda E: E.tensor_tensor(
                    MSC[:, 2560:3072], MSC[:, 3072:3584], pb[0][:, :],
                    op=ALU.mult))
                m["comb"] = ve(lambda E, h=h, q=qc2: E.tensor_tensor(
                    BC4[:, 1024 * h + 512 * q:1024 * h + 512 * (q + 1)],
                    MSC[:, 3584:4096], MSC[:, 2560:3072], op=ALU.add))
                mk[j] = m
            m_comb = cVE.n
            lastv = 0
            for o in range(NT):
                kb = KB2[:, (o % 2) * 2048:(o % 2) * 2048 + 2048]
                dW = sdma(kb.rearrange("p (i q) -> p i q", q=128),
                          s_woT[:, 128 * o:128 * (o + 1)]
                          .rearrange("(i p) q -> p i q", p=128),
                          waits=[(cTE, cTE.n)])
                for half in range(2):
                    bank = pb[2 + half]
                    for i in range(NT):
                        w = []
                        if i == 0:
                            w = [(dSY, dW), (cVE, max(m_comb, lastv)),
                                 (cSC, cSC.n)]
                        te(lambda E, i=i, h=half, b=bank, kb=kb: E.matmul(
                            b[:, :], kb[:, 128 * i:128 * (i + 1)],
                            BC4[:, 1024 * i + 512 * h:1024 * i + 512 * (h + 1)],
                            start=(i == 0), stop=(i == NT - 1)), waits=w)
                    m_mm = cTE.n
                    rsl = R[:, CH * o + 512 * half:CH * o + 512 * half + 512]
                    lastv = ve(lambda E, r=rsl, b=bank: E.tensor_tensor(
                        r, r, b[:, :], op=ALU.add),
                        waits=[(cTE, m_mm), (dSY, d_rload)])
            m_wo = lastv
            m_u4 = 0
            for half in range(2):
                m_u4 = ln_feat(lambda i, h=half: U23[:, CH * i + 512 * h:
                                                     CH * i + 512 * h + 512],
                               512 * half, 512,
                               wait_ve=m_wo, wait_x=[(cTE, cTE.n)])
            mx = ve(lambda E: E.tensor_tensor(F32A[:, 2048:3072], U23[:, 0:CH],
                                              U23[:, 0:CH], op=ALU.mult),
                    waits=[(cVE, m_u4)])
            for i in range(1, NT):
                ve(lambda E, i=i: E.tensor_tensor(
                    F32B[:, 1024:2048], U23[:, CH * i:CH * i + CH],
                    U23[:, CH * i:CH * i + CH], op=ALU.mult))
                mx = ve(lambda E: E.tensor_tensor(
                    F32A[:, 2048:3072], F32A[:, 2048:3072], F32B[:, 1024:2048],
                    op=ALU.max))
            width = 64
            while width >= 1:
                dsh = gdma(F32B[0:width, 0:1024], F32A[width:2 * width, 2048:3072],
                           waits=[(cVE, mx)])
                mx = ve(lambda E, w=width: E.tensor_tensor(
                    F32A[0:w, 2048:3072], F32A[0:w, 2048:3072],
                    F32B[0:w, 0:1024], op=ALU.max), waits=[(dGP, dsh)])
                width //= 2
            ms = sca(lambda E: E.activation(ROWS[0:1, 0:1024],
                                            F32A[0:1, 2048:3072], AF.Sqrt),
                     waits=[(cVE, mx), (cTE, cTE.n)])
            mv = ve(lambda E: E.tensor_scalar(ROWS[0:1, 1024:2048],
                                              ROWS[0:1, 0:1024], 1.0 / 127.0,
                                              1e-12, op0=ALU.mult, op1=ALU.max),
                    waits=[(cSC, ms)])
            mvr = ve(lambda E: E.reciprocal(ROWS[0:1, 2048:3072],
                                            ROWS[0:1, 1024:2048]))
            for j in range(2):
                te(lambda E, j=j: E.matmul(
                    pb[3 + j][:, :], ONER[0:1, :],
                    ROWS[0:1, 1024 + 512 * j:1024 + 512 * (j + 1)],
                    start=True, stop=True),
                   waits=[(cVE, mvr), (cSC, cSC.n)] if j == 0 else [])
                te(lambda E, j=j: E.matmul(
                    pb[5 + j][:, :], ONER[0:1, :],
                    ROWS[0:1, 2048 + 512 * j:2048 + 512 * (j + 1)],
                    start=True, stop=True))
            m_reps = cTE.n
            m_xsr = 0
            for j in range(2):
                ve(lambda E, j=j: E.tensor_copy(F32A[:, 512 * j:512 * (j + 1)],
                                                pb[3 + j][:, :]),
                   waits=[(cTE, m_reps)])
                m_xsr = ve(lambda E, j=j: E.tensor_copy(
                    F32A[:, 1024 + 512 * j:1024 + 512 * (j + 1)],
                    pb[5 + j][:, :]))
            m_xq = 0
            for i in range(NT):
                ve(lambda E, i=i: E.scalar_tensor_tensor(
                    F32B[:, 0:1024], U23[:, CH * i:CH * i + CH], 1.0,
                    F32A[:, 1024:2048], op0=ALU.mult, op1=ALU.mult),
                   waits=[(cVE, m_xsr)] if i == 0 else [])
                m_xq = ve(lambda E, i=i: E.tensor_scalar(
                    U23[:, CH * i:CH * i + CH], F32B[:, 0:1024], MAGIC, MAGIC,
                    op0=ALU.add, op1=ALU.subtract))
            tr_marks = {}
            for o in range(NT):
                kb = KB2[:, (o % 2) * 2048:(o % 2) * 2048 + 2048]
                dW = sdma(kb.rearrange("p (i q) -> p i q", q=128),
                          s_wq1[:, 128 * o:128 * (o + 1)]
                          .rearrange("(i p) q -> p i q", p=128),
                          waits=[(cTE, cTE.n)])
                for half in range(2):
                    bank = pb[half]
                    for i in range(NT):
                        w = []
                        if i == 0:
                            w = [(dSY, dW), (cVE, max(m_xq, cVE.n)), (cSC, cSC.n)]
                        te(lambda E, i=i, h=half, b=bank, kb=kb: E.matmul(
                            b[:, :], kb[:, 128 * i:128 * (i + 1)],
                            U23[:, CH * i + 512 * h:CH * i + 512 * (h + 1)],
                            start=(i == 0), stop=(i == NT - 1)), waits=w)
                    m_mm = cTE.n
                    ve(lambda E, o=o, b=bank, h=half: E.scalar_tensor_tensor(
                        F32B[:, 0:512], b[:, :],
                        PAR[:, P_ws1 + o:P_ws1 + o + 1],
                        F32A[:, 512 * h:512 * (h + 1)],
                        op0=ALU.mult, op1=ALU.mult), waits=[(cTE, m_mm)])
                    rsl = R[:, CH * o + 512 * half:CH * o + 512 * half + 512]
                    ve(lambda E, o=o, r=rsl, h=half: E.scalar_tensor_tensor(
                        OUTS[:, 512 * h:512 * (h + 1)], F32B[:, 0:512],
                        PAR[:, P_b1 + o:P_b1 + o + 1], r,
                        op0=ALU.add, op1=ALU.add))
                m_row = cVE.n
                for tt in range(8):
                    bank = pb[3 + tt % 2]
                    w = [(cVE, m_row)]
                    if (o, tt) != (0, 0):
                        pass
                    if tt >= 2 or o > 0:
                        prev = tr_marks.get((o, tt - 2)) or \
                            tr_marks.get((o - 1, tt + 6))
                        if prev:
                            w.append((cGP, prev))
                    mtr = te(lambda E, tt=tt, b=bank: E.transpose(
                        b[:, 0:128], OUTS[:, 128 * tt:128 * (tt + 1)],
                        IDEN[:, :]), waits=w)
                    msc = sca(lambda E, tt=tt, b=bank: E.copy(
                        F32A[:, 3072 + 128 * (tt % 2):3072 + 128 * (tt % 2) + 128],
                        b[:, 0:128]), waits=[(cTE, mtr)])
                    P.op("gp", lambda E, tt=tt, o=o, q=qb0: E.dma_start(
                        out=out[q + 128 * tt:q + 128 * (tt + 1),
                                128 * o:128 * (o + 1)],
                        in_=F32A[:, 3072 + 128 * (tt % 2):
                                 3072 + 128 * (tt % 2) + 128]),
                         waits=[(cSC, msc)], inc=cGP, k=16)
                    tr_marks[(o, tt)] = cGP.n
        _finish(P, cGP, dGP, dSY)
        return _close(nc, P)


def _cols16(v):
    return np.ascontiguousarray(np.asarray(v).reshape(16, 128).T)


def _host_inputs(inp):
    f32 = np.float32
    m = {}
    m["convw"] = np.ascontiguousarray(
        np.asarray(inp["L0_attn_conv_w"]).reshape(16, 128, 4)
        .transpose(1, 0, 2).reshape(128, 64)).astype(f32)
    m["convb"] = _cols16(inp["L0_attn_conv_b"]).astype(f32)
    m["xprojT"] = np.ascontiguousarray(np.asarray(inp["L0_attn_xproj_w"]).T).astype(f32)
    m["dtprojT"] = np.ascontiguousarray(np.asarray(inp["L0_attn_dtproj_w"]).T).astype(f32)
    m["dtprojb"] = _cols16(inp["L0_attn_dtproj_b"]).astype(f32)
    m["alog"] = np.ascontiguousarray(
        np.asarray(inp["L0_attn_A_log"]).reshape(16, 128, 16)
        .transpose(1, 0, 2).reshape(128, 256)).astype(f32)
    m["dpar"] = _cols16(inp["L0_attn_D_param"]).astype(f32)
    m["w0T"] = np.ascontiguousarray(np.asarray(inp["L0_ffn_weight"]).T).astype(f32)
    m["b0c"] = _cols16(inp["L0_ffn_bias"]).astype(f32)
    m["wqT"] = np.ascontiguousarray(np.asarray(inp["L1_attn_Wq"]).T).astype(f32)
    m["wkT"] = np.ascontiguousarray(np.asarray(inp["L1_attn_Wk"]).T).astype(f32)
    m["wvT"] = np.ascontiguousarray(np.asarray(inp["L1_attn_Wv"]).T).astype(f32)
    m["woT"] = np.ascontiguousarray(np.asarray(inp["L1_attn_Wo"]).T).astype(f32)
    m["gate"] = np.ascontiguousarray(np.asarray(inp["L1_attn_gate"]).reshape(1, 16)).astype(f32)
    m["w1T"] = np.ascontiguousarray(np.asarray(inp["L1_ffn_weight"]).T).astype(f32)
    m["b1c"] = _cols16(inp["L1_ffn_bias"]).astype(f32)
    t = np.arange(T, dtype=np.float64)
    invf = 1.0 / (10000.0 ** (np.arange(0, D, 2, dtype=np.float64) / D))
    ang = invf[:, None] * t[None, :]
    m["cosT"] = np.concatenate([np.cos(ang), np.cos(ang)], 0).astype(f32)
    m["sinT"] = np.concatenate([-np.sin(ang), np.sin(ang)], 0).astype(f32)
    m["ident"] = np.eye(128, dtype=f32)
    return m


def kernel(**inputs):
    shared = _host_inputs(inputs)
    x = np.asarray(inputs["L0_attn_x"])
    nc = build_program()
    in_maps = []
    for b in range(8):
        mm = dict(shared)
        mm["xT"] = np.ascontiguousarray(x[b].T).astype(np.float32)
        in_maps.append(mm)
    res = run_bass_kernel_spmd(nc, in_maps, core_ids=list(range(8)))
    return np.stack([np.asarray(r["y"], np.float32) for r in res.results], 0)


# version 13
# speedup vs baseline: 1.0632x; 1.0062x over previous
import numpy as np
import ml_dtypes
import concourse.bass as bass
import concourse.mybir as mybir
from concourse.bass_utils import run_bass_kernel_spmd
from contextlib import ExitStack

F32 = mybir.dt.float32
BF16 = mybir.dt.bfloat16
AF = mybir.ActivationFunctionType
ALU = mybir.AluOpType

T = 2048; D = 2048; NT = 16; NST = 16; NH = 16
CH = 1024; NCH = T // CH
SC = 512; NSC = T // SC
SCPC = CH // SC
MAGIC = 12582912.0
INVSQ = float(1.0 / np.sqrt(128.0))
SOFTBIAS = -20.0
UPAD = 8
UW = SC + UPAD


class Ctr:
    def __init__(self, name):
        self.name = name
        self.n = 0


class Prog:
    def __init__(self):
        self.items = {e: [] for e in ("gp", "sy", "te", "ve", "sc")}

    def op(self, eng, fn, waits=(), inc=None, k=1):
        wl = [(c.name, v) for (c, v) in waits if v and v > 0]
        tgt = None
        if inc is not None:
            inc.n += k
            tgt = (inc.name, k)

        def run(E, sems):
            for nm, v in wl:
                E.wait_ge(sems[nm], v)
            ins = fn(E)
            if tgt is not None:
                ins.then_inc(sems[tgt[0]], tgt[1])
        self.items[eng].append(run)


def _finish(P, cGP, dGP, dSY):
    P.op("gp", lambda E: None, waits=[(cGP, cGP.n), (dGP, dGP.n)])
    P.op("sy", lambda E: None, waits=[(cGP, cGP.n), (dSY, dSY.n)])


def _close(nc, P):
    with (nc.semaphore("sdGP") as s0, nc.semaphore("sdSY") as s1,
          nc.semaphore("scTE") as s2, nc.semaphore("scVE") as s3,
          nc.semaphore("scSC") as s4, nc.semaphore("scGP") as s5,
          nc.Block() as block):
        sems = {"dGP": s0, "dSY": s1, "cTE": s2, "cVE": s3,
                "cSC": s4, "cGP": s5}

        @block.gpsimd
        def _(E):
            for fn in P.items["gp"]:
                fn(E, sems)

        @block.sync
        def _(E):
            for fn in P.items["sy"]:
                fn(E, sems)

        @block.tensor
        def _(E):
            for fn in P.items["te"]:
                fn(E, sems)

        @block.vector
        def _(E):
            for fn in P.items["ve"]:
                fn(E, sems)

        @block.scalar
        def _(E):
            for fn in P.items["sc"]:
                fn(E, sems)

    return nc


def build_program(taps=(), stage=99):
    nc = bass.Bass("TRN2", target_bir_lowering=False,
                   detect_race_conditions=False)
    P = Prog()
    taps = set(taps)

    def din(name, shape, dt=F32):
        return nc.dram_tensor(name, shape, dt, kind="ExternalInput").ap()

    xT = din("xT", [D, T])
    convw = din("convw", [128, 4 * NT])
    convb = din("convb", [128, NT])
    xprojT_d = din("xprojT", [D, 160])
    dtprojT_d = din("dtprojT", [128, D])
    dtprojb = din("dtprojb", [128, NT])
    alog = din("alog", [128, NST * NT])
    dpar = din("dpar", [128, NT])
    w0T = din("w0T", [D, D])
    b0c = din("b0c", [128, NT])
    wqT = din("wqT", [D, D])
    wkT = din("wkT", [D, D])
    wvT = din("wvT", [D, D])
    woT = din("woT", [D, D])
    gate = din("gate", [1, NH])
    w1T = din("w1T", [D, D])
    b1c = din("b1c", [128, NT])
    cosTd = din("cosT", [D, T])
    sinTd = din("sinT", [D, T])
    identd = din("ident", [128, 128])
    out = nc.dram_tensor("y", [T, D], F32, kind="ExternalOutput").ap()

    def scr(name, shape, dt):
        return nc.dram_tensor(name, shape, dt, kind="Internal").ap()

    s_bc = scr("s_bc", [NSC, 32, SC], BF16)
    s_sign = scr("s_sign", [D, D], BF16)
    s_wq1 = scr("s_wq1", [D, D], BF16)
    s_woT = scr("s_woT", [D, D], BF16)
    s_qT = scr("s_qT", [D, T], BF16)
    s_kT = scr("s_kT", [D, T], BF16)
    s_v = scr("s_v", [T, D], BF16)
    s_row = scr("s_row", [8, 2048], F32)
    s_R = scr("s_R", [D, T], BF16)
    s_qTr = scr("s_qTr", [D, T], BF16)
    s_kTr = scr("s_kTr", [D, T], BF16)

    dGP = Ctr("dGP"); dSY = Ctr("dSY")
    cTE = Ctr("cTE"); cVE = Ctr("cVE"); cSC = Ctr("cSC"); cGP = Ctr("cGP")

    tap_info = {}

    with ExitStack() as ctx:
        def sbuf(name, shape, dt):
            return ctx.enter_context(nc.sbuf_tensor(name, shape, dt))

        R = sbuf("R", [128, NT * CH], BF16)
        U1 = sbuf("U1", [128, NT * UW], BF16)
        U23 = sbuf("U23", [128, NT * CH], BF16)
        BC4 = sbuf("BC4", [128, 16384], BF16)
        XC2 = sbuf("XC2", [128, NT * SC], BF16)
        KB2 = sbuf("KB2", [128, 8192], BF16)
        MSC = sbuf("MSC", [128, 4096], BF16)
        F32A = sbuf("F32A", [128, 3584], F32)
        F32B = sbuf("F32B", [128, 2048], F32)
        PAR = sbuf("PAR", [128, 768], F32)
        PARB = sbuf("PARB", [128, 4640], BF16)
        ROWS = sbuf("ROWS", [1, 4096], F32)
        OUTS = sbuf("OUTS", [128, 1024], F32)
        IDEN = sbuf("IDEN", [128, 128], F32)
        ONER = sbuf("ONER", [1, 128], F32)
        IDENB = sbuf("IDENB", [128, 128], BF16)
        GS = sbuf("GS", [1, 32], F32)
        CAPS = sbuf("CAPS", [128, 8], F32)

        PS3 = ctx.enter_context(nc.psum_tensor("PS3", [128, 1536], F32))
        pb = [PS3[:, 512 * i:512 * (i + 1)] for i in range(3)]
        pb += [ctx.enter_context(nc.psum_tensor(f"pb{i}", [128, 512], F32))
               for i in range(3, 7)]
        pbt = ctx.enter_context(nc.psum_tensor("pbt", [128, 1024], BF16))

        P_convw = 0; P_convb = 64; P_dtb = 80; P_dpar = 96
        P_b0 = 112; P_b1 = 128; P_alpha = 144; P_ws1 = 160
        P_aneg = 176
        P_hc = 432
        P_meanf = 688
        B_dtp = 0
        B_xp = 2048
        B_ones = 4608
        B_mean = 4610
        B_norm = 4612


        def gdma(dst, src, waits=(), slow=False):
            P.op("gp", lambda E, d=dst, s=src, sl=slow: E.dma_start(
                out=d, in_=s, allow_slow_non_contiguous=sl),
                 waits=waits, inc=dGP, k=16)
            return dGP.n

        def sdma(dst, src, waits=(), transpose=False):
            P.op("sy", lambda E, d=dst, s=src, t=transpose:
                 E.dma_start(out=d, in_=s, transpose=t),
                 waits=waits, inc=dSY, k=16)
            return dSY.n

        def te(fn, waits=()):
            P.op("te", fn, waits=waits, inc=cTE)
            return cTE.n

        def ve(fn, waits=()):
            P.op("ve", fn, waits=waits, inc=cVE)
            return cVE.n

        def sca(fn, waits=()):
            P.op("sc", fn, waits=waits, inc=cSC)
            return cSC.n

        def tap(name, ap, shape, dt=F32):
            if name not in taps:
                return
            t = nc.dram_tensor("tap_" + name, shape, dt, kind="ExternalOutput").ap()
            tap_info[name] = True
            w = [(cTE, cTE.n), (cVE, cVE.n), (cSC, cSC.n), (dSY, dSY.n)]
            gdma(t, ap, waits=w)

        gdma(PARB[:, B_dtp:B_dtp + 2048], dtprojT_d[:, :])
        gdma(PARB[:, B_xp:B_xp + 2560].rearrange("p (i q) -> p i q", q=160),
             xprojT_d.rearrange("(i p) q -> p i q", p=128))
        gdma(PAR[:, P_convw:P_convw + 64], convw[:, :])
        gdma(PAR[:, P_convb:P_convb + 16], convb[:, :])
        gdma(PAR[:, P_dtb:P_dtb + 16], dtprojb[:, :])
        gdma(PAR[:, P_dpar:P_dpar + 16], dpar[:, :])
        gdma(PAR[:, P_b0:P_b0 + 16], b0c[:, :])
        gdma(PAR[:, P_b1:P_b1 + 16], b1c[:, :])
        gdma(PAR[:, P_aneg:P_aneg + 256], alog[:, :])
        gdma(IDEN[:, :], identd[:, :])
        gdma(IDENB[:, :], identd[:, :])
        gdma(ROWS[0:1, 0:16], gate[:, :])
        d_par = dGP.n
        sca(lambda E: E.activation(PAR[:, P_aneg:P_aneg + 256],
                                   PAR[:, P_aneg:P_aneg + 256], AF.Exp),
            waits=[(dGP, d_par)])
        ve(lambda E: E.tensor_scalar_mul(PAR[:, P_aneg:P_aneg + 256],
                                         PAR[:, P_aneg:P_aneg + 256], -1.0),
           waits=[(cSC, cSC.n)])
        ve(lambda E: E.memset(PARB[:, B_ones:B_ones + 1], 1.0))
        ve(lambda E: E.memset(ONER[0:1, :], 1.0))
        ve(lambda E: E.memset(PAR[:, P_meanf:P_meanf + 1], 1.0 / 2048.0))
        ve(lambda E: E.memset(CAPS[:, 0:1], 1e-5))
        ve(lambda E: E.memset(CAPS[:, 1:2], -0.5))
        ve(lambda E: E.memset(CAPS[:, 2:3], -1.0))
        ve(lambda E: E.memset(CAPS[:, 3:4], 1.0))
        ve(lambda E: E.memset(CAPS[:, 4:5], SOFTBIAS))
        ve(lambda E: E.memset(CAPS[:, 5:6], INVSQ))
        m_ones = ve(lambda E: E.memset(PARB[:, B_mean:B_mean + 1], 1.0 / 2048.0))
        ve(lambda E: E.memset(PAR[:, P_hc:P_hc + 256], 0.0))
        m_ut0 = ve(lambda E: E.memset(U1[:, :], 0.0))
        m_gs = sca(lambda E: E.activation(GS[0:1, 0:16], ROWS[0:1, 0:16],
                                          AF.Sigmoid), waits=[(dGP, d_par)])
        m_g1m = ve(lambda E: E.tensor_scalar(GS[0:1, 16:32], GS[0:1, 0:16],
                                             -1.0, 1.0, op0=ALU.mult, op1=ALU.add),
                   waits=[(cSC, m_gs)])

        for i in range(NT):
            kb = KB2[:, (i % 2) * 2048:(i % 2) * 2048 + 2048]
            dW = gdma(kb, w0T[128 * i:128 * (i + 1), :],
                      waits=[(cVE, cVE.n), (cSC, cSC.n)])
            m1 = ve(lambda E, b=kb: E.tensor_scalar(
                MSC[:, 0:2048], b, 0.0, None, op0=ALU.is_ge),
                waits=[(dGP, dW), (dSY, dSY.n)])
            m2 = ve(lambda E: E.tensor_scalar(
                MSC[:, 0:2048], MSC[:, 0:2048], 2.0, -1.0,
                op0=ALU.mult, op1=ALU.add))
            sdma(s_sign[128 * i:128 * (i + 1), :], MSC[:, 0:2048],
                 waits=[(cVE, m2)])
            ma = sca(lambda E, b=kb: E.activation(MSC[:, 2048:4096], b, AF.Abs),
                     waits=[(dGP, dW), (cTE, cTE.n)])
            for j in range(4):
                te(lambda E, j=j, i=i: E.matmul(
                    pb[3 + j][0:1, :], PARB[:, B_mean:B_mean + 1],
                    MSC[:, 2048 + 512 * j:2048 + 512 * (j + 1)],
                    start=(i == 0), stop=(i == NT - 1)),
                   waits=[(cSC, ma), (cVE, m_ones)])
        m_amm = cTE.n
        m_acp = 0
        for j in range(4):
            m_acp = sca(lambda E, j=j: E.copy(ROWS[0:1, 512 * j:512 * (j + 1)],
                                              pb[3 + j][0:1, :]),
                        waits=[(cTE, m_amm)])
        d_aw = gdma(s_row[1:2, 0:2048], ROWS[0:1, 0:2048], waits=[(cSC, m_acp)])
        gdma(PAR[:, P_alpha:P_alpha + 16],
             s_row[1, 0:2048].rearrange("(i p) -> p i", p=128),
             waits=[(dGP, d_aw)], slow=True)

        def w1_prep():
            mxm = 0
            for i in range(NT):
                kb = KB2[:, (i % 2) * 2048:(i % 2) * 2048 + 2048]
                dW = gdma(kb, w1T[128 * i:128 * (i + 1), :],
                          waits=[(cVE, cVE.n), (cSC, cSC.n)])
                ma = sca(lambda E, b=kb: E.activation(MSC[:, 0:2048], b, AF.Abs),
                         waits=[(dGP, dW), (cVE, mxm)])
                if i == 0:
                    mxm = ve(lambda E: E.tensor_copy(F32A[:, 0:2048], MSC[:, 0:2048]),
                             waits=[(cSC, ma)])
                else:
                    mxm = ve(lambda E: E.tensor_tensor(
                        F32A[:, 0:2048], F32A[:, 0:2048], MSC[:, 0:2048], op=ALU.max),
                        waits=[(cSC, ma)])
            width = 64
            while width >= 1:
                dsh = gdma(F32B[0:width, 0:2048], F32A[width:2 * width, 0:2048],
                           waits=[(cVE, mxm)])
                mxm = ve(lambda E, w=width: E.tensor_tensor(
                    F32A[0:w, 0:2048], F32A[0:w, 0:2048], F32B[0:w, 0:2048],
                    op=ALU.max), waits=[(dGP, dsh)])
                width //= 2
            m_ws = ve(lambda E: E.tensor_scalar(ROWS[0:1, 0:2048],
                                                F32A[0:1, 0:2048], 1e-8, 1.0 / 127.0,
                                                op0=ALU.max, op1=ALU.mult),
                      waits=[(dGP, dGP.n)])
            m_wsr = ve(lambda E: E.reciprocal(ROWS[0:1, 2048:4096],
                                              ROWS[0:1, 0:2048]),
                       waits=[(cVE, m_ws)])
            d_ww = gdma(s_row[2:3, 0:2048], ROWS[0:1, 0:2048], waits=[(cVE, m_ws)])
            gdma(PAR[:, P_ws1:P_ws1 + 16],
                 s_row[2, 0:2048].rearrange("(i p) -> p i", p=128),
                 waits=[(dGP, d_ww)], slow=True)
            for j in range(4):
                te(lambda E, j=j: E.matmul(
                    pb[j][:, :], ONER[0:1, :],
                    ROWS[0:1, 2048 + 512 * j:2048 + 512 * (j + 1)],
                    start=True, stop=True),
                   waits=[(cVE, m_wsr), (cVE, m_ones), (cSC, cSC.n)])
            m_wmm = cTE.n
            m_wcp = 0
            for j in range(4):
                m_wcp = ve(lambda E, j=j: E.tensor_copy(
                    F32B[:, 512 * j:512 * (j + 1)], pb[j][:, :]),
                    waits=[(cTE, m_wmm)])
            for i in range(NT):
                kb = KB2[:, (i % 2) * 2048:(i % 2) * 2048 + 2048]
                dW = gdma(kb, w1T[128 * i:128 * (i + 1), :],
                          waits=[(cVE, cVE.n), (dSY, dSY.n)])
                mq1 = ve(lambda E, b=kb: E.tensor_tensor(
                    F32A[:, 0:2048], b, F32B[:, 0:2048], op=ALU.mult),
                    waits=[(dGP, dW), (cVE, m_wcp)])
                mq2 = ve(lambda E: E.tensor_scalar(
                    MSC[:, 0:2048], F32A[:, 0:2048], MAGIC, MAGIC,
                    op0=ALU.add, op1=ALU.subtract))
                sdma(s_wq1[128 * i:128 * (i + 1), :], MSC[:, 0:2048],
                     waits=[(cVE, mq2)])

        for i in range(NT):
            kb = KB2[:, (i % 2) * 2048:(i % 2) * 2048 + 2048]
            dW = gdma(kb, woT[128 * i:128 * (i + 1), :], waits=[(dSY, dSY.n)])
            sdma(s_woT[128 * i:128 * (i + 1), :], kb, waits=[(dGP, dW)])
        m_prep_ve = cVE.n
        m_prep_sc = cSC.n
        m_prep_sy = dSY.n

        sq_marks = []

        def ln_feat(dst_cols, src_col, width, wait_ve=0, wait_x=()):
            sqs = [0, 512]
            mm_sq = {}
            for i in range(NT):
                rsl = R[:, CH * i + src_col:CH * i + src_col + width]
                w_sq = []
                if i >= 2:
                    w_sq = [(cTE, mm_sq[i - 2])]
                msq = ve(lambda E, r=rsl, s=sqs[i % 2]: E.tensor_tensor(
                    F32B[:, s:s + width], r, r, op=ALU.mult),
                    waits=w_sq + (list(wait_x) if i == 0 else []))
                w_mm = []
                if i == 0:
                    w_mm = [(cVE, max(wait_ve, cVE.n)), (cSC, cSC.n)] + list(wait_x)
                te(lambda E, r=rsl, i=i: E.matmul(
                    pb[3][0:1, 0:width], PARB[:, B_mean:B_mean + 1], r,
                    start=(i == 0), stop=(i == NT - 1)), waits=w_mm)
                mm_sq[i] = te(lambda E, s=sqs[i % 2], i=i: E.matmul(
                    pb[4][0:1, 0:width], PAR[:, P_meanf:P_meanf + 1],
                    F32B[:, s:s + width],
                    start=(i == 0), stop=(i == NT - 1)), waits=[(cVE, msq)])
            m_stat = cTE.n
            mvc = ve(lambda E: E.tensor_copy(ROWS[0:1, 1024:1024 + width],
                                             pb[3][0:1, 0:width]),
                     waits=[(cTE, cTE.n)])
            mv = ve(lambda E: E.tensor_tensor(
                ROWS[0:1, 0:width], ROWS[0:1, 1024:1024 + width],
                ROWS[0:1, 1024:1024 + width], op=ALU.mult))
            mv = ve(lambda E: E.tensor_tensor(
                ROWS[0:1, 0:width], pb[4][0:1, 0:width], ROWS[0:1, 0:width],
                op=ALU.subtract))
            ms = sca(lambda E: E.activation(ROWS[0:1, 512:512 + width],
                                            ROWS[0:1, 0:width],
                                            AF.Ln, bias=CAPS[0:1, 0:1]),
                     waits=[(cVE, mv), (cTE, cTE.n)])
            ms = sca(lambda E: E.activation(ROWS[0:1, 512:512 + width],
                                            ROWS[0:1, 512:512 + width],
                                            AF.Exp, scale=CAPS[0:1, 1:2]))
            mn = ve(lambda E: E.tensor_scalar(ROWS[0:1, 1536:1536 + width],
                                              ROWS[0:1, 0:width], 1e-5, None,
                                              op0=ALU.add), waits=[(cSC, ms)])
            mn = ve(lambda E: E.tensor_tensor(
                ROWS[0:1, 1536 + width:1536 + 2 * width] if width <= 256 else
                ROWS[0:1, 0:width],
                ROWS[0:1, 512:512 + width], ROWS[0:1, 512:512 + width],
                op=ALU.mult))
            mn = ve(lambda E: E.tensor_tensor(
                ROWS[0:1, 0:width], ROWS[0:1, 0:width],
                ROWS[0:1, 1536:1536 + width], op=ALU.mult))
            mn = ve(lambda E: E.tensor_scalar(
                ROWS[0:1, 0:width], ROWS[0:1, 0:width], -0.5, 1.5,
                op0=ALU.mult, op1=ALU.add))
            mn = ve(lambda E: E.tensor_tensor(
                ROWS[0:1, 512:512 + width], ROWS[0:1, 512:512 + width],
                ROWS[0:1, 0:width], op=ALU.mult))
            te(lambda E: E.matmul(pb[5][:, 0:width], ONER[0:1, :],
                                  ROWS[0:1, 1024:1024 + width],
                                  start=True, stop=True), waits=[(cVE, mvc)])
            m_bc = te(lambda E: E.matmul(
                pb[6][:, 0:width], ONER[0:1, :],
                ROWS[0:1, 512:512 + width], start=True, stop=True),
                waits=[(cVE, mn)])
            last = 0
            for i in range(NT):
                rsl = R[:, CH * i + src_col:CH * i + src_col + width]
                dsl = dst_cols(i)
                ve(lambda E, r=rsl, d=dsl: E.tensor_tensor(
                    d, r, pb[5][:, 0:width], op=ALU.subtract),
                   waits=[(cTE, m_bc)] if i == 0 else [])
                last = ve(lambda E, d=dsl: E.tensor_tensor(
                    d, d, pb[6][:, 0:width], op=ALU.mult))
            return last

        def mamba_sc(s, first_wait=(), bc_free=0, feeder=None):
            base = SC * s
            rbase = SC * (s % SCPC)
            if s > 0:
                ve(lambda E: E.tensor_copy(
                    U1.rearrange("p (i c) -> p i c", c=UW)[:, :, 5:8],
                    U1.rearrange("p (i c) -> p i c", c=UW)
                    [:, :, UPAD + SC - 3:UPAD + SC]), waits=list(first_wait))
                first_wait = ()
            wx = list(first_wait)
            if s == 0:
                wx += [(cVE, m_ut0)]
            m_u = ln_feat(lambda i: U1[:, i * UW + UPAD:i * UW + UPAD + SC],
                          rbase, SC, wait_x=wx + [(cTE, cTE.n)])
            if stage == 15:
                tap("rows", ROWS[0:1, :], [1, 4096], F32)
                return m_u
            silu_marks = {}
            d_bcw = dGP.n
            for i in range(NT):
                ub = U1[:, i * UW:(i + 1) * UW]
                ac = 2560 + 512 * (i % 2)
                w0_ = []
                if i < 2:
                    w0_ = [(cSC, cSC.n)]
                else:
                    w0_ = [(cSC, silu_marks[i - 2])]
                ve(lambda E, u=ub, a=ac, i=i: E.tensor_scalar(
                    F32A[:, a:a + SC], u[:, 5:5 + SC],
                    PAR[:, P_convw + 4 * i:P_convw + 4 * i + 1], None,
                    op0=ALU.mult), waits=w0_)
                for j in range(1, 4):
                    ve(lambda E, u=ub, a=ac, i=i, j=j: E.scalar_tensor_tensor(
                        F32A[:, a:a + SC], u[:, 5 + j:5 + j + SC],
                        PAR[:, P_convw + 4 * i + j:P_convw + 4 * i + j + 1],
                        F32A[:, a:a + SC], op0=ALU.mult, op1=ALU.add))
                m_cv = cVE.n
                silu_marks[i] = sca(lambda E, i=i, a=ac: E.activation(
                    XC2[:, i * SC:(i + 1) * SC], F32A[:, a:a + SC], AF.Silu,
                    bias=PAR[:, P_convb + i:P_convb + i + 1]),
                    waits=[(cVE, m_cv)])
            m_xc = cSC.n
            if stage == 17:
                return m_xc
            for i in range(NT):
                w = [(cSC, m_xc if i == 0 else 0), (cVE, cVE.n if i == 0 else 0)]
                w = [x for x in w if x[1]]
                te(lambda E, i=i: E.matmul(
                    pb[3][:, :], PARB[:, B_xp + 160 * i:B_xp + 160 * i + 128],
                    XC2[:, i * SC:(i + 1) * SC],
                    start=(i == 0), stop=(i == NT - 1)), waits=w)
                te(lambda E, i=i: E.matmul(
                    pb[4][0:32, :],
                    PARB[:, B_xp + 160 * i + 128:B_xp + 160 * i + 160],
                    XC2[:, i * SC:(i + 1) * SC],
                    start=(i == 0), stop=(i == NT - 1)))
            m_proj = cTE.n
            m_dtin = sca(lambda E: E.copy(MSC[:, 2048:2560], pb[3][:, :]),
                         waits=[(cTE, m_proj), (cVE, cVE.n)])
            m_bcv = sca(lambda E: E.copy(F32A[0:32, 2560:2560 + SC],
                                         pb[4][0:32, :]),
                        waits=[(cTE, m_proj), (cVE, cVE.n)])
            d_bcw = gdma(s_bc[s, :, :], F32A[0:32, 2560:2560 + SC],
                         waits=[(cSC, m_bcv)])
            d_ball = sdma(BC4[:, 0:8192].rearrange("p (n c) -> p n c", c=SC),
                          s_bc[s:s + 1, 0:16, :].broadcast_to([128, 16, SC]),
                          waits=[(dGP, d_bcw), (cVE, bc_free)])
            d_call = sdma(BC4[:, 8192:16384].rearrange("p (n c) -> p n c", c=SC),
                          s_bc[s:s + 1, 16:32, :].broadcast_to([128, 16, SC]),
                          waits=[])
            if stage == 18:
                return cVE.n
            prev_sp = (0, 0)
            m_last = cVE.n
            for i in range(NT):
                dcol = 512 * (i % 2)
                wmt = [(cSC, m_dtin)]
                if i == 0:
                    wmt.append((cVE, m_u))
                if prev_sp[0]:
                    wmt += [(cSC, prev_sp[0]), (cVE, prev_sp[1])]
                mt = te(lambda E, i=i: E.matmul(
                    pb[5][:, :], PARB[:, B_dtp + 128 * i:B_dtp + 128 * (i + 1)],
                    MSC[:, 2048:2560], start=True, stop=True), waits=wmt)
                if stage == 1801:
                    return mt
                ms1 = sca(lambda E, i=i, d=dcol: E.activation(
                    F32A[:, d:d + SC], pb[5][:, :], AF.Abs,
                    bias=PAR[:, P_dtb + i:P_dtb + i + 1]), waits=[(cTE, mt)])
                if stage == 1802:
                    return cVE.n
                sca(lambda E, d=dcol: E.activation(
                    F32A[:, d:d + SC], F32A[:, d:d + SC], AF.Exp, scale=CAPS[:, 2:3]))
                if stage == 1803:
                    return cVE.n
                ms3 = sca(lambda E, d=dcol: E.activation(
                    F32A[:, d:d + SC], F32A[:, d:d + SC], AF.Ln, bias=CAPS[:, 3:4]))
                if stage == 1804:
                    return cVE.n
                mr = ve(lambda E, i=i: E.tensor_scalar(
                    F32B[:, 0:SC], pb[5][:, :],
                    PAR[:, P_dtb + i:P_dtb + i + 1], 0.0,
                    op0=ALU.add, op1=ALU.max), waits=[(cTE, mt), (cSC, ms1)])
                prev_sp = (ms1, mr)
                if stage == 1805:
                    return cVE.n
                mdelta = ve(lambda E, d=dcol: E.tensor_tensor(
                    F32A[:, d:d + SC], F32A[:, d:d + SC], F32B[:, 0:SC],
                    op=ALU.add), waits=[(cSC, ms3)])
                if stage == 181:
                    return mdelta
                mw_w = [(dSY, dSY.n)] if (s == 0 and i == 0) else []
                m_w = ve(lambda E, i=i, d=dcol: E.tensor_tensor(
                    MSC[:, 0:512], F32A[:, d:d + SC],
                    XC2[:, i * SC:(i + 1) * SC], op=ALU.mult), waits=mw_w)
                ve(lambda E, i=i: E.tensor_scalar(
                    MSC[:, 2560:3072],
                    U1[:, i * UW + UPAD:i * UW + UPAD + SC],
                    PAR[:, P_dpar + i:P_dpar + i + 1], None, op0=ALU.mult))
                if stage == 182:
                    return cVE.n
                h_marks = {}
                for n in range(NST):
                    acol = 1024 + 512 * (n % 2)
                    wsa = [(cVE, mdelta)]
                    if n >= 2:
                        wsa.append((cVE, h_marks[n - 2]))
                    sa_ = sca(lambda E, i=i, n=n, a=acol, d=dcol: E.activation(
                        F32A[:, a:a + SC], F32A[:, d:d + SC], AF.Exp,
                        scale=PAR[:, P_aneg + 16 * i + n:P_aneg + 16 * i + n + 1]),
                        waits=wsa)
                    wb = [(dSY, d_ball)] if n == 0 else []
                    ve(lambda E, n=n: E.tensor_tensor(
                        MSC[:, 512:1024], MSC[:, 0:512],
                        BC4[:, 512 * n:512 * (n + 1)], op=ALU.mult), waits=wb)
                    m_h = ve(lambda E, i=i, n=n, a=acol: E.tensor_tensor_scan(
                        MSC[:, 1024:1536], F32A[:, a:a + SC], MSC[:, 512:1024],
                        PAR[:, P_hc + 16 * i + n:P_hc + 16 * i + n + 1],
                        op0=ALU.mult, op1=ALU.add), waits=[(cSC, sa_)])
                    h_marks[n] = m_h
                    if stage == 183:
                        return m_h
                    wc = [(dSY, d_call)] if n == 0 else []
                    ve(lambda E, n=n: E.tensor_tensor(
                        MSC[:, 1536:2048], MSC[:, 1024:1536],
                        BC4[:, 8192 + 512 * n:8192 + 512 * (n + 1)],
                        op=ALU.mult), waits=wc)
                    m_ya = ve(lambda E: E.tensor_tensor(
                        MSC[:, 2560:3072], MSC[:, 2560:3072],
                        MSC[:, 1536:2048], op=ALU.add))
                    ve(lambda E, i=i, n=n: E.tensor_copy(
                        PAR[:, P_hc + 16 * i + n:P_hc + 16 * i + n + 1],
                        MSC[:, 1535:1536]))
                    if stage == 184:
                        return m_ya
                m_last = ve(lambda E, i=i, b=rbase: E.tensor_tensor(
                    R[:, CH * i + b:CH * i + b + SC],
                    R[:, CH * i + b:CH * i + b + SC],
                    MSC[:, 2560:3072], op=ALU.add))
                if stage == 19:
                    return m_last
                if feeder:
                    feeder.pop(0)()
                    if feeder:
                        feeder.pop(0)()
            return m_last

        def ffn0(c, m_u2):
            lastv = 0
            for o in range(NT):
                kb = KB2[:, (o % 2) * 2048:(o % 2) * 2048 + 2048]
                wkb = [(cTE, cTE.n)]
                if c == 0 and o < 2:
                    wkb += [(cVE, m_prep_ve), (cSC, m_prep_sc)]
                dW = sdma(kb.rearrange("p (i q) -> p i q", q=128),
                          s_sign[:, 128 * o:128 * (o + 1)]
                          .rearrange("(i p) q -> p i q", p=128), waits=wkb)
                for half in range(2):
                    bank = pb[half]
                    for i in range(NT):
                        w = []
                        if i == 0:
                            w = [(dSY, dW), (cVE, max(m_u2, lastv)),
                                 (cSC, cSC.n)]
                        te(lambda E, i=i, h=half, b=bank, kb=kb: E.matmul(
                            b[:, :], kb[:, 128 * i:128 * (i + 1)],
                            U23[:, CH * i + 512 * h:CH * i + 512 * (h + 1)],
                            start=(i == 0), stop=(i == NT - 1)), waits=w)
                    m_mm = cTE.n
                    ve(lambda E, o=o, b=bank: E.tensor_scalar(
                        F32B[:, 1024:1536], b[:, :],
                        PAR[:, P_alpha + o:P_alpha + o + 1],
                        PAR[:, P_b0 + o:P_b0 + o + 1],
                        op0=ALU.mult, op1=ALU.add), waits=[(cTE, m_mm)])
                    rsl = R[:, CH * o + 512 * half:CH * o + 512 * half + 512]
                    lastv = ve(lambda E, r=rsl: E.tensor_tensor(
                        r, r, F32B[:, 1024:1536], op=ALU.add))
            return lastv

        def qkv_feeder(c, m_u3):
            base = CH * c
            st = {"ev": {}, "sp": {}, "pend": None,
                  "mmdone": {0: cTE.n, 1: cTE.n}}
            items = []

            def qk_load(wT, o, slot):
                def emit():
                    kb = KB2[:, (o % 2) * 2048:(o % 2) * 2048 + 2048]
                    slot[0] = gdma(kb.rearrange("p (i q) -> p i q", q=128),
                                   wT[:, 128 * o:128 * (o + 1)]
                                   .rearrange("(i p) q -> p i q", p=128),
                                   waits=[(cTE, st["mmdone"].get(o % 2, 0))])
                return emit

            def qk_mm(wT, o, slot):
                def emit():
                    kb = KB2[:, (o % 2) * 2048:(o % 2) * 2048 + 2048]
                    dW = slot[0]
                    mms = []
                    for half in range(2):
                        bk = (2 * o + half) % 3 if True else 0
                        bank = pb[bk]
                        for i in range(NT):
                            w = []
                            if i == 0:
                                w = [(dGP, dW), (cVE, m_u3),
                                     (cSC, st["ev"].get(bk, 0))]
                            te(lambda E, i=i, b=bank, kb=kb, h=half: E.matmul(
                                b[:, :], kb[:, 128 * i:128 * (i + 1)],
                                U23[:, CH * i + 512 * h:CH * i + 512 * (h + 1)],
                                start=(i == 0), stop=(i == NT - 1)), waits=w)
                        mms.append((bk, cTE.n))
                    st["mmdone"][o % 2] = cTE.n
                    return mms

                return emit

            def qk_ev(sdst, o, mms):
                def emit():
                    marks = []
                    for half, (bk, m_mm) in enumerate(mms()):
                        m_ev = sca(lambda E, b=pb[mms()[half][0]], h=half: E.copy(
                            KB2[:, 4096 + 512 * h:4096 + 512 * (h + 1)], b[:, :]),
                            waits=[(cTE, m_mm), (dSY, st["sp"].get("qk", 0))])
                        st["ev"][bk] = m_ev
                        marks.append(m_ev)
                    st["sp"]["qk"] = sdma(
                        sdst[128 * o:128 * (o + 1), base:base + CH],
                        KB2[:, 4096:5120], waits=[(cSC, marks[-1])])
                return emit

            def v_mm(ob, tt, dwv):
                def emit():
                    if tt == 0:
                        dwv[0] = gdma(KB2.rearrange("p (i q) -> p i q", q=512),
                                      wvT[:, 512 * ob:512 * (ob + 1)]
                                      .rearrange("(i p) q -> p i q", p=128),
                                      waits=[(cTE, cTE.n), (dSY, dSY.n),
                                             (cSC, st["ev"].get(tt % 2, 0))])
                    bank = pb[tt % 2]
                    for i in range(NT):
                        w = []
                        if i == 0:
                            w = [(dGP, dwv[0]), (cVE, m_u3),
                                 (cSC, st["ev"].get(tt % 2, 0))]
                        te(lambda E, i=i, tt=tt, b=bank: E.matmul(
                            b[:, :],
                            U23[:, CH * i + 128 * tt:CH * i + 128 * (tt + 1)],
                            KB2[:, 512 * i:512 * (i + 1)],
                            start=(i == 0), stop=(i == NT - 1)), waits=w)
                    return [(tt % 2, cTE.n)]
                return emit

            def v_ev(ob, tt, mms):
                def emit():
                    bk, m_mm = mms()[0]
                    m_ev = sca(lambda E, b=pb[bk]: E.copy(F32B[:, 1536:2048],
                                                          b[:, :]),
                               waits=[(cTE, m_mm), (dGP, st["sp"].get("v", 0))])
                    st["ev"][bk] = m_ev
                    st["sp"]["v"] = gdma(
                        s_v[base + 128 * tt:base + 128 * (tt + 1),
                            512 * ob:512 * (ob + 1)], F32B[:, 1536:2048],
                        waits=[(cSC, m_ev)])
                return emit

            units = []
            for o in range(NT):
                units.append(("qk", wqT, s_qT, o))
            for o in range(NT):
                units.append(("qk", wkT, s_kT, o))
            for ob in range(4):
                dwv = [0]
                for tt in range(8):
                    units.append(("v", ob, tt, dwv))
            pend = [None]
            pre = []
            for u in units:
                if u[0] == "qk":
                    _, wT, sdst, o = u
                    res = {}
                    slot = [0]
                    ld = qk_load(wT, o, slot)
                    mm = qk_mm(wT, o, slot)
                    def item(mm=mm, res=res):
                        res["mms"] = mm()
                    ev = qk_ev(sdst, o, lambda res=res: res["mms"])
                else:
                    _, ob, tt, dwv = u
                    res = {}
                    ld = None
                    mm = v_mm(ob, tt, dwv)
                    def item(mm=mm, res=res):
                        res["mms"] = mm()
                    ev = v_ev(ob, tt, lambda res=res: res["mms"])
                pre.append(ld)
                units[len(pre) - 1] = (item, ev)
            for k, (item, ev) in enumerate(units):
                def combined(item=item, prev_ev=pend[0],
                             nld=(pre[k + 1] if k + 1 < len(pre) else None),
                             first=(pre[k] if k == 0 else None)):
                    if first is not None:
                        first()
                    if nld is not None:
                        nld()
                    if prev_ev is not None:
                        prev_ev()
                    item()
                items.append(combined)
                pend[0] = ev
            def final(prev_ev=pend[0]):
                if prev_ev is not None:
                    prev_ev()
            items.append(final)
            return items

        m_u3_prev = 0
        if stage < 2:
            _finish(P, cGP, dGP, dSY)
            return _close(nc, P)
        m_mamba = 0
        d_rspill = 0
        feeder = []
        for c in range(NCH):
            wR = []
            if c > 0:
                wR = [(dSY, d_rspill), (cVE, m_u3_prev)]
            d_xload = gdma(R.rearrange("p (i q) -> p i q", q=CH),
                           xT[:, CH * c:CH * (c + 1)]
                           .rearrange("(i p) q -> p i q", p=128), waits=wR)
            for sci in range(SCPC):
                s = SCPC * c + sci
                fw = []
                if sci == 0:
                    fw = [(dGP, d_xload)]
                m_mamba = mamba_sc(s, first_wait=fw, bc_free=m_mamba,
                                   feeder=feeder)
                if stage == 22 and s == 0:
                    tap("hc", PAR[:, :], [128, 768], F32)
                    _finish(P, cGP, dGP, dSY)
                    return _close(nc, P)
                if stage == 23 and s == 1:
                    tap("R_c0", R[:, :], [128, NT * CH], BF16)
                    _finish(P, cGP, dGP, dSY)
                    return _close(nc, P)
                if stage in (15, 17, 18, 1801, 1802, 1803, 1804, 1805, 181, 182, 183, 184, 19, 2):
                    tap("R_sc0", R[:, :], [128, NT * CH], BF16)
                    tap("u_sc0", U1[:, :], [128, NT * UW], BF16)
                    tap("xc_sc0", XC2[:, :], [128, NT * SC], BF16)
                    _finish(P, cGP, dGP, dSY)
                    return _close(nc, P)
            for fn in feeder:
                fn()
            feeder = []
            m_u2 = 0
            for half in range(2):
                m_u2 = ln_feat(lambda i, h=half: U23[:, CH * i + 512 * h:
                                                     CH * i + 512 * h + 512],
                               512 * half, 512,
                               wait_ve=m_mamba, wait_x=[(cTE, cTE.n)])
            m_ffn = ffn0(c, m_u2)
            if stage == 25:
                tap("R_ffn0", R[:, :], [128, NT * CH], BF16)
                _finish(P, cGP, dGP, dSY)
                return _close(nc, P)
            if stage == 24:
                tap("u2", U23[:, :], [128, NT * CH], BF16)
                _finish(P, cGP, dGP, dSY)
                return _close(nc, P)
            m_u3 = 0
            for half in range(2):
                m_u3 = ln_feat(lambda i, h=half: U23[:, CH * i + 512 * h:
                                                     CH * i + 512 * h + 512],
                               512 * half, 512,
                               wait_ve=m_ffn, wait_x=[(cTE, cTE.n)])
            d_rspill = sdma(s_R[:, CH * c:CH * (c + 1)]
                            .rearrange("(i p) q -> p i q", p=128),
                            R.rearrange("p (i q) -> p i q", q=CH),
                            waits=[(cTE, cTE.n), (cVE, m_u3)])
            feeder = qkv_feeder(c, m_u3)
            m_u3_prev = m_u3
        for fn in feeder:
            fn()
        feeder = []
        w1_prep()
        tap("R_A", R[:, :], [128, NT * CH], BF16)
        if stage < 4:
            _finish(P, cGP, dGP, dSY)
            return _close(nc, P)
        tap("qT", None, None) if False else None

        b0pre = {}
        b0q2 = {}
        b0mem = {}
        for h in range(NH):
            cb = (h % 2) * 2048
            hp = (h + 8) % 16
            if h == 0:
                d_kt = sdma(KB2[:, cb:cb + 2048],
                            s_kT[128 * h:128 * (h + 1), :],
                            waits=[(cTE, cTE.n), (cVE, cVE.n)])
                d_kp = sdma(KB2[:, 4096 + cb:4096 + cb + 2048],
                            s_kT[128 * hp:128 * (hp + 1), :], waits=[])
                d_cs = gdma(U23[:, 0:2048], cosTd[128 * h:128 * (h + 1), :],
                            waits=[(cVE, cVE.n), (cTE, cTE.n)])
                d_sn = gdma(U23[:, 2048:4096], sinTd[128 * h:128 * (h + 1), :],
                            waits=[])
                d_vh = sdma(XC2[:, 4096 + cb:4096 + cb + 2048]
                            .rearrange("p (j q) -> p j q", q=128),
                            s_v[:, 128 * h:128 * (h + 1)]
                            .rearrange("(j p) q -> p j q", p=128),
                            waits=[(cTE, cTE.n), (dGP, dGP.n)])
                kw = [(dSY, d_kp), (dGP, d_sn)]
                vw = [(dSY, d_vh)]
            else:
                d_sn, d_vh = b0pre[h]
                kw = [(dGP, d_sn)]
                vw = [(dGP, d_vh)]
            if h + 1 < NH:
                h1 = h + 1
                cb1 = (h1 % 2) * 2048
                hp1 = (h1 + 8) % 16
                wk = ([(cVE, b0q2[h - 1])] if h >= 1 else
                      [(cVE, cVE.n), (cTE, cTE.n), (cSC, cSC.n),
                       (dSY, dSY.n)])
                gdma(KB2[:, cb1:cb1 + 2048],
                     s_kT[128 * h1:128 * (h1 + 1), :], waits=wk)
                gdma(KB2[:, 4096 + cb1:4096 + cb1 + 2048],
                     s_kT[128 * hp1:128 * (hp1 + 1), :], waits=[])
                wv = ([(cTE, b0mem[h - 1])] if h >= 1 else
                      [(cVE, cVE.n), (cTE, cTE.n), (cSC, cSC.n)])
                p_vh = gdma(XC2[:, 4096 + cb1:4096 + cb1 + 2048]
                            .rearrange("p (j q) -> p j q", q=128),
                            s_v[:, 128 * h1:128 * (h1 + 1)]
                            .rearrange("(j p) q -> p j q", p=128),
                            waits=wv, slow=True)
            ve(lambda E, cb=cb: E.tensor_tensor(
                U23[:, 4096:6144], KB2[:, cb:cb + 2048], U23[:, 0:2048],
                op=ALU.mult),
               waits=kw + [(cSC, cSC.n), (cTE, cTE.n)])
            ve(lambda E, cb=cb: E.tensor_tensor(
                MSC[:, 2048:4096], KB2[:, 4096 + cb:4096 + cb + 2048],
                U23[:, 2048:4096], op=ALU.mult))
            mkr = ve(lambda E: E.tensor_tensor(
                U23[:, 4096:6144], U23[:, 4096:6144], MSC[:, 2048:4096],
                op=ALU.add))
            sdma(s_kTr[128 * h:128 * (h + 1), :], U23[:, 4096:6144],
                 waits=[(cVE, mkr)])
            mtr_ev = 0
            for g in range(2):
                mtr = 0
                for jj in range(8):
                    j = 8 * g + jj
                    wtr = [(cVE, mkr)]
                    if jj == 0 and mtr_ev:
                        wtr.append((cSC, mtr_ev))
                    mtr = te(lambda E, j=j, jj=jj: E.transpose(
                        pbt[:, 128 * jj:128 * (jj + 1)],
                        U23[:, 4096 + 128 * j:4096 + 128 * (j + 1)],
                        IDENB[:, :]), waits=wtr)
                mtr_ev = sca(lambda E, g=g, cb=cb: E.copy(
                    XC2[:, cb + 1024 * g:cb + 1024 * (g + 1)], pbt[:, :]),
                    waits=[(cTE, mtr)])
            d_qt = sdma(KB2[:, cb:cb + 2048], s_qT[128 * h:128 * (h + 1), :],
                        waits=[(cVE, cVE.n)])
            d_qp = sdma(KB2[:, 4096 + cb:4096 + cb + 2048],
                        s_qT[128 * hp:128 * (hp + 1), :], waits=[])
            ve(lambda E, cb=cb: E.tensor_tensor(
                U23[:, 6144:8192], KB2[:, cb:cb + 2048], U23[:, 0:2048],
                op=ALU.mult), waits=[(dSY, d_qp)])
            m_q2 = ve(lambda E, cb=cb: E.tensor_tensor(
                MSC[:, 2048:4096], KB2[:, 4096 + cb:4096 + cb + 2048],
                U23[:, 2048:4096], op=ALU.mult))
            b0q2[h] = m_q2
            mqr = ve(lambda E: E.tensor_tensor(
                U23[:, 6144:8192], U23[:, 6144:8192], MSC[:, 2048:4096],
                op=ALU.add))
            sdma(s_qTr[128 * h:128 * (h + 1), :], U23[:, 6144:8192],
                 waits=[(cVE, mqr)])
            if h + 1 < NH:
                gdma(U23[:, 0:2048], cosTd[128 * (h + 1):128 * (h + 2), :],
                     waits=[(cVE, m_q2)])
                p_sn = gdma(U23[:, 2048:4096],
                            sinTd[128 * (h + 1):128 * (h + 2), :], waits=[])
                b0pre[h + 1] = (p_sn, p_vh)
            mv1 = ve(lambda E, cb=cb: E.tensor_scalar_min(
                MSC[:, 2048:4096], XC2[:, cb:cb + 2048], 0.0),
                waits=[(cSC, mtr_ev)])
            ms1 = sca(lambda E: E.activation(MSC[:, 2048:4096], MSC[:, 2048:4096],
                                             AF.Exp), waits=[(cVE, mv1)])
            mv2 = ve(lambda E, cb=cb: E.tensor_relu(MSC[:, 0:2048],
                                                    XC2[:, cb:cb + 2048]))
            mkf = ve(lambda E: E.tensor_tensor(MSC[:, 0:2048], MSC[:, 0:2048],
                                               MSC[:, 2048:4096], op=ALU.add),
                     waits=[(cSC, ms1)])
            for j in range(NT):
                w = []
                if j == 0:
                    w = [(cVE, mkf), (cSC, cSC.n)] + vw
                te(lambda E, j=j, cb=cb: E.matmul(
                    pb[0][:, 0:128], MSC[:, 128 * j:128 * (j + 1)],
                    XC2[:, 4096 + cb + 128 * j:4096 + cb + 128 * (j + 1)],
                    start=(j == 0), stop=(j == NT - 1)), waits=w)
                te(lambda E, j=j: E.matmul(
                    pb[1][0:1, 0:128], PARB[:, B_ones:B_ones + 1],
                    MSC[:, 128 * j:128 * (j + 1)],
                    start=(j == 0), stop=(j == NT - 1)))
            m_mem = cTE.n
            b0mem[h] = m_mem
            sca(lambda E, h=h: E.copy(U1[:, 128 * h:128 * (h + 1)],
                                      pb[0][:, 0:128]),
                waits=[(cTE, m_mem)] + ([(cVE, cVE.n)] if h == 0 else []))
            m_e2 = sca(lambda E, h=h: E.copy(ROWS[0:1, 128 * h:128 * (h + 1)],
                                             pb[1][0:1, 0:128]),
                       waits=[(cTE, m_mem)])
        d_nw = gdma(s_row[3:4, 0:2048], ROWS[0:1, 0:2048], waits=[(cSC, cSC.n)])
        d_norm = gdma(PARB[:, B_norm:B_norm + 16],
                      s_row[3, 0:2048].rearrange("(h p) -> p h", p=128),
                      waits=[(dGP, d_nw)], slow=True)

        tap("memB", U1[:, 0:2048], [128, 2048], BF16)
        if stage < 5:
            _finish(P, cGP, dGP, dSY)
            return _close(nc, P)

        for qcc in range(2):
            qb0 = 1024 * qcc
            wR = [(cVE, cVE.n), (cTE, cTE.n)] if qcc > 0 else [(dSY, d_rspill)]
            d_rload = sdma(R.rearrange("p (i q) -> p i q", q=CH),
                           s_R[:, qb0:qb0 + CH]
                           .rearrange("(i p) q -> p i q", p=128), waits=wR)
            hq = [(qc2, h) for h in range(NH) for qc2 in range(2)]
            lds = {}
            mk = {}

            def issue_loads(j):
                qc2, h = hq[j]
                qb = qb0 + 512 * qc2
                cb = (h % 2) * 2048
                pm = mk.get(j - 2, {})
                if qc2 == 0:
                    pm4 = mk.get(j - 3, {})
                    wkv = ([(cTE, pm4.get("av", 0))] if pm4 else
                           [(cTE, cTE.n)])
                    d_kt = sdma(KB2[:, cb:cb + 2048],
                                s_kTr[128 * h:128 * (h + 1), :], waits=wkv)
                    d_vh = sdma(KB2[:, 4096 + cb:4096 + cb + 2048]
                                .rearrange("p (j q) -> p j q", q=128),
                                s_v[:, 128 * h:128 * (h + 1)]
                                .rearrange("(j p) q -> p j q", p=128),
                                waits=(wkv if pm4 else
                                       [(cTE, cTE.n), (dGP, dGP.n)]))
                    lds[("kv", h)] = (d_kt, d_vh)
                d_kt, d_vh = lds[("kv", h)]
                d_qt = sdma(MSC[:, 512 * (j % 2):512 * (j % 2) + 512],
                            s_qTr[128 * h:128 * (h + 1), qb:qb + 512],
                            waits=([(cTE, pm.get("mo", 0)),
                                    (cVE, pm.get("qdve", 0))] if pm else
                                   [(cTE, cTE.n), (cVE, cVE.n)]))
                lds[j] = (d_kt, d_vh, d_qt)

            issue_loads(0)
            issue_loads(1)
            for j, (qc2, h) in enumerate(hq):
                cb = (h % 2) * 2048
                qsl = MSC[:, 512 * (j % 2):512 * (j % 2) + 512]
                qf = MSC[:, 1536 + 512 * (j % 2):1536 + 512 * (j % 2) + 512]
                d_kt, d_vh, d_qt = lds[j]
                pm = mk.get(j - 1, {})
                m = {}
                mv1 = ve(lambda E, q=qsl: E.tensor_scalar_min(
                    MSC[:, 1024:1536], q, 0.0),
                    waits=[(dSY, d_qt), (cSC, pm.get("me2", 0))])
                ms1 = sca(lambda E: E.activation(
                    MSC[:, 1024:1536], MSC[:, 1024:1536], AF.Exp),
                    waits=[(cVE, mv1)])
                mv2 = ve(lambda E, q=qsl, f=qf: E.tensor_relu(f, q))
                m["qdve"] = mv2
                mqf = ve(lambda E, f=qf: E.tensor_tensor(
                    f, f, MSC[:, 1024:1536], op=ALU.add), waits=[(cSC, ms1)])
                m_exps = []
                mmks = []

                def emit_S(kt):
                    bank = pb[kt % 3]
                    w = []
                    if kt == 0:
                        w = [(dSY, d_kt), (cSC, pm.get("exp15", 0)),
                             (cVE, pm.get("comb", 0))]
                    elif kt in (1, 2):
                        w = [(cSC, pm.get("exp15", 0)),
                             (cVE, pm.get("comb", 0))]
                    else:
                        w = [(cSC, m_exps[kt - 3])]
                    mmk = te(lambda E, kt=kt, b=bank, cb=cb, q=qsl: E.matmul(
                        b[:, :], KB2[:, cb + 128 * kt:cb + 128 * (kt + 1)],
                        q, start=True, stop=True), waits=w)
                    me = sca(lambda E, kt=kt, b=bank: E.activation(
                        XC2[:, 512 * kt:512 * (kt + 1)], b[:, :], AF.Exp,
                        bias=CAPS[:, 4:5], scale=CAPS[:, 5:6]),
                        waits=[(cTE, mmk)])
                    m_exps.append(me)

                for kt in range(3):
                    emit_S(kt)
                for kt in range(NT):
                    if kt + 3 < NT:
                        emit_S(kt + 3)
                    w = [(cSC, m_exps[kt])]
                    if kt == 0:
                        w += [(dSY, d_vh), (cSC, pm.get("me1", 0)),
                              (cVE, pm.get("mr1", 0))]
                    te(lambda E, kt=kt, cb=cb: E.matmul(
                        pb[3][:, :],
                        KB2[:, 4096 + cb + 128 * kt:4096 + cb + 128 * (kt + 1)],
                        XC2[:, 512 * kt:512 * (kt + 1)],
                        start=(kt == 0), stop=(kt == NT - 1)), waits=w)
                    te(lambda E, kt=kt: E.matmul(
                        pb[4][0:1, :], PARB[:, B_ones:B_ones + 1],
                        XC2[:, 512 * kt:512 * (kt + 1)],
                        start=(kt == 0), stop=(kt == NT - 1)))
                m["exp15"] = m_exps[15]
                m["exp14"] = m_exps[14]
                m["av"] = cTE.n
                te(lambda E, h=h, f=qf: E.matmul(
                    pb[5][:, :], U1[:, 128 * h:128 * (h + 1)], f,
                    start=True, stop=True),
                   waits=[(cVE, mqf), (dGP, d_norm), (cSC, pm.get("me2", 0)),
                          (cVE, pm.get("mr2", 0))])
                m_mo = te(lambda E, h=h, f=qf: E.matmul(
                    pb[6][0:1, :], PARB[:, B_norm + h:B_norm + h + 1], f,
                    start=True, stop=True))
                m["mo"] = m_mo
                if j + 2 < len(hq):
                    issue_loads(j + 2)
                mr1 = ve(lambda E: E.reciprocal(ROWS[0:1, 0:512],
                                                pb[4][0:1, :]),
                         waits=[(cTE, m_mo)])
                m["mr1"] = mr1
                mr2 = ve(lambda E: E.reciprocal(ROWS[0:1, 512:1024],
                                                pb[6][0:1, :]))
                m["mr2"] = mr2
                mrb = ve(lambda E, h=h: E.tensor_scalar(
                    ROWS[0:1, 1024:1536], ROWS[0:1, 512:1024],
                    GS[0:1, h:h + 1], None, op0=ALU.mult),
                    waits=[(cTE, pm.get("rep", 0))])
                mrb = ve(lambda E, h=h: E.tensor_scalar(
                    ROWS[0:1, 1536:2048], ROWS[0:1, 0:512],
                    GS[0:1, 16 + h:16 + h + 1], None, op0=ALU.mult),
                    waits=[(cVE, m_g1m)])
                te(lambda E: E.matmul(
                    pb[0][:, :], ONER[0:1, :], ROWS[0:1, 1024:1536],
                    start=True, stop=True),
                   waits=[(cVE, mrb), (cSC, m_exps[15])])
                m["rep"] = te(lambda E: E.matmul(
                    pb[1][:, :], ONER[0:1, :], ROWS[0:1, 1536:2048],
                    start=True, stop=True), waits=[(cSC, m_exps[13])])
                m["sc"] = m["rep"]
                me1 = sca(lambda E: E.copy(MSC[:, 2560:3072], pb[3][:, :]),
                          waits=[(cTE, m["rep"])])
                m["me1"] = me1
                me2 = sca(lambda E: E.copy(MSC[:, 3072:3584], pb[5][:, :]))
                m["me2"] = me2
                mt1 = ve(lambda E: E.tensor_tensor(
                    MSC[:, 3584:4096], MSC[:, 2560:3072], pb[1][:, :],
                    op=ALU.mult), waits=[(cSC, me2), (cTE, m["rep"])])
                ve(lambda E: E.tensor_tensor(
                    MSC[:, 2560:3072], MSC[:, 3072:3584], pb[0][:, :],
                    op=ALU.mult))
                m["comb"] = ve(lambda E, h=h, q=qc2: E.tensor_tensor(
                    BC4[:, 1024 * h + 512 * q:1024 * h + 512 * (q + 1)],
                    MSC[:, 3584:4096], MSC[:, 2560:3072], op=ALU.add))
                mk[j] = m
            m_comb = cVE.n
            lastv = 0
            for o in range(NT):
                kb = KB2[:, (o % 2) * 2048:(o % 2) * 2048 + 2048]
                dW = sdma(kb.rearrange("p (i q) -> p i q", q=128),
                          s_woT[:, 128 * o:128 * (o + 1)]
                          .rearrange("(i p) q -> p i q", p=128),
                          waits=[(cTE, cTE.n)])
                for half in range(2):
                    bank = pb[2 + half]
                    for i in range(NT):
                        w = []
                        if i == 0:
                            w = [(dSY, dW), (cVE, max(m_comb, lastv)),
                                 (cSC, cSC.n)]
                        te(lambda E, i=i, h=half, b=bank, kb=kb: E.matmul(
                            b[:, :], kb[:, 128 * i:128 * (i + 1)],
                            BC4[:, 1024 * i + 512 * h:1024 * i + 512 * (h + 1)],
                            start=(i == 0), stop=(i == NT - 1)), waits=w)
                    m_mm = cTE.n
                    rsl = R[:, CH * o + 512 * half:CH * o + 512 * half + 512]
                    lastv = ve(lambda E, r=rsl, b=bank: E.tensor_tensor(
                        r, r, b[:, :], op=ALU.add),
                        waits=[(cTE, m_mm), (dSY, d_rload)])
            m_wo = lastv
            m_u4 = 0
            for half in range(2):
                m_u4 = ln_feat(lambda i, h=half: U23[:, CH * i + 512 * h:
                                                     CH * i + 512 * h + 512],
                               512 * half, 512,
                               wait_ve=m_wo, wait_x=[(cTE, cTE.n)])
            mx = ve(lambda E: E.tensor_tensor(F32A[:, 2048:3072], U23[:, 0:CH],
                                              U23[:, 0:CH], op=ALU.mult),
                    waits=[(cVE, m_u4)])
            for i in range(1, NT):
                ve(lambda E, i=i: E.tensor_tensor(
                    F32B[:, 1024:2048], U23[:, CH * i:CH * i + CH],
                    U23[:, CH * i:CH * i + CH], op=ALU.mult))
                mx = ve(lambda E: E.tensor_tensor(
                    F32A[:, 2048:3072], F32A[:, 2048:3072], F32B[:, 1024:2048],
                    op=ALU.max))
            width = 64
            while width >= 1:
                dsh = gdma(F32B[0:width, 0:1024], F32A[width:2 * width, 2048:3072],
                           waits=[(cVE, mx)])
                mx = ve(lambda E, w=width: E.tensor_tensor(
                    F32A[0:w, 2048:3072], F32A[0:w, 2048:3072],
                    F32B[0:w, 0:1024], op=ALU.max), waits=[(dGP, dsh)])
                width //= 2
            ms = sca(lambda E: E.activation(ROWS[0:1, 0:1024],
                                            F32A[0:1, 2048:3072], AF.Sqrt),
                     waits=[(cVE, mx), (cTE, cTE.n)])
            mv = ve(lambda E: E.tensor_scalar(ROWS[0:1, 1024:2048],
                                              ROWS[0:1, 0:1024], 1.0 / 127.0,
                                              1e-12, op0=ALU.mult, op1=ALU.max),
                    waits=[(cSC, ms)])
            mvr = ve(lambda E: E.reciprocal(ROWS[0:1, 2048:3072],
                                            ROWS[0:1, 1024:2048]))
            for j in range(2):
                te(lambda E, j=j: E.matmul(
                    pb[3 + j][:, :], ONER[0:1, :],
                    ROWS[0:1, 1024 + 512 * j:1024 + 512 * (j + 1)],
                    start=True, stop=True),
                   waits=[(cVE, mvr), (cSC, cSC.n)] if j == 0 else [])
                te(lambda E, j=j: E.matmul(
                    pb[5 + j][:, :], ONER[0:1, :],
                    ROWS[0:1, 2048 + 512 * j:2048 + 512 * (j + 1)],
                    start=True, stop=True))
            m_reps = cTE.n
            m_xsr = 0
            for j in range(2):
                ve(lambda E, j=j: E.tensor_copy(F32A[:, 512 * j:512 * (j + 1)],
                                                pb[3 + j][:, :]),
                   waits=[(cTE, m_reps)])
                m_xsr = ve(lambda E, j=j: E.tensor_copy(
                    F32A[:, 1024 + 512 * j:1024 + 512 * (j + 1)],
                    pb[5 + j][:, :]))
            m_xq = 0
            for i in range(NT):
                ve(lambda E, i=i: E.scalar_tensor_tensor(
                    F32B[:, 0:1024], U23[:, CH * i:CH * i + CH], 1.0,
                    F32A[:, 1024:2048], op0=ALU.mult, op1=ALU.mult),
                   waits=[(cVE, m_xsr)] if i == 0 else [])
                m_xq = ve(lambda E, i=i: E.tensor_scalar(
                    U23[:, CH * i:CH * i + CH], F32B[:, 0:1024], MAGIC, MAGIC,
                    op0=ALU.add, op1=ALU.subtract))
            tr_marks = {}
            for o in range(NT):
                kb = KB2[:, (o % 2) * 2048:(o % 2) * 2048 + 2048]
                dW = sdma(kb.rearrange("p (i q) -> p i q", q=128),
                          s_wq1[:, 128 * o:128 * (o + 1)]
                          .rearrange("(i p) q -> p i q", p=128),
                          waits=[(cTE, cTE.n)])
                for half in range(2):
                    bank = pb[half]
                    for i in range(NT):
                        w = []
                        if i == 0:
                            w = [(dSY, dW), (cVE, max(m_xq, cVE.n)), (cSC, cSC.n)]
                        te(lambda E, i=i, h=half, b=bank, kb=kb: E.matmul(
                            b[:, :], kb[:, 128 * i:128 * (i + 1)],
                            U23[:, CH * i + 512 * h:CH * i + 512 * (h + 1)],
                            start=(i == 0), stop=(i == NT - 1)), waits=w)
                    m_mm = cTE.n
                    ve(lambda E, o=o, b=bank, h=half: E.scalar_tensor_tensor(
                        F32B[:, 0:512], b[:, :],
                        PAR[:, P_ws1 + o:P_ws1 + o + 1],
                        F32A[:, 512 * h:512 * (h + 1)],
                        op0=ALU.mult, op1=ALU.mult), waits=[(cTE, m_mm)])
                    rsl = R[:, CH * o + 512 * half:CH * o + 512 * half + 512]
                    ve(lambda E, o=o, r=rsl, h=half: E.scalar_tensor_tensor(
                        OUTS[:, 512 * h:512 * (h + 1)], F32B[:, 0:512],
                        PAR[:, P_b1 + o:P_b1 + o + 1], r,
                        op0=ALU.add, op1=ALU.add))
                m_row = cVE.n
                for tt in range(8):
                    bank = pb[3 + tt % 2]
                    w = [(cVE, m_row)]
                    if (o, tt) != (0, 0):
                        pass
                    if tt >= 2 or o > 0:
                        prev = tr_marks.get((o, tt - 2)) or \
                            tr_marks.get((o - 1, tt + 6))
                        if prev:
                            w.append((cGP, prev))
                    mtr = te(lambda E, tt=tt, b=bank: E.transpose(
                        b[:, 0:128], OUTS[:, 128 * tt:128 * (tt + 1)],
                        IDEN[:, :]), waits=w)
                    msc = sca(lambda E, tt=tt, b=bank: E.copy(
                        F32A[:, 3072 + 128 * (tt % 2):3072 + 128 * (tt % 2) + 128],
                        b[:, 0:128]), waits=[(cTE, mtr)])
                    P.op("gp", lambda E, tt=tt, o=o, q=qb0: E.dma_start(
                        out=out[q + 128 * tt:q + 128 * (tt + 1),
                                128 * o:128 * (o + 1)],
                        in_=F32A[:, 3072 + 128 * (tt % 2):
                                 3072 + 128 * (tt % 2) + 128]),
                         waits=[(cSC, msc)], inc=cGP, k=16)
                    tr_marks[(o, tt)] = cGP.n
        _finish(P, cGP, dGP, dSY)
        return _close(nc, P)


def _cols16(v):
    return np.ascontiguousarray(np.asarray(v).reshape(16, 128).T)


def _host_inputs(inp):
    f32 = np.float32
    m = {}
    m["convw"] = np.ascontiguousarray(
        np.asarray(inp["L0_attn_conv_w"]).reshape(16, 128, 4)
        .transpose(1, 0, 2).reshape(128, 64)).astype(f32)
    m["convb"] = _cols16(inp["L0_attn_conv_b"]).astype(f32)
    m["xprojT"] = np.ascontiguousarray(np.asarray(inp["L0_attn_xproj_w"]).T).astype(f32)
    m["dtprojT"] = np.ascontiguousarray(np.asarray(inp["L0_attn_dtproj_w"]).T).astype(f32)
    m["dtprojb"] = _cols16(inp["L0_attn_dtproj_b"]).astype(f32)
    m["alog"] = np.ascontiguousarray(
        np.asarray(inp["L0_attn_A_log"]).reshape(16, 128, 16)
        .transpose(1, 0, 2).reshape(128, 256)).astype(f32)
    m["dpar"] = _cols16(inp["L0_attn_D_param"]).astype(f32)
    m["w0T"] = np.ascontiguousarray(np.asarray(inp["L0_ffn_weight"]).T).astype(f32)
    m["b0c"] = _cols16(inp["L0_ffn_bias"]).astype(f32)
    m["wqT"] = np.ascontiguousarray(np.asarray(inp["L1_attn_Wq"]).T).astype(f32)
    m["wkT"] = np.ascontiguousarray(np.asarray(inp["L1_attn_Wk"]).T).astype(f32)
    m["wvT"] = np.ascontiguousarray(np.asarray(inp["L1_attn_Wv"]).T).astype(f32)
    m["woT"] = np.ascontiguousarray(np.asarray(inp["L1_attn_Wo"]).T).astype(f32)
    m["gate"] = np.ascontiguousarray(np.asarray(inp["L1_attn_gate"]).reshape(1, 16)).astype(f32)
    m["w1T"] = np.ascontiguousarray(np.asarray(inp["L1_ffn_weight"]).T).astype(f32)
    m["b1c"] = _cols16(inp["L1_ffn_bias"]).astype(f32)
    t = np.arange(T, dtype=np.float64)
    invf = 1.0 / (10000.0 ** (np.arange(0, D, 2, dtype=np.float64) / D))
    ang = invf[:, None] * t[None, :]
    m["cosT"] = np.concatenate([np.cos(ang), np.cos(ang)], 0).astype(f32)
    m["sinT"] = np.concatenate([-np.sin(ang), np.sin(ang)], 0).astype(f32)
    m["ident"] = np.eye(128, dtype=f32)
    return m


def kernel(**inputs):
    shared = _host_inputs(inputs)
    x = np.asarray(inputs["L0_attn_x"])
    nc = build_program()
    in_maps = []
    for b in range(8):
        mm = dict(shared)
        mm["xT"] = np.ascontiguousarray(x[b].T).astype(np.float32)
        in_maps.append(mm)
    res = run_bass_kernel_spmd(nc, in_maps, core_ids=list(range(8)))
    return np.stack([np.asarray(r["y"], np.float32) for r in res.results], 0)


# version 14
# speedup vs baseline: 1.0790x; 1.0149x over previous
import numpy as np
import ml_dtypes
import concourse.bass as bass
import concourse.mybir as mybir
from concourse.bass_utils import run_bass_kernel_spmd
from contextlib import ExitStack

F32 = mybir.dt.float32
BF16 = mybir.dt.bfloat16
AF = mybir.ActivationFunctionType
ALU = mybir.AluOpType

T = 2048; D = 2048; NT = 16; NST = 16; NH = 16
CH = 1024; NCH = T // CH
SC = 512; NSC = T // SC
SCPC = CH // SC
MAGIC = 12582912.0
INVSQ = float(1.0 / np.sqrt(128.0))
SOFTBIAS = -20.0
UPAD = 8
UW = SC + UPAD


class Ctr:
    def __init__(self, name):
        self.name = name
        self.n = 0


class Prog:
    def __init__(self):
        self.items = {e: [] for e in ("gp", "sy", "te", "ve", "sc")}

    def op(self, eng, fn, waits=(), inc=None, k=1):
        wl = [(c.name, v) for (c, v) in waits if v and v > 0]
        tgt = None
        if inc is not None:
            inc.n += k
            tgt = (inc.name, k)

        def run(E, sems):
            for nm, v in wl:
                E.wait_ge(sems[nm], v)
            ins = fn(E)
            if tgt is not None:
                ins.then_inc(sems[tgt[0]], tgt[1])
        self.items[eng].append(run)


def _finish(P, cGP, dGP, dSY):
    P.op("gp", lambda E: None, waits=[(cGP, cGP.n), (dGP, dGP.n)])
    P.op("sy", lambda E: None, waits=[(cGP, cGP.n), (dSY, dSY.n)])


def _close(nc, P):
    with (nc.semaphore("sdGP") as s0, nc.semaphore("sdSY") as s1,
          nc.semaphore("scTE") as s2, nc.semaphore("scVE") as s3,
          nc.semaphore("scSC") as s4, nc.semaphore("scGP") as s5,
          nc.Block() as block):
        sems = {"dGP": s0, "dSY": s1, "cTE": s2, "cVE": s3,
                "cSC": s4, "cGP": s5}

        @block.gpsimd
        def _(E):
            for fn in P.items["gp"]:
                fn(E, sems)

        @block.sync
        def _(E):
            for fn in P.items["sy"]:
                fn(E, sems)

        @block.tensor
        def _(E):
            for fn in P.items["te"]:
                fn(E, sems)

        @block.vector
        def _(E):
            for fn in P.items["ve"]:
                fn(E, sems)

        @block.scalar
        def _(E):
            for fn in P.items["sc"]:
                fn(E, sems)

    return nc


def build_program(taps=(), stage=99):
    nc = bass.Bass("TRN2", target_bir_lowering=False,
                   detect_race_conditions=False)
    P = Prog()
    taps = set(taps)

    def din(name, shape, dt=F32):
        return nc.dram_tensor(name, shape, dt, kind="ExternalInput").ap()

    xT = din("xT", [D, T])
    convw = din("convw", [128, 4 * NT])
    convb = din("convb", [128, NT])
    xprojT_d = din("xprojT", [D, 160])
    dtprojT_d = din("dtprojT", [128, D])
    dtprojb = din("dtprojb", [128, NT])
    alog = din("alog", [128, NST * NT])
    dpar = din("dpar", [128, NT])
    w0T = din("w0T", [D, D])
    b0c = din("b0c", [128, NT])
    wqT = din("wqT", [D, D])
    wkT = din("wkT", [D, D])
    wvT = din("wvT", [D, D])
    woT = din("woT", [D, D])
    gate = din("gate", [1, NH])
    w1T = din("w1T", [D, D])
    b1c = din("b1c", [128, NT])
    cosTd = din("cosT", [D, T])
    sinTd = din("sinT", [D, T])
    identd = din("ident", [128, 128])
    out = nc.dram_tensor("y", [T, D], F32, kind="ExternalOutput").ap()

    def scr(name, shape, dt):
        return nc.dram_tensor(name, shape, dt, kind="Internal").ap()

    s_bc = scr("s_bc", [NSC, 32, SC], BF16)
    s_sign = scr("s_sign", [D, D], BF16)
    s_wq1 = scr("s_wq1", [D, D], BF16)
    s_woT = scr("s_woT", [D, D], BF16)
    s_qT = scr("s_qT", [D, T], BF16)
    s_kT = scr("s_kT", [D, T], BF16)
    s_v = scr("s_v", [T, D], BF16)
    s_row = scr("s_row", [8, 2048], F32)
    s_R = scr("s_R", [D, T], BF16)
    s_qTr = scr("s_qTr", [D, T], BF16)
    s_kTr = scr("s_kTr", [D, T], BF16)

    dGP = Ctr("dGP"); dSY = Ctr("dSY")
    cTE = Ctr("cTE"); cVE = Ctr("cVE"); cSC = Ctr("cSC"); cGP = Ctr("cGP")

    tap_info = {}

    with ExitStack() as ctx:
        def sbuf(name, shape, dt):
            return ctx.enter_context(nc.sbuf_tensor(name, shape, dt))

        R = sbuf("R", [128, NT * CH], BF16)
        U1 = sbuf("U1", [128, NT * UW], BF16)
        U23 = sbuf("U23", [128, NT * CH], BF16)
        BC4 = sbuf("BC4", [128, 16384], BF16)
        XC2 = sbuf("XC2", [128, NT * SC], BF16)
        KB2 = sbuf("KB2", [128, 8192], BF16)
        MSC = sbuf("MSC", [128, 4096], BF16)
        F32A = sbuf("F32A", [128, 3584], F32)
        F32B = sbuf("F32B", [128, 2048], F32)
        PAR = sbuf("PAR", [128, 768], F32)
        PARB = sbuf("PARB", [128, 4640], BF16)
        ROWS = sbuf("ROWS", [1, 4096], F32)
        OUTS = sbuf("OUTS", [128, 1024], F32)
        IDEN = sbuf("IDEN", [128, 128], F32)
        ONER = sbuf("ONER", [1, 128], F32)
        IDENB = sbuf("IDENB", [128, 128], BF16)
        GS = sbuf("GS", [1, 32], F32)
        CAPS = sbuf("CAPS", [128, 8], F32)

        PS3 = ctx.enter_context(nc.psum_tensor("PS3", [128, 1536], F32))
        pb = [PS3[:, 512 * i:512 * (i + 1)] for i in range(3)]
        pb += [ctx.enter_context(nc.psum_tensor(f"pb{i}", [128, 512], F32))
               for i in range(3, 7)]
        pbt = ctx.enter_context(nc.psum_tensor("pbt", [128, 1024], BF16))

        P_convw = 0; P_convb = 64; P_dtb = 80; P_dpar = 96
        P_b0 = 112; P_b1 = 128; P_alpha = 144; P_ws1 = 160
        P_aneg = 176
        P_hc = 432
        P_meanf = 688
        B_dtp = 0
        B_xp = 2048
        B_ones = 4608
        B_mean = 4610
        B_norm = 4612


        def gdma(dst, src, waits=(), slow=False):
            P.op("gp", lambda E, d=dst, s=src, sl=slow: E.dma_start(
                out=d, in_=s, allow_slow_non_contiguous=sl),
                 waits=waits, inc=dGP, k=16)
            return dGP.n

        def sdma(dst, src, waits=(), transpose=False):
            P.op("sy", lambda E, d=dst, s=src, t=transpose:
                 E.dma_start(out=d, in_=s, transpose=t),
                 waits=waits, inc=dSY, k=16)
            return dSY.n

        def te(fn, waits=()):
            P.op("te", fn, waits=waits, inc=cTE)
            return cTE.n

        def ve(fn, waits=()):
            P.op("ve", fn, waits=waits, inc=cVE)
            return cVE.n

        def sca(fn, waits=()):
            P.op("sc", fn, waits=waits, inc=cSC)
            return cSC.n

        def tap(name, ap, shape, dt=F32):
            if name not in taps:
                return
            t = nc.dram_tensor("tap_" + name, shape, dt, kind="ExternalOutput").ap()
            tap_info[name] = True
            w = [(cTE, cTE.n), (cVE, cVE.n), (cSC, cSC.n), (dSY, dSY.n)]
            gdma(t, ap, waits=w)

        gdma(PARB[:, B_dtp:B_dtp + 2048], dtprojT_d[:, :])
        gdma(PARB[:, B_xp:B_xp + 2560].rearrange("p (i q) -> p i q", q=160),
             xprojT_d.rearrange("(i p) q -> p i q", p=128))
        gdma(PAR[:, P_convw:P_convw + 64], convw[:, :])
        gdma(PAR[:, P_convb:P_convb + 16], convb[:, :])
        gdma(PAR[:, P_dtb:P_dtb + 16], dtprojb[:, :])
        gdma(PAR[:, P_dpar:P_dpar + 16], dpar[:, :])
        gdma(PAR[:, P_b0:P_b0 + 16], b0c[:, :])
        gdma(PAR[:, P_b1:P_b1 + 16], b1c[:, :])
        gdma(PAR[:, P_aneg:P_aneg + 256], alog[:, :])
        gdma(IDEN[:, :], identd[:, :])
        gdma(IDENB[:, :], identd[:, :])
        gdma(ROWS[0:1, 0:16], gate[:, :])
        d_par = dGP.n
        sca(lambda E: E.activation(PAR[:, P_aneg:P_aneg + 256],
                                   PAR[:, P_aneg:P_aneg + 256], AF.Exp),
            waits=[(dGP, d_par)])
        ve(lambda E: E.tensor_scalar_mul(PAR[:, P_aneg:P_aneg + 256],
                                         PAR[:, P_aneg:P_aneg + 256], -1.0),
           waits=[(cSC, cSC.n)])
        ve(lambda E: E.memset(PARB[:, B_ones:B_ones + 1], 1.0))
        ve(lambda E: E.memset(ONER[0:1, :], 1.0))
        ve(lambda E: E.memset(PAR[:, P_meanf:P_meanf + 1], 1.0 / 2048.0))
        ve(lambda E: E.memset(CAPS[:, 0:1], 1e-5))
        ve(lambda E: E.memset(CAPS[:, 1:2], -0.5))
        ve(lambda E: E.memset(CAPS[:, 2:3], -1.0))
        ve(lambda E: E.memset(CAPS[:, 3:4], 1.0))
        ve(lambda E: E.memset(CAPS[:, 4:5], SOFTBIAS))
        ve(lambda E: E.memset(CAPS[:, 5:6], INVSQ))
        m_ones = ve(lambda E: E.memset(PARB[:, B_mean:B_mean + 1], 1.0 / 2048.0))
        ve(lambda E: E.memset(PAR[:, P_hc:P_hc + 256], 0.0))
        m_ut0 = ve(lambda E: E.memset(U1[:, :], 0.0))
        m_gs = sca(lambda E: E.activation(GS[0:1, 0:16], ROWS[0:1, 0:16],
                                          AF.Sigmoid), waits=[(dGP, d_par)])
        m_g1m = ve(lambda E: E.tensor_scalar(GS[0:1, 16:32], GS[0:1, 0:16],
                                             -1.0, 1.0, op0=ALU.mult, op1=ALU.add),
                   waits=[(cSC, m_gs)])

        d_xload0 = gdma(R.rearrange("p (i q) -> p i q", q=CH),
                        xT[:, 0:CH].rearrange("(i p) q -> p i q", p=128),
                        waits=[])

        for i in range(NT):
            kb = KB2[:, (i % 2) * 2048:(i % 2) * 2048 + 2048]
            dW = gdma(kb, w0T[128 * i:128 * (i + 1), :],
                      waits=[(cVE, cVE.n), (cSC, cSC.n)])
            m1 = ve(lambda E, b=kb: E.tensor_scalar(
                MSC[:, 0:2048], b, 0.0, None, op0=ALU.is_ge),
                waits=[(dGP, dW), (dSY, dSY.n)])
            m2 = ve(lambda E: E.tensor_scalar(
                MSC[:, 0:2048], MSC[:, 0:2048], 2.0, -1.0,
                op0=ALU.mult, op1=ALU.add))
            sdma(s_sign[128 * i:128 * (i + 1), :], MSC[:, 0:2048],
                 waits=[(cVE, m2)])
            ma = sca(lambda E, b=kb: E.activation(MSC[:, 2048:4096], b, AF.Abs),
                     waits=[(dGP, dW), (cTE, cTE.n)])
            for j in range(4):
                te(lambda E, j=j, i=i: E.matmul(
                    pb[3 + j][0:1, :], PARB[:, B_mean:B_mean + 1],
                    MSC[:, 2048 + 512 * j:2048 + 512 * (j + 1)],
                    start=(i == 0), stop=(i == NT - 1)),
                   waits=[(cSC, ma), (cVE, m_ones)])
        m_amm = cTE.n
        m_acp = 0
        for j in range(4):
            m_acp = sca(lambda E, j=j: E.copy(ROWS[0:1, 512 * j:512 * (j + 1)],
                                              pb[3 + j][0:1, :]),
                        waits=[(cTE, m_amm)])
        d_aw = gdma(s_row[1:2, 0:2048], ROWS[0:1, 0:2048], waits=[(cSC, m_acp)])
        gdma(PAR[:, P_alpha:P_alpha + 16],
             s_row[1, 0:2048].rearrange("(i p) -> p i", p=128),
             waits=[(dGP, d_aw)], slow=True)

        def w1_prep():
            mxm = 0
            for i in range(NT):
                kb = KB2[:, (i % 2) * 2048:(i % 2) * 2048 + 2048]
                dW = gdma(kb, w1T[128 * i:128 * (i + 1), :],
                          waits=[(cVE, cVE.n), (cSC, cSC.n)])
                ma = sca(lambda E, b=kb: E.activation(MSC[:, 0:2048], b, AF.Abs),
                         waits=[(dGP, dW), (cVE, mxm)])
                if i == 0:
                    mxm = ve(lambda E: E.tensor_copy(F32A[:, 0:2048], MSC[:, 0:2048]),
                             waits=[(cSC, ma)])
                else:
                    mxm = ve(lambda E: E.tensor_tensor(
                        F32A[:, 0:2048], F32A[:, 0:2048], MSC[:, 0:2048], op=ALU.max),
                        waits=[(cSC, ma)])
            width = 64
            while width >= 1:
                dsh = gdma(F32B[0:width, 0:2048], F32A[width:2 * width, 0:2048],
                           waits=[(cVE, mxm)])
                mxm = ve(lambda E, w=width: E.tensor_tensor(
                    F32A[0:w, 0:2048], F32A[0:w, 0:2048], F32B[0:w, 0:2048],
                    op=ALU.max), waits=[(dGP, dsh)])
                width //= 2
            m_ws = ve(lambda E: E.tensor_scalar(ROWS[0:1, 0:2048],
                                                F32A[0:1, 0:2048], 1e-8, 1.0 / 127.0,
                                                op0=ALU.max, op1=ALU.mult),
                      waits=[(dGP, dGP.n)])
            m_wsr = ve(lambda E: E.reciprocal(ROWS[0:1, 2048:4096],
                                              ROWS[0:1, 0:2048]),
                       waits=[(cVE, m_ws)])
            d_ww = gdma(s_row[2:3, 0:2048], ROWS[0:1, 0:2048], waits=[(cVE, m_ws)])
            gdma(PAR[:, P_ws1:P_ws1 + 16],
                 s_row[2, 0:2048].rearrange("(i p) -> p i", p=128),
                 waits=[(dGP, d_ww)], slow=True)
            for j in range(4):
                te(lambda E, j=j: E.matmul(
                    pb[j][:, :], ONER[0:1, :],
                    ROWS[0:1, 2048 + 512 * j:2048 + 512 * (j + 1)],
                    start=True, stop=True),
                   waits=[(cVE, m_wsr), (cVE, m_ones), (cSC, cSC.n)])
            m_wmm = cTE.n
            m_wcp = 0
            for j in range(4):
                m_wcp = ve(lambda E, j=j: E.tensor_copy(
                    F32B[:, 512 * j:512 * (j + 1)], pb[j][:, :]),
                    waits=[(cTE, m_wmm)])
            for i in range(NT):
                kb = KB2[:, (i % 2) * 2048:(i % 2) * 2048 + 2048]
                dW = gdma(kb, w1T[128 * i:128 * (i + 1), :],
                          waits=[(cVE, cVE.n), (dSY, dSY.n)])
                mq1 = ve(lambda E, b=kb: E.tensor_tensor(
                    F32A[:, 0:2048], b, F32B[:, 0:2048], op=ALU.mult),
                    waits=[(dGP, dW), (cVE, m_wcp)])
                mq2 = ve(lambda E: E.tensor_scalar(
                    MSC[:, 0:2048], F32A[:, 0:2048], MAGIC, MAGIC,
                    op0=ALU.add, op1=ALU.subtract))
                sdma(s_wq1[128 * i:128 * (i + 1), :], MSC[:, 0:2048],
                     waits=[(cVE, mq2)])

        for i in range(NT):
            kb = KB2[:, (i % 2) * 2048:(i % 2) * 2048 + 2048]
            dW = gdma(kb, woT[128 * i:128 * (i + 1), :], waits=[(dSY, dSY.n)])
            sdma(s_woT[128 * i:128 * (i + 1), :], kb, waits=[(dGP, dW)])
        m_prep_ve = cVE.n
        m_prep_sc = cSC.n
        m_prep_sy = dSY.n

        sq_marks = []

        def ln_feat(dst_cols, src_col, width, wait_ve=0, wait_x=()):
            sqs = [0, 512]
            mm_sq = {}
            for i in range(NT):
                rsl = R[:, CH * i + src_col:CH * i + src_col + width]
                w_sq = []
                if i >= 2:
                    w_sq = [(cTE, mm_sq[i - 2])]
                msq = ve(lambda E, r=rsl, s=sqs[i % 2]: E.tensor_tensor(
                    F32B[:, s:s + width], r, r, op=ALU.mult),
                    waits=w_sq + (list(wait_x) if i == 0 else []))
                w_mm = []
                if i == 0:
                    w_mm = [(cVE, max(wait_ve, cVE.n)), (cSC, cSC.n)] + list(wait_x)
                te(lambda E, r=rsl, i=i: E.matmul(
                    pb[3][0:1, 0:width], PARB[:, B_mean:B_mean + 1], r,
                    start=(i == 0), stop=(i == NT - 1)), waits=w_mm)
                mm_sq[i] = te(lambda E, s=sqs[i % 2], i=i: E.matmul(
                    pb[4][0:1, 0:width], PAR[:, P_meanf:P_meanf + 1],
                    F32B[:, s:s + width],
                    start=(i == 0), stop=(i == NT - 1)), waits=[(cVE, msq)])
            m_stat = cTE.n
            mvc = ve(lambda E: E.tensor_copy(ROWS[0:1, 1024:1024 + width],
                                             pb[3][0:1, 0:width]),
                     waits=[(cTE, cTE.n)])
            mv = ve(lambda E: E.tensor_tensor(
                ROWS[0:1, 0:width], ROWS[0:1, 1024:1024 + width],
                ROWS[0:1, 1024:1024 + width], op=ALU.mult))
            mv = ve(lambda E: E.tensor_tensor(
                ROWS[0:1, 0:width], pb[4][0:1, 0:width], ROWS[0:1, 0:width],
                op=ALU.subtract))
            ms = sca(lambda E: E.activation(ROWS[0:1, 512:512 + width],
                                            ROWS[0:1, 0:width],
                                            AF.Ln, bias=CAPS[0:1, 0:1]),
                     waits=[(cVE, mv), (cTE, cTE.n)])
            ms = sca(lambda E: E.activation(ROWS[0:1, 512:512 + width],
                                            ROWS[0:1, 512:512 + width],
                                            AF.Exp, scale=CAPS[0:1, 1:2]))
            mn = ve(lambda E: E.tensor_scalar(ROWS[0:1, 1536:1536 + width],
                                              ROWS[0:1, 0:width], 1e-5, None,
                                              op0=ALU.add), waits=[(cSC, ms)])
            mn = ve(lambda E: E.tensor_tensor(
                ROWS[0:1, 1536 + width:1536 + 2 * width] if width <= 256 else
                ROWS[0:1, 0:width],
                ROWS[0:1, 512:512 + width], ROWS[0:1, 512:512 + width],
                op=ALU.mult))
            mn = ve(lambda E: E.tensor_tensor(
                ROWS[0:1, 0:width], ROWS[0:1, 0:width],
                ROWS[0:1, 1536:1536 + width], op=ALU.mult))
            mn = ve(lambda E: E.tensor_scalar(
                ROWS[0:1, 0:width], ROWS[0:1, 0:width], -0.5, 1.5,
                op0=ALU.mult, op1=ALU.add))
            mn = ve(lambda E: E.tensor_tensor(
                ROWS[0:1, 512:512 + width], ROWS[0:1, 512:512 + width],
                ROWS[0:1, 0:width], op=ALU.mult))
            te(lambda E: E.matmul(pb[5][:, 0:width], ONER[0:1, :],
                                  ROWS[0:1, 1024:1024 + width],
                                  start=True, stop=True), waits=[(cVE, mvc)])
            m_bc = te(lambda E: E.matmul(
                pb[6][:, 0:width], ONER[0:1, :],
                ROWS[0:1, 512:512 + width], start=True, stop=True),
                waits=[(cVE, mn)])
            last = 0
            for i in range(NT):
                rsl = R[:, CH * i + src_col:CH * i + src_col + width]
                dsl = dst_cols(i)
                ve(lambda E, r=rsl, d=dsl: E.tensor_tensor(
                    d, r, pb[5][:, 0:width], op=ALU.subtract),
                   waits=[(cTE, m_bc)] if i == 0 else [])
                last = ve(lambda E, d=dsl: E.tensor_tensor(
                    d, d, pb[6][:, 0:width], op=ALU.mult))
            return last

        def mamba_sc(s, first_wait=(), bc_free=0, feeder=None):
            base = SC * s
            rbase = SC * (s % SCPC)
            if s > 0:
                ve(lambda E: E.tensor_copy(
                    U1.rearrange("p (i c) -> p i c", c=UW)[:, :, 5:8],
                    U1.rearrange("p (i c) -> p i c", c=UW)
                    [:, :, UPAD + SC - 3:UPAD + SC]), waits=list(first_wait))
                first_wait = ()
            wx = list(first_wait)
            if s == 0:
                wx += [(cVE, m_ut0)]
            m_u = ln_feat(lambda i: U1[:, i * UW + UPAD:i * UW + UPAD + SC],
                          rbase, SC, wait_x=wx + [(cTE, cTE.n)])
            if stage == 15:
                tap("rows", ROWS[0:1, :], [1, 4096], F32)
                return m_u
            silu_marks = {}
            d_bcw = dGP.n
            for i in range(NT):
                ub = U1[:, i * UW:(i + 1) * UW]
                ac = 2560 + 512 * (i % 2)
                w0_ = []
                if i < 2:
                    w0_ = [(cSC, cSC.n)]
                else:
                    w0_ = [(cSC, silu_marks[i - 2])]
                ve(lambda E, u=ub, a=ac, i=i: E.tensor_scalar(
                    F32A[:, a:a + SC], u[:, 5:5 + SC],
                    PAR[:, P_convw + 4 * i:P_convw + 4 * i + 1], None,
                    op0=ALU.mult), waits=w0_)
                for j in range(1, 4):
                    ve(lambda E, u=ub, a=ac, i=i, j=j: E.scalar_tensor_tensor(
                        F32A[:, a:a + SC], u[:, 5 + j:5 + j + SC],
                        PAR[:, P_convw + 4 * i + j:P_convw + 4 * i + j + 1],
                        F32A[:, a:a + SC], op0=ALU.mult, op1=ALU.add))
                m_cv = cVE.n
                silu_marks[i] = sca(lambda E, i=i, a=ac: E.activation(
                    XC2[:, i * SC:(i + 1) * SC], F32A[:, a:a + SC], AF.Silu,
                    bias=PAR[:, P_convb + i:P_convb + i + 1]),
                    waits=[(cVE, m_cv)])
            m_xc = cSC.n
            if stage == 17:
                return m_xc
            for i in range(NT):
                w = [(cSC, m_xc if i == 0 else 0), (cVE, cVE.n if i == 0 else 0)]
                w = [x for x in w if x[1]]
                te(lambda E, i=i: E.matmul(
                    pb[3][:, :], PARB[:, B_xp + 160 * i:B_xp + 160 * i + 128],
                    XC2[:, i * SC:(i + 1) * SC],
                    start=(i == 0), stop=(i == NT - 1)), waits=w)
                te(lambda E, i=i: E.matmul(
                    pb[4][0:32, :],
                    PARB[:, B_xp + 160 * i + 128:B_xp + 160 * i + 160],
                    XC2[:, i * SC:(i + 1) * SC],
                    start=(i == 0), stop=(i == NT - 1)))
            m_proj = cTE.n
            m_dtin = sca(lambda E: E.copy(MSC[:, 2048:2560], pb[3][:, :]),
                         waits=[(cTE, m_proj), (cVE, cVE.n)])
            m_bcv = sca(lambda E: E.copy(F32A[0:32, 2560:2560 + SC],
                                         pb[4][0:32, :]),
                        waits=[(cTE, m_proj), (cVE, cVE.n)])
            d_bcw = gdma(s_bc[s, :, :], F32A[0:32, 2560:2560 + SC],
                         waits=[(cSC, m_bcv)])
            d_ball = sdma(BC4[:, 0:8192].rearrange("p (n c) -> p n c", c=SC),
                          s_bc[s:s + 1, 0:16, :].broadcast_to([128, 16, SC]),
                          waits=[(dGP, d_bcw), (cVE, bc_free)])
            d_call = sdma(BC4[:, 8192:16384].rearrange("p (n c) -> p n c", c=SC),
                          s_bc[s:s + 1, 16:32, :].broadcast_to([128, 16, SC]),
                          waits=[])
            if stage == 18:
                return cVE.n
            prev_sp = (0, 0)
            m_last = cVE.n
            for i in range(NT):
                dcol = 512 * (i % 2)
                wmt = [(cSC, m_dtin)]
                if i == 0:
                    wmt.append((cVE, m_u))
                if prev_sp[0]:
                    wmt += [(cSC, prev_sp[0]), (cVE, prev_sp[1])]
                mt = te(lambda E, i=i: E.matmul(
                    pb[5][:, :], PARB[:, B_dtp + 128 * i:B_dtp + 128 * (i + 1)],
                    MSC[:, 2048:2560], start=True, stop=True), waits=wmt)
                if stage == 1801:
                    return mt
                ms1 = sca(lambda E, i=i, d=dcol: E.activation(
                    F32A[:, d:d + SC], pb[5][:, :], AF.Abs,
                    bias=PAR[:, P_dtb + i:P_dtb + i + 1]), waits=[(cTE, mt)])
                if stage == 1802:
                    return cVE.n
                sca(lambda E, d=dcol: E.activation(
                    F32A[:, d:d + SC], F32A[:, d:d + SC], AF.Exp, scale=CAPS[:, 2:3]))
                if stage == 1803:
                    return cVE.n
                ms3 = sca(lambda E, d=dcol: E.activation(
                    F32A[:, d:d + SC], F32A[:, d:d + SC], AF.Ln, bias=CAPS[:, 3:4]))
                if stage == 1804:
                    return cVE.n
                mr = ve(lambda E, i=i: E.tensor_scalar(
                    F32B[:, 0:SC], pb[5][:, :],
                    PAR[:, P_dtb + i:P_dtb + i + 1], 0.0,
                    op0=ALU.add, op1=ALU.max), waits=[(cTE, mt), (cSC, ms1)])
                prev_sp = (ms1, mr)
                if stage == 1805:
                    return cVE.n
                mdelta = ve(lambda E, d=dcol: E.tensor_tensor(
                    F32A[:, d:d + SC], F32A[:, d:d + SC], F32B[:, 0:SC],
                    op=ALU.add), waits=[(cSC, ms3)])
                if stage == 181:
                    return mdelta
                mw_w = [(dSY, dSY.n)] if (s == 0 and i == 0) else []
                m_w = ve(lambda E, i=i, d=dcol: E.tensor_tensor(
                    MSC[:, 0:512], F32A[:, d:d + SC],
                    XC2[:, i * SC:(i + 1) * SC], op=ALU.mult), waits=mw_w)
                ve(lambda E, i=i: E.tensor_scalar(
                    MSC[:, 2560:3072],
                    U1[:, i * UW + UPAD:i * UW + UPAD + SC],
                    PAR[:, P_dpar + i:P_dpar + i + 1], None, op0=ALU.mult))
                if stage == 182:
                    return cVE.n
                h_marks = {}
                for n in range(NST):
                    acol = 1024 + 512 * (n % 2)
                    wsa = [(cVE, mdelta)]
                    if n >= 2:
                        wsa.append((cVE, h_marks[n - 2]))
                    sa_ = sca(lambda E, i=i, n=n, a=acol, d=dcol: E.activation(
                        F32A[:, a:a + SC], F32A[:, d:d + SC], AF.Exp,
                        scale=PAR[:, P_aneg + 16 * i + n:P_aneg + 16 * i + n + 1]),
                        waits=wsa)
                    wb = [(dSY, d_ball)] if n == 0 else []
                    ve(lambda E, n=n: E.tensor_tensor(
                        MSC[:, 512:1024], MSC[:, 0:512],
                        BC4[:, 512 * n:512 * (n + 1)], op=ALU.mult), waits=wb)
                    m_h = ve(lambda E, i=i, n=n, a=acol: E.tensor_tensor_scan(
                        MSC[:, 1024:1536], F32A[:, a:a + SC], MSC[:, 512:1024],
                        PAR[:, P_hc + 16 * i + n:P_hc + 16 * i + n + 1],
                        op0=ALU.mult, op1=ALU.add), waits=[(cSC, sa_)])
                    h_marks[n] = m_h
                    if stage == 183:
                        return m_h
                    wc = [(dSY, d_call)] if n == 0 else []
                    ve(lambda E, n=n: E.tensor_tensor(
                        MSC[:, 1536:2048], MSC[:, 1024:1536],
                        BC4[:, 8192 + 512 * n:8192 + 512 * (n + 1)],
                        op=ALU.mult), waits=wc)
                    m_ya = ve(lambda E: E.tensor_tensor(
                        MSC[:, 2560:3072], MSC[:, 2560:3072],
                        MSC[:, 1536:2048], op=ALU.add))
                    ve(lambda E, i=i, n=n: E.tensor_copy(
                        PAR[:, P_hc + 16 * i + n:P_hc + 16 * i + n + 1],
                        MSC[:, 1535:1536]))
                    if stage == 184:
                        return m_ya
                m_last = ve(lambda E, i=i, b=rbase: E.tensor_tensor(
                    R[:, CH * i + b:CH * i + b + SC],
                    R[:, CH * i + b:CH * i + b + SC],
                    MSC[:, 2560:3072], op=ALU.add))
                if stage == 19:
                    return m_last
                if feeder:
                    feeder.pop(0)()
                    if feeder:
                        feeder.pop(0)()
            return m_last

        def ffn0(c, m_u2):
            lastv = 0
            for o in range(NT):
                kb = KB2[:, (o % 2) * 2048:(o % 2) * 2048 + 2048]
                wkb = [(cTE, cTE.n)]
                if c == 0 and o < 2:
                    wkb += [(cVE, m_prep_ve), (cSC, m_prep_sc)]
                dW = sdma(kb.rearrange("p (i q) -> p i q", q=128),
                          s_sign[:, 128 * o:128 * (o + 1)]
                          .rearrange("(i p) q -> p i q", p=128), waits=wkb)
                for half in range(2):
                    bank = pb[half]
                    for i in range(NT):
                        w = []
                        if i == 0:
                            w = [(dSY, dW), (cVE, max(m_u2, lastv)),
                                 (cSC, cSC.n)]
                        te(lambda E, i=i, h=half, b=bank, kb=kb: E.matmul(
                            b[:, :], kb[:, 128 * i:128 * (i + 1)],
                            U23[:, CH * i + 512 * h:CH * i + 512 * (h + 1)],
                            start=(i == 0), stop=(i == NT - 1)), waits=w)
                    m_mm = cTE.n
                    ve(lambda E, o=o, b=bank: E.tensor_scalar(
                        F32B[:, 1024:1536], b[:, :],
                        PAR[:, P_alpha + o:P_alpha + o + 1],
                        PAR[:, P_b0 + o:P_b0 + o + 1],
                        op0=ALU.mult, op1=ALU.add), waits=[(cTE, m_mm)])
                    rsl = R[:, CH * o + 512 * half:CH * o + 512 * half + 512]
                    lastv = ve(lambda E, r=rsl: E.tensor_tensor(
                        r, r, F32B[:, 1024:1536], op=ALU.add))
            return lastv

        def qkv_feeder(c, m_u3):
            base = CH * c
            st = {"ev": {}, "sp": {}, "pend": None,
                  "mmdone": {0: cTE.n, 1: cTE.n}}
            items = []

            def qk_load(wT, o, slot):
                def emit():
                    kb = KB2[:, (o % 2) * 2048:(o % 2) * 2048 + 2048]
                    slot[0] = gdma(kb.rearrange("p (i q) -> p i q", q=128),
                                   wT[:, 128 * o:128 * (o + 1)]
                                   .rearrange("(i p) q -> p i q", p=128),
                                   waits=[(cTE, st["mmdone"].get(o % 2, 0))])
                return emit

            def qk_mm(wT, o, slot):
                def emit():
                    kb = KB2[:, (o % 2) * 2048:(o % 2) * 2048 + 2048]
                    dW = slot[0]
                    mms = []
                    for half in range(2):
                        bk = (2 * o + half) % 3 if True else 0
                        bank = pb[bk]
                        for i in range(NT):
                            w = []
                            if i == 0:
                                w = [(dGP, dW), (cVE, m_u3),
                                     (cSC, st["ev"].get(bk, 0))]
                            te(lambda E, i=i, b=bank, kb=kb, h=half: E.matmul(
                                b[:, :], kb[:, 128 * i:128 * (i + 1)],
                                U23[:, CH * i + 512 * h:CH * i + 512 * (h + 1)],
                                start=(i == 0), stop=(i == NT - 1)), waits=w)
                        mms.append((bk, cTE.n))
                    st["mmdone"][o % 2] = cTE.n
                    return mms

                return emit

            def qk_ev(sdst, o, mms):
                def emit():
                    marks = []
                    for half, (bk, m_mm) in enumerate(mms()):
                        m_ev = sca(lambda E, b=pb[mms()[half][0]], h=half: E.copy(
                            KB2[:, 4096 + 512 * h:4096 + 512 * (h + 1)], b[:, :]),
                            waits=[(cTE, m_mm), (dSY, st["sp"].get("qk", 0))])
                        st["ev"][bk] = m_ev
                        marks.append(m_ev)
                    st["sp"]["qk"] = sdma(
                        sdst[128 * o:128 * (o + 1), base:base + CH],
                        KB2[:, 4096:5120], waits=[(cSC, marks[-1])])
                return emit

            def v_mm(ob, tt, dwv):
                def emit():
                    if tt == 0:
                        dwv[0] = gdma(KB2.rearrange("p (i q) -> p i q", q=512),
                                      wvT[:, 512 * ob:512 * (ob + 1)]
                                      .rearrange("(i p) q -> p i q", p=128),
                                      waits=[(cTE, cTE.n), (dSY, dSY.n),
                                             (cSC, st["ev"].get(tt % 2, 0))])
                    bank = pb[tt % 2]
                    for i in range(NT):
                        w = []
                        if i == 0:
                            w = [(dGP, dwv[0]), (cVE, m_u3),
                                 (cSC, st["ev"].get(tt % 2, 0))]
                        te(lambda E, i=i, tt=tt, b=bank: E.matmul(
                            b[:, :],
                            U23[:, CH * i + 128 * tt:CH * i + 128 * (tt + 1)],
                            KB2[:, 512 * i:512 * (i + 1)],
                            start=(i == 0), stop=(i == NT - 1)), waits=w)
                    return [(tt % 2, cTE.n)]
                return emit

            def v_ev(ob, tt, mms):
                def emit():
                    bk, m_mm = mms()[0]
                    m_ev = sca(lambda E, b=pb[bk]: E.copy(F32B[:, 1536:2048],
                                                          b[:, :]),
                               waits=[(cTE, m_mm), (dGP, st["sp"].get("v", 0))])
                    st["ev"][bk] = m_ev
                    st["sp"]["v"] = gdma(
                        s_v[base + 128 * tt:base + 128 * (tt + 1),
                            512 * ob:512 * (ob + 1)], F32B[:, 1536:2048],
                        waits=[(cSC, m_ev)])
                return emit

            units = []
            for o in range(NT):
                units.append(("qk", wqT, s_qT, o))
            for o in range(NT):
                units.append(("qk", wkT, s_kT, o))
            for ob in range(4):
                dwv = [0]
                for tt in range(8):
                    units.append(("v", ob, tt, dwv))
            pend = [None]
            pre = []
            for u in units:
                if u[0] == "qk":
                    _, wT, sdst, o = u
                    res = {}
                    slot = [0]
                    ld = qk_load(wT, o, slot)
                    mm = qk_mm(wT, o, slot)
                    def item(mm=mm, res=res):
                        res["mms"] = mm()
                    ev = qk_ev(sdst, o, lambda res=res: res["mms"])
                else:
                    _, ob, tt, dwv = u
                    res = {}
                    ld = None
                    mm = v_mm(ob, tt, dwv)
                    def item(mm=mm, res=res):
                        res["mms"] = mm()
                    ev = v_ev(ob, tt, lambda res=res: res["mms"])
                pre.append(ld)
                units[len(pre) - 1] = (item, ev)
            for k, (item, ev) in enumerate(units):
                def combined(item=item, prev_ev=pend[0],
                             nld=(pre[k + 1] if k + 1 < len(pre) else None),
                             first=(pre[k] if k == 0 else None)):
                    if first is not None:
                        first()
                    if nld is not None:
                        nld()
                    if prev_ev is not None:
                        prev_ev()
                    item()
                items.append(combined)
                pend[0] = ev
            def final(prev_ev=pend[0]):
                if prev_ev is not None:
                    prev_ev()
            items.append(final)
            return items

        m_u3_prev = 0
        if stage < 2:
            _finish(P, cGP, dGP, dSY)
            return _close(nc, P)
        m_mamba = 0
        d_rspill = 0
        feeder = []
        for c in range(NCH):
            if c == 0:
                d_xload = d_xload0
            else:
                wR = [(dSY, d_rspill), (cVE, m_u3_prev)]
                d_xload = gdma(R.rearrange("p (i q) -> p i q", q=CH),
                               xT[:, CH * c:CH * (c + 1)]
                               .rearrange("(i p) q -> p i q", p=128), waits=wR)
            for sci in range(SCPC):
                s = SCPC * c + sci
                fw = []
                if sci == 0:
                    fw = [(dGP, d_xload)]
                m_mamba = mamba_sc(s, first_wait=fw, bc_free=m_mamba,
                                   feeder=feeder)
                if stage == 22 and s == 0:
                    tap("hc", PAR[:, :], [128, 768], F32)
                    _finish(P, cGP, dGP, dSY)
                    return _close(nc, P)
                if stage == 23 and s == 1:
                    tap("R_c0", R[:, :], [128, NT * CH], BF16)
                    _finish(P, cGP, dGP, dSY)
                    return _close(nc, P)
                if stage in (15, 17, 18, 1801, 1802, 1803, 1804, 1805, 181, 182, 183, 184, 19, 2):
                    tap("R_sc0", R[:, :], [128, NT * CH], BF16)
                    tap("u_sc0", U1[:, :], [128, NT * UW], BF16)
                    tap("xc_sc0", XC2[:, :], [128, NT * SC], BF16)
                    _finish(P, cGP, dGP, dSY)
                    return _close(nc, P)
            for fn in feeder:
                fn()
            feeder = []
            m_u2 = 0
            for half in range(2):
                m_u2 = ln_feat(lambda i, h=half: U23[:, CH * i + 512 * h:
                                                     CH * i + 512 * h + 512],
                               512 * half, 512,
                               wait_ve=m_mamba, wait_x=[(cTE, cTE.n)])
            m_ffn = ffn0(c, m_u2)
            if stage == 25:
                tap("R_ffn0", R[:, :], [128, NT * CH], BF16)
                _finish(P, cGP, dGP, dSY)
                return _close(nc, P)
            if stage == 24:
                tap("u2", U23[:, :], [128, NT * CH], BF16)
                _finish(P, cGP, dGP, dSY)
                return _close(nc, P)
            m_u3 = 0
            for half in range(2):
                m_u3 = ln_feat(lambda i, h=half: U23[:, CH * i + 512 * h:
                                                     CH * i + 512 * h + 512],
                               512 * half, 512,
                               wait_ve=m_ffn, wait_x=[(cTE, cTE.n)])
            d_rspill = sdma(s_R[:, CH * c:CH * (c + 1)]
                            .rearrange("(i p) q -> p i q", p=128),
                            R.rearrange("p (i q) -> p i q", q=CH),
                            waits=[(cTE, cTE.n), (cVE, m_u3)])
            feeder = qkv_feeder(c, m_u3)
            m_u3_prev = m_u3
        for fn in feeder:
            fn()
        feeder = []
        w1_prep()
        tap("R_A", R[:, :], [128, NT * CH], BF16)
        if stage < 4:
            _finish(P, cGP, dGP, dSY)
            return _close(nc, P)
        tap("qT", None, None) if False else None

        b0pre = {}
        b0q2 = {}
        b0mem = {}
        for h in range(NH):
            cb = (h % 2) * 2048
            hp = (h + 8) % 16
            if h == 0:
                d_kt = sdma(KB2[:, cb:cb + 2048],
                            s_kT[128 * h:128 * (h + 1), :],
                            waits=[(cTE, cTE.n), (cVE, cVE.n)])
                d_kp = sdma(KB2[:, 4096 + cb:4096 + cb + 2048],
                            s_kT[128 * hp:128 * (hp + 1), :], waits=[])
                d_cs = gdma(U23[:, 0:2048], cosTd[128 * h:128 * (h + 1), :],
                            waits=[(cVE, cVE.n), (cTE, cTE.n)])
                d_sn = gdma(U23[:, 2048:4096], sinTd[128 * h:128 * (h + 1), :],
                            waits=[])
                d_vh = sdma(XC2[:, 4096 + cb:4096 + cb + 2048]
                            .rearrange("p (j q) -> p j q", q=128),
                            s_v[:, 128 * h:128 * (h + 1)]
                            .rearrange("(j p) q -> p j q", p=128),
                            waits=[(cTE, cTE.n), (dGP, dGP.n)])
                kw = [(dSY, d_kp), (dGP, d_sn)]
                vw = [(dSY, d_vh)]
            else:
                d_sn, d_vh = b0pre[h]
                kw = [(dGP, d_sn)]
                vw = [(dGP, d_vh)]
            if h + 1 < NH:
                h1 = h + 1
                cb1 = (h1 % 2) * 2048
                hp1 = (h1 + 8) % 16
                wk = ([(cVE, b0q2[h - 1])] if h >= 1 else
                      [(cVE, cVE.n), (cTE, cTE.n), (cSC, cSC.n),
                       (dSY, dSY.n)])
                gdma(KB2[:, cb1:cb1 + 2048],
                     s_kT[128 * h1:128 * (h1 + 1), :], waits=wk)
                gdma(KB2[:, 4096 + cb1:4096 + cb1 + 2048],
                     s_kT[128 * hp1:128 * (hp1 + 1), :], waits=[])
                wv = ([(cTE, b0mem[h - 1])] if h >= 1 else
                      [(cVE, cVE.n), (cTE, cTE.n), (cSC, cSC.n)])
                p_vh = gdma(XC2[:, 4096 + cb1:4096 + cb1 + 2048]
                            .rearrange("p (j q) -> p j q", q=128),
                            s_v[:, 128 * h1:128 * (h1 + 1)]
                            .rearrange("(j p) q -> p j q", p=128),
                            waits=wv, slow=True)
            ve(lambda E, cb=cb: E.tensor_tensor(
                U23[:, 4096:6144], KB2[:, cb:cb + 2048], U23[:, 0:2048],
                op=ALU.mult),
               waits=kw + [(cSC, cSC.n), (cTE, cTE.n)])
            ve(lambda E, cb=cb: E.tensor_tensor(
                MSC[:, 2048:4096], KB2[:, 4096 + cb:4096 + cb + 2048],
                U23[:, 2048:4096], op=ALU.mult))
            mkr = ve(lambda E: E.tensor_tensor(
                U23[:, 4096:6144], U23[:, 4096:6144], MSC[:, 2048:4096],
                op=ALU.add))
            sdma(s_kTr[128 * h:128 * (h + 1), :], U23[:, 4096:6144],
                 waits=[(cVE, mkr)])
            mtr_ev = 0
            for g in range(2):
                mtr = 0
                for jj in range(8):
                    j = 8 * g + jj
                    wtr = [(cVE, mkr)]
                    if jj == 0 and mtr_ev:
                        wtr.append((cSC, mtr_ev))
                    mtr = te(lambda E, j=j, jj=jj: E.transpose(
                        pbt[:, 128 * jj:128 * (jj + 1)],
                        U23[:, 4096 + 128 * j:4096 + 128 * (j + 1)],
                        IDENB[:, :]), waits=wtr)
                mtr_ev = sca(lambda E, g=g, cb=cb: E.copy(
                    XC2[:, cb + 1024 * g:cb + 1024 * (g + 1)], pbt[:, :]),
                    waits=[(cTE, mtr)])
            d_qt = sdma(KB2[:, cb:cb + 2048], s_qT[128 * h:128 * (h + 1), :],
                        waits=[(cVE, cVE.n)])
            d_qp = sdma(KB2[:, 4096 + cb:4096 + cb + 2048],
                        s_qT[128 * hp:128 * (hp + 1), :], waits=[])
            ve(lambda E, cb=cb: E.tensor_tensor(
                U23[:, 6144:8192], KB2[:, cb:cb + 2048], U23[:, 0:2048],
                op=ALU.mult), waits=[(dSY, d_qp)])
            m_q2 = ve(lambda E, cb=cb: E.tensor_tensor(
                MSC[:, 2048:4096], KB2[:, 4096 + cb:4096 + cb + 2048],
                U23[:, 2048:4096], op=ALU.mult))
            b0q2[h] = m_q2
            mqr = ve(lambda E: E.tensor_tensor(
                U23[:, 6144:8192], U23[:, 6144:8192], MSC[:, 2048:4096],
                op=ALU.add))
            sdma(s_qTr[128 * h:128 * (h + 1), :], U23[:, 6144:8192],
                 waits=[(cVE, mqr)])
            if h + 1 < NH:
                gdma(U23[:, 0:2048], cosTd[128 * (h + 1):128 * (h + 2), :],
                     waits=[(cVE, m_q2)])
                p_sn = gdma(U23[:, 2048:4096],
                            sinTd[128 * (h + 1):128 * (h + 2), :], waits=[])
                b0pre[h + 1] = (p_sn, p_vh)
            mv1 = ve(lambda E, cb=cb: E.tensor_scalar_min(
                MSC[:, 2048:4096], XC2[:, cb:cb + 2048], 0.0),
                waits=[(cSC, mtr_ev)])
            ms1 = sca(lambda E: E.activation(MSC[:, 2048:4096], MSC[:, 2048:4096],
                                             AF.Exp), waits=[(cVE, mv1)])
            mv2 = ve(lambda E, cb=cb: E.tensor_relu(MSC[:, 0:2048],
                                                    XC2[:, cb:cb + 2048]))
            mkf = ve(lambda E: E.tensor_tensor(MSC[:, 0:2048], MSC[:, 0:2048],
                                               MSC[:, 2048:4096], op=ALU.add),
                     waits=[(cSC, ms1)])
            for j in range(NT):
                w = []
                if j == 0:
                    w = [(cVE, mkf), (cSC, cSC.n)] + vw
                te(lambda E, j=j, cb=cb: E.matmul(
                    pb[0][:, 0:128], MSC[:, 128 * j:128 * (j + 1)],
                    XC2[:, 4096 + cb + 128 * j:4096 + cb + 128 * (j + 1)],
                    start=(j == 0), stop=(j == NT - 1)), waits=w)
                te(lambda E, j=j: E.matmul(
                    pb[1][0:1, 0:128], PARB[:, B_ones:B_ones + 1],
                    MSC[:, 128 * j:128 * (j + 1)],
                    start=(j == 0), stop=(j == NT - 1)))
            m_mem = cTE.n
            b0mem[h] = m_mem
            sca(lambda E, h=h: E.copy(U1[:, 128 * h:128 * (h + 1)],
                                      pb[0][:, 0:128]),
                waits=[(cTE, m_mem)] + ([(cVE, cVE.n)] if h == 0 else []))
            m_e2 = sca(lambda E, h=h: E.copy(ROWS[0:1, 128 * h:128 * (h + 1)],
                                             pb[1][0:1, 0:128]),
                       waits=[(cTE, m_mem)])
        d_nw = gdma(s_row[3:4, 0:2048], ROWS[0:1, 0:2048], waits=[(cSC, cSC.n)])
        d_norm = gdma(PARB[:, B_norm:B_norm + 16],
                      s_row[3, 0:2048].rearrange("(h p) -> p h", p=128),
                      waits=[(dGP, d_nw)], slow=True)

        tap("memB", U1[:, 0:2048], [128, 2048], BF16)
        if stage < 5:
            _finish(P, cGP, dGP, dSY)
            return _close(nc, P)

        for qcc in range(2):
            qb0 = 1024 * qcc
            wR = [(cVE, cVE.n), (cTE, cTE.n)] if qcc > 0 else [(dSY, d_rspill)]
            d_rload = sdma(R.rearrange("p (i q) -> p i q", q=CH),
                           s_R[:, qb0:qb0 + CH]
                           .rearrange("(i p) q -> p i q", p=128), waits=wR)
            hq = [(qc2, h) for h in range(NH) for qc2 in range(2)]
            lds = {}
            mk = {}

            def issue_loads(j):
                qc2, h = hq[j]
                qb = qb0 + 512 * qc2
                cb = (h % 2) * 2048
                pm = mk.get(j - 2, {})
                if qc2 == 0:
                    pm4 = mk.get(j - 3, {})
                    wkv = ([(cTE, pm4.get("av", 0))] if pm4 else
                           [(cTE, cTE.n)])
                    d_kt = sdma(KB2[:, cb:cb + 2048],
                                s_kTr[128 * h:128 * (h + 1), :], waits=wkv)
                    d_vh = sdma(KB2[:, 4096 + cb:4096 + cb + 2048]
                                .rearrange("p (j q) -> p j q", q=128),
                                s_v[:, 128 * h:128 * (h + 1)]
                                .rearrange("(j p) q -> p j q", p=128),
                                waits=(wkv if pm4 else
                                       [(cTE, cTE.n), (dGP, dGP.n)]))
                    lds[("kv", h)] = (d_kt, d_vh)
                d_kt, d_vh = lds[("kv", h)]
                d_qt = sdma(MSC[:, 512 * (j % 2):512 * (j % 2) + 512],
                            s_qTr[128 * h:128 * (h + 1), qb:qb + 512],
                            waits=([(cTE, pm.get("mo", 0)),
                                    (cVE, pm.get("qdve", 0))] if pm else
                                   [(cTE, cTE.n), (cVE, cVE.n)]))
                lds[j] = (d_kt, d_vh, d_qt)

            issue_loads(0)
            issue_loads(1)
            for j, (qc2, h) in enumerate(hq):
                cb = (h % 2) * 2048
                qsl = MSC[:, 512 * (j % 2):512 * (j % 2) + 512]
                qf = MSC[:, 1536 + 512 * (j % 2):1536 + 512 * (j % 2) + 512]
                d_kt, d_vh, d_qt = lds[j]
                pm = mk.get(j - 1, {})
                m = {}
                mv1 = ve(lambda E, q=qsl: E.tensor_scalar_min(
                    MSC[:, 1024:1536], q, 0.0),
                    waits=[(dSY, d_qt), (cSC, pm.get("me2", 0))])
                ms1 = sca(lambda E: E.activation(
                    MSC[:, 1024:1536], MSC[:, 1024:1536], AF.Exp),
                    waits=[(cVE, mv1)])
                mv2 = ve(lambda E, q=qsl, f=qf: E.tensor_relu(f, q))
                m["qdve"] = mv2
                mqf = ve(lambda E, f=qf: E.tensor_tensor(
                    f, f, MSC[:, 1024:1536], op=ALU.add), waits=[(cSC, ms1)])
                m_exps = []
                mmks = []

                def emit_S(kt):
                    bank = pb[kt % 3]
                    w = []
                    if kt == 0:
                        w = [(dSY, d_kt), (cSC, pm.get("exp15", 0)),
                             (cVE, pm.get("comb", 0))]
                    elif kt in (1, 2):
                        w = [(cSC, pm.get("exp15", 0)),
                             (cVE, pm.get("comb", 0))]
                    else:
                        w = [(cSC, m_exps[kt - 3])]
                    mmk = te(lambda E, kt=kt, b=bank, cb=cb, q=qsl: E.matmul(
                        b[:, :], KB2[:, cb + 128 * kt:cb + 128 * (kt + 1)],
                        q, start=True, stop=True), waits=w)
                    me = sca(lambda E, kt=kt, b=bank: E.activation(
                        XC2[:, 512 * kt:512 * (kt + 1)], b[:, :], AF.Exp,
                        bias=CAPS[:, 4:5], scale=CAPS[:, 5:6]),
                        waits=[(cTE, mmk)])
                    m_exps.append(me)

                for kt in range(3):
                    emit_S(kt)
                for kt in range(NT):
                    if kt + 3 < NT:
                        emit_S(kt + 3)
                    w = [(cSC, m_exps[kt])]
                    if kt == 0:
                        w += [(dSY, d_vh), (cSC, pm.get("me1", 0)),
                              (cVE, pm.get("mr1", 0))]
                    te(lambda E, kt=kt, cb=cb: E.matmul(
                        pb[3][:, :],
                        KB2[:, 4096 + cb + 128 * kt:4096 + cb + 128 * (kt + 1)],
                        XC2[:, 512 * kt:512 * (kt + 1)],
                        start=(kt == 0), stop=(kt == NT - 1)), waits=w)
                    te(lambda E, kt=kt: E.matmul(
                        pb[4][0:1, :], PARB[:, B_ones:B_ones + 1],
                        XC2[:, 512 * kt:512 * (kt + 1)],
                        start=(kt == 0), stop=(kt == NT - 1)))
                m["exp15"] = m_exps[15]
                m["exp14"] = m_exps[14]
                m["av"] = cTE.n
                te(lambda E, h=h, f=qf: E.matmul(
                    pb[5][:, :], U1[:, 128 * h:128 * (h + 1)], f,
                    start=True, stop=True),
                   waits=[(cVE, mqf), (dGP, d_norm), (cSC, pm.get("me2", 0)),
                          (cVE, pm.get("mr2", 0))])
                m_mo = te(lambda E, h=h, f=qf: E.matmul(
                    pb[6][0:1, :], PARB[:, B_norm + h:B_norm + h + 1], f,
                    start=True, stop=True))
                m["mo"] = m_mo
                if j + 2 < len(hq):
                    issue_loads(j + 2)
                mr1 = ve(lambda E: E.reciprocal(ROWS[0:1, 0:512],
                                                pb[4][0:1, :]),
                         waits=[(cTE, m_mo)])
                m["mr1"] = mr1
                mr2 = ve(lambda E: E.reciprocal(ROWS[0:1, 512:1024],
                                                pb[6][0:1, :]))
                m["mr2"] = mr2
                mrb = ve(lambda E, h=h: E.tensor_scalar(
                    ROWS[0:1, 1024:1536], ROWS[0:1, 512:1024],
                    GS[0:1, h:h + 1], None, op0=ALU.mult),
                    waits=[(cTE, pm.get("rep", 0))])
                mrb = ve(lambda E, h=h: E.tensor_scalar(
                    ROWS[0:1, 1536:2048], ROWS[0:1, 0:512],
                    GS[0:1, 16 + h:16 + h + 1], None, op0=ALU.mult),
                    waits=[(cVE, m_g1m)])
                te(lambda E: E.matmul(
                    pb[0][:, :], ONER[0:1, :], ROWS[0:1, 1024:1536],
                    start=True, stop=True),
                   waits=[(cVE, mrb), (cSC, m_exps[15])])
                m["rep"] = te(lambda E: E.matmul(
                    pb[1][:, :], ONER[0:1, :], ROWS[0:1, 1536:2048],
                    start=True, stop=True), waits=[(cSC, m_exps[13])])
                m["sc"] = m["rep"]
                me1 = sca(lambda E: E.copy(MSC[:, 2560:3072], pb[3][:, :]),
                          waits=[(cTE, m["rep"])])
                m["me1"] = me1
                me2 = sca(lambda E: E.copy(MSC[:, 3072:3584], pb[5][:, :]))
                m["me2"] = me2
                mt1 = ve(lambda E: E.tensor_tensor(
                    MSC[:, 3584:4096], MSC[:, 2560:3072], pb[1][:, :],
                    op=ALU.mult), waits=[(cSC, me2), (cTE, m["rep"])])
                ve(lambda E: E.tensor_tensor(
                    MSC[:, 2560:3072], MSC[:, 3072:3584], pb[0][:, :],
                    op=ALU.mult))
                m["comb"] = ve(lambda E, h=h, q=qc2: E.tensor_tensor(
                    BC4[:, 1024 * h + 512 * q:1024 * h + 512 * (q + 1)],
                    MSC[:, 3584:4096], MSC[:, 2560:3072], op=ALU.add))
                mk[j] = m
            m_comb = cVE.n
            lastv = 0
            for o in range(NT):
                kb = KB2[:, (o % 2) * 2048:(o % 2) * 2048 + 2048]
                dW = sdma(kb.rearrange("p (i q) -> p i q", q=128),
                          s_woT[:, 128 * o:128 * (o + 1)]
                          .rearrange("(i p) q -> p i q", p=128),
                          waits=[(cTE, cTE.n)])
                for half in range(2):
                    bank = pb[2 + half]
                    for i in range(NT):
                        w = []
                        if i == 0:
                            w = [(dSY, dW), (cVE, max(m_comb, lastv)),
                                 (cSC, cSC.n)]
                        te(lambda E, i=i, h=half, b=bank, kb=kb: E.matmul(
                            b[:, :], kb[:, 128 * i:128 * (i + 1)],
                            BC4[:, 1024 * i + 512 * h:1024 * i + 512 * (h + 1)],
                            start=(i == 0), stop=(i == NT - 1)), waits=w)
                    m_mm = cTE.n
                    rsl = R[:, CH * o + 512 * half:CH * o + 512 * half + 512]
                    lastv = ve(lambda E, r=rsl, b=bank: E.tensor_tensor(
                        r, r, b[:, :], op=ALU.add),
                        waits=[(cTE, m_mm), (dSY, d_rload)])
            m_wo = lastv
            m_u4 = 0
            for half in range(2):
                m_u4 = ln_feat(lambda i, h=half: U23[:, CH * i + 512 * h:
                                                     CH * i + 512 * h + 512],
                               512 * half, 512,
                               wait_ve=m_wo, wait_x=[(cTE, cTE.n)])
            mx = ve(lambda E: E.tensor_tensor(F32A[:, 2048:3072], U23[:, 0:CH],
                                              U23[:, 0:CH], op=ALU.mult),
                    waits=[(cVE, m_u4)])
            for i in range(1, NT):
                ve(lambda E, i=i: E.tensor_tensor(
                    F32B[:, 1024:2048], U23[:, CH * i:CH * i + CH],
                    U23[:, CH * i:CH * i + CH], op=ALU.mult))
                mx = ve(lambda E: E.tensor_tensor(
                    F32A[:, 2048:3072], F32A[:, 2048:3072], F32B[:, 1024:2048],
                    op=ALU.max))
            width = 64
            while width >= 1:
                dsh = gdma(F32B[0:width, 0:1024], F32A[width:2 * width, 2048:3072],
                           waits=[(cVE, mx)])
                mx = ve(lambda E, w=width: E.tensor_tensor(
                    F32A[0:w, 2048:3072], F32A[0:w, 2048:3072],
                    F32B[0:w, 0:1024], op=ALU.max), waits=[(dGP, dsh)])
                width //= 2
            ms = sca(lambda E: E.activation(ROWS[0:1, 0:1024],
                                            F32A[0:1, 2048:3072], AF.Sqrt),
                     waits=[(cVE, mx), (cTE, cTE.n)])
            mv = ve(lambda E: E.tensor_scalar(ROWS[0:1, 1024:2048],
                                              ROWS[0:1, 0:1024], 1.0 / 127.0,
                                              1e-12, op0=ALU.mult, op1=ALU.max),
                    waits=[(cSC, ms)])
            mvr = ve(lambda E: E.reciprocal(ROWS[0:1, 2048:3072],
                                            ROWS[0:1, 1024:2048]))
            for j in range(2):
                te(lambda E, j=j: E.matmul(
                    pb[3 + j][:, :], ONER[0:1, :],
                    ROWS[0:1, 1024 + 512 * j:1024 + 512 * (j + 1)],
                    start=True, stop=True),
                   waits=[(cVE, mvr), (cSC, cSC.n)] if j == 0 else [])
                te(lambda E, j=j: E.matmul(
                    pb[5 + j][:, :], ONER[0:1, :],
                    ROWS[0:1, 2048 + 512 * j:2048 + 512 * (j + 1)],
                    start=True, stop=True))
            m_reps = cTE.n
            m_xsr = 0
            for j in range(2):
                ve(lambda E, j=j: E.tensor_copy(F32A[:, 512 * j:512 * (j + 1)],
                                                pb[3 + j][:, :]),
                   waits=[(cTE, m_reps)])
                m_xsr = ve(lambda E, j=j: E.tensor_copy(
                    F32A[:, 1024 + 512 * j:1024 + 512 * (j + 1)],
                    pb[5 + j][:, :]))
            m_xq = 0
            for i in range(NT):
                ve(lambda E, i=i: E.scalar_tensor_tensor(
                    F32B[:, 0:1024], U23[:, CH * i:CH * i + CH], 1.0,
                    F32A[:, 1024:2048], op0=ALU.mult, op1=ALU.mult),
                   waits=[(cVE, m_xsr)] if i == 0 else [])
                m_xq = ve(lambda E, i=i: E.tensor_scalar(
                    U23[:, CH * i:CH * i + CH], F32B[:, 0:1024], MAGIC, MAGIC,
                    op0=ALU.add, op1=ALU.subtract))
            tr_marks = {}
            for o in range(NT):
                kb = KB2[:, (o % 2) * 2048:(o % 2) * 2048 + 2048]
                dW = sdma(kb.rearrange("p (i q) -> p i q", q=128),
                          s_wq1[:, 128 * o:128 * (o + 1)]
                          .rearrange("(i p) q -> p i q", p=128),
                          waits=[(cTE, cTE.n)])
                for half in range(2):
                    bank = pb[half]
                    for i in range(NT):
                        w = []
                        if i == 0:
                            w = [(dSY, dW), (cVE, max(m_xq, cVE.n)), (cSC, cSC.n)]
                        te(lambda E, i=i, h=half, b=bank, kb=kb: E.matmul(
                            b[:, :], kb[:, 128 * i:128 * (i + 1)],
                            U23[:, CH * i + 512 * h:CH * i + 512 * (h + 1)],
                            start=(i == 0), stop=(i == NT - 1)), waits=w)
                    m_mm = cTE.n
                    ve(lambda E, o=o, b=bank, h=half: E.scalar_tensor_tensor(
                        F32B[:, 0:512], b[:, :],
                        PAR[:, P_ws1 + o:P_ws1 + o + 1],
                        F32A[:, 512 * h:512 * (h + 1)],
                        op0=ALU.mult, op1=ALU.mult), waits=[(cTE, m_mm)])
                    rsl = R[:, CH * o + 512 * half:CH * o + 512 * half + 512]
                    ve(lambda E, o=o, r=rsl, h=half: E.scalar_tensor_tensor(
                        OUTS[:, 512 * h:512 * (h + 1)], F32B[:, 0:512],
                        PAR[:, P_b1 + o:P_b1 + o + 1], r,
                        op0=ALU.add, op1=ALU.add))
                m_row = cVE.n
                for tt in range(8):
                    bank = pb[3 + tt % 2]
                    w = [(cVE, m_row)]
                    if (o, tt) != (0, 0):
                        pass
                    if tt >= 2 or o > 0:
                        prev = tr_marks.get((o, tt - 2)) or \
                            tr_marks.get((o - 1, tt + 6))
                        if prev:
                            w.append((cGP, prev))
                    mtr = te(lambda E, tt=tt, b=bank: E.transpose(
                        b[:, 0:128], OUTS[:, 128 * tt:128 * (tt + 1)],
                        IDEN[:, :]), waits=w)
                    msc = sca(lambda E, tt=tt, b=bank: E.copy(
                        F32A[:, 3072 + 128 * (tt % 2):3072 + 128 * (tt % 2) + 128],
                        b[:, 0:128]), waits=[(cTE, mtr)])
                    P.op("gp", lambda E, tt=tt, o=o, q=qb0: E.dma_start(
                        out=out[q + 128 * tt:q + 128 * (tt + 1),
                                128 * o:128 * (o + 1)],
                        in_=F32A[:, 3072 + 128 * (tt % 2):
                                 3072 + 128 * (tt % 2) + 128]),
                         waits=[(cSC, msc)], inc=cGP, k=16)
                    tr_marks[(o, tt)] = cGP.n
        _finish(P, cGP, dGP, dSY)
        return _close(nc, P)


def _cols16(v):
    return np.ascontiguousarray(np.asarray(v).reshape(16, 128).T)


def _host_inputs(inp):
    f32 = np.float32
    m = {}
    m["convw"] = np.ascontiguousarray(
        np.asarray(inp["L0_attn_conv_w"]).reshape(16, 128, 4)
        .transpose(1, 0, 2).reshape(128, 64)).astype(f32)
    m["convb"] = _cols16(inp["L0_attn_conv_b"]).astype(f32)
    m["xprojT"] = np.ascontiguousarray(np.asarray(inp["L0_attn_xproj_w"]).T).astype(f32)
    m["dtprojT"] = np.ascontiguousarray(np.asarray(inp["L0_attn_dtproj_w"]).T).astype(f32)
    m["dtprojb"] = _cols16(inp["L0_attn_dtproj_b"]).astype(f32)
    m["alog"] = np.ascontiguousarray(
        np.asarray(inp["L0_attn_A_log"]).reshape(16, 128, 16)
        .transpose(1, 0, 2).reshape(128, 256)).astype(f32)
    m["dpar"] = _cols16(inp["L0_attn_D_param"]).astype(f32)
    m["w0T"] = np.ascontiguousarray(np.asarray(inp["L0_ffn_weight"]).T).astype(f32)
    m["b0c"] = _cols16(inp["L0_ffn_bias"]).astype(f32)
    m["wqT"] = np.ascontiguousarray(np.asarray(inp["L1_attn_Wq"]).T).astype(f32)
    m["wkT"] = np.ascontiguousarray(np.asarray(inp["L1_attn_Wk"]).T).astype(f32)
    m["wvT"] = np.ascontiguousarray(np.asarray(inp["L1_attn_Wv"]).T).astype(f32)
    m["woT"] = np.ascontiguousarray(np.asarray(inp["L1_attn_Wo"]).T).astype(f32)
    m["gate"] = np.ascontiguousarray(np.asarray(inp["L1_attn_gate"]).reshape(1, 16)).astype(f32)
    m["w1T"] = np.ascontiguousarray(np.asarray(inp["L1_ffn_weight"]).T).astype(f32)
    m["b1c"] = _cols16(inp["L1_ffn_bias"]).astype(f32)
    t = np.arange(T, dtype=np.float64)
    invf = 1.0 / (10000.0 ** (np.arange(0, D, 2, dtype=np.float64) / D))
    ang = invf[:, None] * t[None, :]
    m["cosT"] = np.concatenate([np.cos(ang), np.cos(ang)], 0).astype(f32)
    m["sinT"] = np.concatenate([-np.sin(ang), np.sin(ang)], 0).astype(f32)
    m["ident"] = np.eye(128, dtype=f32)
    return m


def kernel(**inputs):
    shared = _host_inputs(inputs)
    x = np.asarray(inputs["L0_attn_x"])
    nc = build_program()
    in_maps = []
    for b in range(8):
        mm = dict(shared)
        mm["xT"] = np.ascontiguousarray(x[b].T).astype(np.float32)
        in_maps.append(mm)
    res = run_bass_kernel_spmd(nc, in_maps, core_ids=list(range(8)))
    return np.stack([np.asarray(r["y"], np.float32) for r in res.results], 0)


# version 15 (final)
# speedup vs baseline: 1.0840x; 1.0046x over previous
import numpy as np
import ml_dtypes
import concourse.bass as bass
import concourse.mybir as mybir
from concourse.bass_utils import run_bass_kernel_spmd
from contextlib import ExitStack

F32 = mybir.dt.float32
BF16 = mybir.dt.bfloat16
AF = mybir.ActivationFunctionType
ALU = mybir.AluOpType

T = 2048; D = 2048; NT = 16; NST = 16; NH = 16
CH = 1024; NCH = T // CH
SC = 512; NSC = T // SC
SCPC = CH // SC
MAGIC = 12582912.0
INVSQ = float(1.0 / np.sqrt(128.0))
SOFTBIAS = -20.0
UPAD = 8
UW = SC + UPAD


class Ctr:
    def __init__(self, name):
        self.name = name
        self.n = 0


class Prog:
    def __init__(self):
        self.items = {e: [] for e in ("gp", "sy", "te", "ve", "sc")}

    def op(self, eng, fn, waits=(), inc=None, k=1):
        wl = [(c.name, v) for (c, v) in waits if v and v > 0]
        tgt = None
        if inc is not None:
            inc.n += k
            tgt = (inc.name, k)

        def run(E, sems):
            for nm, v in wl:
                E.wait_ge(sems[nm], v)
            ins = fn(E)
            if tgt is not None:
                ins.then_inc(sems[tgt[0]], tgt[1])
        self.items[eng].append(run)


def _finish(P, cGP, dGP, dSY):
    P.op("gp", lambda E: None, waits=[(cGP, cGP.n), (dGP, dGP.n)])
    P.op("sy", lambda E: None, waits=[(cGP, cGP.n), (dSY, dSY.n)])


def _close(nc, P):
    with (nc.semaphore("sdGP") as s0, nc.semaphore("sdSY") as s1,
          nc.semaphore("scTE") as s2, nc.semaphore("scVE") as s3,
          nc.semaphore("scSC") as s4, nc.semaphore("scGP") as s5,
          nc.Block() as block):
        sems = {"dGP": s0, "dSY": s1, "cTE": s2, "cVE": s3,
                "cSC": s4, "cGP": s5}

        @block.gpsimd
        def _(E):
            for fn in P.items["gp"]:
                fn(E, sems)

        @block.sync
        def _(E):
            for fn in P.items["sy"]:
                fn(E, sems)

        @block.tensor
        def _(E):
            for fn in P.items["te"]:
                fn(E, sems)

        @block.vector
        def _(E):
            for fn in P.items["ve"]:
                fn(E, sems)

        @block.scalar
        def _(E):
            for fn in P.items["sc"]:
                fn(E, sems)

    return nc


def build_program(taps=(), stage=99):
    nc = bass.Bass("TRN2", target_bir_lowering=False,
                   detect_race_conditions=False)
    P = Prog()
    taps = set(taps)

    def din(name, shape, dt=F32):
        return nc.dram_tensor(name, shape, dt, kind="ExternalInput").ap()

    xT = din("xT", [D, T])
    convw = din("convw", [128, 4 * NT])
    convb = din("convb", [128, NT])
    xprojT_d = din("xprojT", [D, 160])
    dtprojT_d = din("dtprojT", [128, D])
    dtprojb = din("dtprojb", [128, NT])
    alog = din("alog", [128, NST * NT])
    dpar = din("dpar", [128, NT])
    w0T = din("w0T", [D, D])
    b0c = din("b0c", [128, NT])
    wqT = din("wqT", [D, D])
    wkT = din("wkT", [D, D])
    wvT = din("wvT", [D, D])
    woT = din("woT", [D, D])
    gate = din("gate", [1, NH])
    w1T = din("w1T", [D, D])
    b1c = din("b1c", [128, NT])
    cosTd = din("cosT", [D, T])
    sinTd = din("sinT", [D, T])
    identd = din("ident", [128, 128])
    out = nc.dram_tensor("y", [T, D], F32, kind="ExternalOutput").ap()

    def scr(name, shape, dt):
        return nc.dram_tensor(name, shape, dt, kind="Internal").ap()

    s_bc = scr("s_bc", [NSC, 32, SC], BF16)
    s_sign = scr("s_sign", [D, D], BF16)
    s_wq1 = scr("s_wq1", [D, D], BF16)
    s_woT = scr("s_woT", [D, D], BF16)
    s_qT = scr("s_qT", [D, T], BF16)
    s_kT = scr("s_kT", [D, T], BF16)
    s_v = scr("s_v", [T, D], BF16)
    s_row = scr("s_row", [8, 2048], F32)
    s_R = scr("s_R", [D, T], BF16)
    s_qTr = scr("s_qTr", [D, T], BF16)
    s_kTr = scr("s_kTr", [D, T], BF16)

    dGP = Ctr("dGP"); dSY = Ctr("dSY")
    cTE = Ctr("cTE"); cVE = Ctr("cVE"); cSC = Ctr("cSC"); cGP = Ctr("cGP")

    tap_info = {}

    with ExitStack() as ctx:
        def sbuf(name, shape, dt):
            return ctx.enter_context(nc.sbuf_tensor(name, shape, dt))

        R = sbuf("R", [128, NT * CH], BF16)
        U1 = sbuf("U1", [128, NT * UW], BF16)
        U23 = sbuf("U23", [128, NT * CH], BF16)
        BC4 = sbuf("BC4", [128, 16384], BF16)
        XC2 = sbuf("XC2", [128, NT * SC], BF16)
        KB2 = sbuf("KB2", [128, 8192], BF16)
        MSC = sbuf("MSC", [128, 4096], BF16)
        F32A = sbuf("F32A", [128, 3584], F32)
        F32B = sbuf("F32B", [128, 2048], F32)
        PAR = sbuf("PAR", [128, 768], F32)
        PARB = sbuf("PARB", [128, 4640], BF16)
        ROWS = sbuf("ROWS", [1, 4096], F32)
        OUTS = sbuf("OUTS", [128, 1024], F32)
        IDEN = sbuf("IDEN", [128, 128], F32)
        ONER = sbuf("ONER", [1, 128], F32)
        IDENB = sbuf("IDENB", [128, 128], BF16)
        GS = sbuf("GS", [1, 32], F32)
        CAPS = sbuf("CAPS", [128, 8], F32)

        PS3 = ctx.enter_context(nc.psum_tensor("PS3", [128, 1536], F32))
        pb = [PS3[:, 512 * i:512 * (i + 1)] for i in range(3)]
        pb += [ctx.enter_context(nc.psum_tensor(f"pb{i}", [128, 512], F32))
               for i in range(3, 7)]
        pbt = ctx.enter_context(nc.psum_tensor("pbt", [128, 1024], BF16))

        P_convw = 0; P_convb = 64; P_dtb = 80; P_dpar = 96
        P_b0 = 112; P_b1 = 128; P_alpha = 144; P_ws1 = 160
        P_aneg = 176
        P_hc = 432
        P_meanf = 688
        B_dtp = 0
        B_xp = 2048
        B_ones = 4608
        B_mean = 4610
        B_norm = 4612


        def gdma(dst, src, waits=(), slow=False):
            P.op("gp", lambda E, d=dst, s=src, sl=slow: E.dma_start(
                out=d, in_=s, allow_slow_non_contiguous=sl),
                 waits=waits, inc=dGP, k=16)
            return dGP.n

        def sdma(dst, src, waits=(), transpose=False):
            P.op("sy", lambda E, d=dst, s=src, t=transpose:
                 E.dma_start(out=d, in_=s, transpose=t),
                 waits=waits, inc=dSY, k=16)
            return dSY.n

        def te(fn, waits=()):
            P.op("te", fn, waits=waits, inc=cTE)
            return cTE.n

        def ve(fn, waits=()):
            P.op("ve", fn, waits=waits, inc=cVE)
            return cVE.n

        def sca(fn, waits=()):
            P.op("sc", fn, waits=waits, inc=cSC)
            return cSC.n

        def tap(name, ap, shape, dt=F32):
            if name not in taps:
                return
            t = nc.dram_tensor("tap_" + name, shape, dt, kind="ExternalOutput").ap()
            tap_info[name] = True
            w = [(cTE, cTE.n), (cVE, cVE.n), (cSC, cSC.n), (dSY, dSY.n)]
            gdma(t, ap, waits=w)

        gdma(PARB[:, B_dtp:B_dtp + 2048], dtprojT_d[:, :])
        gdma(PARB[:, B_xp:B_xp + 2560].rearrange("p (i q) -> p i q", q=160),
             xprojT_d.rearrange("(i p) q -> p i q", p=128))
        gdma(PAR[:, P_convw:P_convw + 64], convw[:, :])
        gdma(PAR[:, P_convb:P_convb + 16], convb[:, :])
        gdma(PAR[:, P_dtb:P_dtb + 16], dtprojb[:, :])
        gdma(PAR[:, P_dpar:P_dpar + 16], dpar[:, :])
        gdma(PAR[:, P_b0:P_b0 + 16], b0c[:, :])
        gdma(PAR[:, P_b1:P_b1 + 16], b1c[:, :])
        gdma(PAR[:, P_aneg:P_aneg + 256], alog[:, :])
        gdma(IDEN[:, :], identd[:, :])
        gdma(IDENB[:, :], identd[:, :])
        gdma(ROWS[0:1, 0:16], gate[:, :])
        d_par = dGP.n
        sca(lambda E: E.activation(PAR[:, P_aneg:P_aneg + 256],
                                   PAR[:, P_aneg:P_aneg + 256], AF.Exp),
            waits=[(dGP, d_par)])
        ve(lambda E: E.tensor_scalar_mul(PAR[:, P_aneg:P_aneg + 256],
                                         PAR[:, P_aneg:P_aneg + 256], -1.0),
           waits=[(cSC, cSC.n)])
        ve(lambda E: E.memset(PARB[:, B_ones:B_ones + 1], 1.0))
        ve(lambda E: E.memset(ONER[0:1, :], 1.0))
        ve(lambda E: E.memset(PAR[:, P_meanf:P_meanf + 1], 1.0 / 2048.0))
        ve(lambda E: E.memset(CAPS[:, 0:1], 1e-5))
        ve(lambda E: E.memset(CAPS[:, 1:2], -0.5))
        ve(lambda E: E.memset(CAPS[:, 2:3], -1.0))
        ve(lambda E: E.memset(CAPS[:, 3:4], 1.0))
        ve(lambda E: E.memset(CAPS[:, 4:5], SOFTBIAS))
        ve(lambda E: E.memset(CAPS[:, 5:6], INVSQ))
        m_ones = ve(lambda E: E.memset(PARB[:, B_mean:B_mean + 1], 1.0 / 2048.0))
        ve(lambda E: E.memset(PAR[:, P_hc:P_hc + 256], 0.0))
        m_ut0 = ve(lambda E: E.memset(U1[:, :], 0.0))
        m_gs = sca(lambda E: E.activation(GS[0:1, 0:16], ROWS[0:1, 0:16],
                                          AF.Sigmoid), waits=[(dGP, d_par)])
        m_g1m = ve(lambda E: E.tensor_scalar(GS[0:1, 16:32], GS[0:1, 0:16],
                                             -1.0, 1.0, op0=ALU.mult, op1=ALU.add),
                   waits=[(cSC, m_gs)])

        d_xload0 = gdma(R.rearrange("p (i q) -> p i q", q=CH),
                        xT[:, 0:CH].rearrange("(i p) q -> p i q", p=128),
                        waits=[])

        for i in range(NT):
            kb = KB2[:, (i % 2) * 2048:(i % 2) * 2048 + 2048]
            dW = gdma(kb, w0T[128 * i:128 * (i + 1), :],
                      waits=[(cVE, cVE.n), (cSC, cSC.n)])
            m1 = ve(lambda E, b=kb: E.tensor_scalar(
                MSC[:, 0:2048], b, 0.0, None, op0=ALU.is_ge),
                waits=[(dGP, dW), (dSY, dSY.n)])
            m2 = ve(lambda E: E.tensor_scalar(
                MSC[:, 0:2048], MSC[:, 0:2048], 2.0, -1.0,
                op0=ALU.mult, op1=ALU.add))
            sdma(s_sign[128 * i:128 * (i + 1), :], MSC[:, 0:2048],
                 waits=[(cVE, m2)])
            ma = sca(lambda E, b=kb: E.activation(MSC[:, 2048:4096], b, AF.Abs),
                     waits=[(dGP, dW), (cTE, cTE.n)])
            for j in range(4):
                te(lambda E, j=j, i=i: E.matmul(
                    pb[3 + j][0:1, :], PARB[:, B_mean:B_mean + 1],
                    MSC[:, 2048 + 512 * j:2048 + 512 * (j + 1)],
                    start=(i == 0), stop=(i == NT - 1)),
                   waits=[(cSC, ma), (cVE, m_ones)])
        m_amm = cTE.n
        m_acp = 0
        for j in range(4):
            m_acp = sca(lambda E, j=j: E.copy(ROWS[0:1, 512 * j:512 * (j + 1)],
                                              pb[3 + j][0:1, :]),
                        waits=[(cTE, m_amm)])
        d_aw = gdma(s_row[1:2, 0:2048], ROWS[0:1, 0:2048], waits=[(cSC, m_acp)])
        gdma(PAR[:, P_alpha:P_alpha + 16],
             s_row[1, 0:2048].rearrange("(i p) -> p i", p=128),
             waits=[(dGP, d_aw)], slow=True)

        def w1_prep():
            mxm = 0
            for i in range(NT):
                kb = KB2[:, (i % 2) * 2048:(i % 2) * 2048 + 2048]
                dW = gdma(kb, w1T[128 * i:128 * (i + 1), :],
                          waits=[(cVE, cVE.n), (cSC, cSC.n)])
                ma = sca(lambda E, b=kb: E.activation(MSC[:, 0:2048], b, AF.Abs),
                         waits=[(dGP, dW), (cVE, mxm)])
                if i == 0:
                    mxm = ve(lambda E: E.tensor_copy(F32A[:, 0:2048], MSC[:, 0:2048]),
                             waits=[(cSC, ma)])
                else:
                    mxm = ve(lambda E: E.tensor_tensor(
                        F32A[:, 0:2048], F32A[:, 0:2048], MSC[:, 0:2048], op=ALU.max),
                        waits=[(cSC, ma)])
            width = 64
            while width >= 1:
                dsh = gdma(F32B[0:width, 0:2048], F32A[width:2 * width, 0:2048],
                           waits=[(cVE, mxm)])
                mxm = ve(lambda E, w=width: E.tensor_tensor(
                    F32A[0:w, 0:2048], F32A[0:w, 0:2048], F32B[0:w, 0:2048],
                    op=ALU.max), waits=[(dGP, dsh)])
                width //= 2
            m_ws = ve(lambda E: E.tensor_scalar(ROWS[0:1, 0:2048],
                                                F32A[0:1, 0:2048], 1e-8, 1.0 / 127.0,
                                                op0=ALU.max, op1=ALU.mult),
                      waits=[(dGP, dGP.n)])
            m_wsr = ve(lambda E: E.reciprocal(ROWS[0:1, 2048:4096],
                                              ROWS[0:1, 0:2048]),
                       waits=[(cVE, m_ws)])
            d_ww = gdma(s_row[2:3, 0:2048], ROWS[0:1, 0:2048], waits=[(cVE, m_ws)])
            gdma(PAR[:, P_ws1:P_ws1 + 16],
                 s_row[2, 0:2048].rearrange("(i p) -> p i", p=128),
                 waits=[(dGP, d_ww)], slow=True)
            for j in range(4):
                te(lambda E, j=j: E.matmul(
                    pb[j][:, :], ONER[0:1, :],
                    ROWS[0:1, 2048 + 512 * j:2048 + 512 * (j + 1)],
                    start=True, stop=True),
                   waits=[(cVE, m_wsr), (cVE, m_ones), (cSC, cSC.n)])
            m_wmm = cTE.n
            m_wcp = 0
            for j in range(4):
                m_wcp = ve(lambda E, j=j: E.tensor_copy(
                    F32B[:, 512 * j:512 * (j + 1)], pb[j][:, :]),
                    waits=[(cTE, m_wmm)])
            for i in range(NT):
                kb = KB2[:, (i % 2) * 2048:(i % 2) * 2048 + 2048]
                dW = gdma(kb, w1T[128 * i:128 * (i + 1), :],
                          waits=[(cVE, cVE.n), (dSY, dSY.n)])
                mq1 = ve(lambda E, b=kb: E.tensor_tensor(
                    F32A[:, 0:2048], b, F32B[:, 0:2048], op=ALU.mult),
                    waits=[(dGP, dW), (cVE, m_wcp)])
                mq2 = ve(lambda E: E.tensor_scalar(
                    MSC[:, 0:2048], F32A[:, 0:2048], MAGIC, MAGIC,
                    op0=ALU.add, op1=ALU.subtract))
                sdma(s_wq1[128 * i:128 * (i + 1), :], MSC[:, 0:2048],
                     waits=[(cVE, mq2)])

        for i in range(NT):
            kb = KB2[:, (i % 2) * 2048:(i % 2) * 2048 + 2048]
            dW = gdma(kb, woT[128 * i:128 * (i + 1), :], waits=[(dSY, dSY.n)])
            sdma(s_woT[128 * i:128 * (i + 1), :], kb, waits=[(dGP, dW)])
        m_prep_ve = cVE.n
        m_prep_sc = cSC.n
        m_prep_sy = dSY.n

        sq_marks = []

        def ln_feat(dst_cols, src_col, width, wait_ve=0, wait_x=()):
            sqs = [0, 512]
            mm_sq = {}
            for i in range(NT):
                rsl = R[:, CH * i + src_col:CH * i + src_col + width]
                w_sq = []
                if i >= 2:
                    w_sq = [(cTE, mm_sq[i - 2])]
                msq = ve(lambda E, r=rsl, s=sqs[i % 2]: E.tensor_tensor(
                    F32B[:, s:s + width], r, r, op=ALU.mult),
                    waits=w_sq + (list(wait_x) if i == 0 else []))
                w_mm = []
                if i == 0:
                    w_mm = [(cVE, max(wait_ve, cVE.n)), (cSC, cSC.n)] + list(wait_x)
                te(lambda E, r=rsl, i=i: E.matmul(
                    pb[3][0:1, 0:width], PARB[:, B_mean:B_mean + 1], r,
                    start=(i == 0), stop=(i == NT - 1)), waits=w_mm)
                mm_sq[i] = te(lambda E, s=sqs[i % 2], i=i: E.matmul(
                    pb[4][0:1, 0:width], PAR[:, P_meanf:P_meanf + 1],
                    F32B[:, s:s + width],
                    start=(i == 0), stop=(i == NT - 1)), waits=[(cVE, msq)])
            m_stat = cTE.n
            mvc = ve(lambda E: E.tensor_copy(ROWS[0:1, 1024:1024 + width],
                                             pb[3][0:1, 0:width]),
                     waits=[(cTE, cTE.n)])
            mv = ve(lambda E: E.tensor_tensor(
                ROWS[0:1, 0:width], ROWS[0:1, 1024:1024 + width],
                ROWS[0:1, 1024:1024 + width], op=ALU.mult))
            mv = ve(lambda E: E.tensor_tensor(
                ROWS[0:1, 0:width], pb[4][0:1, 0:width], ROWS[0:1, 0:width],
                op=ALU.subtract))
            ms = sca(lambda E: E.activation(ROWS[0:1, 512:512 + width],
                                            ROWS[0:1, 0:width],
                                            AF.Ln, bias=CAPS[0:1, 0:1]),
                     waits=[(cVE, mv), (cTE, cTE.n)])
            ms = sca(lambda E: E.activation(ROWS[0:1, 512:512 + width],
                                            ROWS[0:1, 512:512 + width],
                                            AF.Exp, scale=CAPS[0:1, 1:2]))
            mn = ve(lambda E: E.tensor_scalar(ROWS[0:1, 1536:1536 + width],
                                              ROWS[0:1, 0:width], 1e-5, None,
                                              op0=ALU.add), waits=[(cSC, ms)])
            mn = ve(lambda E: E.tensor_tensor(
                ROWS[0:1, 1536 + width:1536 + 2 * width] if width <= 256 else
                ROWS[0:1, 0:width],
                ROWS[0:1, 512:512 + width], ROWS[0:1, 512:512 + width],
                op=ALU.mult))
            mn = ve(lambda E: E.tensor_tensor(
                ROWS[0:1, 0:width], ROWS[0:1, 0:width],
                ROWS[0:1, 1536:1536 + width], op=ALU.mult))
            mn = ve(lambda E: E.tensor_scalar(
                ROWS[0:1, 0:width], ROWS[0:1, 0:width], -0.5, 1.5,
                op0=ALU.mult, op1=ALU.add))
            mn = ve(lambda E: E.tensor_tensor(
                ROWS[0:1, 512:512 + width], ROWS[0:1, 512:512 + width],
                ROWS[0:1, 0:width], op=ALU.mult))
            te(lambda E: E.matmul(pb[5][:, 0:width], ONER[0:1, :],
                                  ROWS[0:1, 1024:1024 + width],
                                  start=True, stop=True), waits=[(cVE, mvc)])
            m_bc = te(lambda E: E.matmul(
                pb[6][:, 0:width], ONER[0:1, :],
                ROWS[0:1, 512:512 + width], start=True, stop=True),
                waits=[(cVE, mn)])
            last = 0
            for i in range(NT):
                rsl = R[:, CH * i + src_col:CH * i + src_col + width]
                dsl = dst_cols(i)
                ve(lambda E, r=rsl, d=dsl: E.tensor_tensor(
                    d, r, pb[5][:, 0:width], op=ALU.subtract),
                   waits=[(cTE, m_bc)] if i == 0 else [])
                last = ve(lambda E, d=dsl: E.tensor_tensor(
                    d, d, pb[6][:, 0:width], op=ALU.mult))
            return last

        def mamba_sc(s, first_wait=(), bc_free=0, feeder=None):
            base = SC * s
            rbase = SC * (s % SCPC)
            if s > 0:
                ve(lambda E: E.tensor_copy(
                    U1.rearrange("p (i c) -> p i c", c=UW)[:, :, 5:8],
                    U1.rearrange("p (i c) -> p i c", c=UW)
                    [:, :, UPAD + SC - 3:UPAD + SC]), waits=list(first_wait))
                first_wait = ()
            wx = list(first_wait)
            if s == 0:
                wx += [(cVE, m_ut0)]
            m_u = ln_feat(lambda i: U1[:, i * UW + UPAD:i * UW + UPAD + SC],
                          rbase, SC, wait_x=wx + [(cTE, cTE.n)])
            if stage == 15:
                tap("rows", ROWS[0:1, :], [1, 4096], F32)
                return m_u
            silu_marks = {}
            d_bcw = dGP.n
            for i in range(NT):
                ub = U1[:, i * UW:(i + 1) * UW]
                ac = 2560 + 512 * (i % 2)
                w0_ = []
                if i < 2:
                    w0_ = [(cSC, cSC.n)]
                else:
                    w0_ = [(cSC, silu_marks[i - 2])]
                ve(lambda E, u=ub, a=ac, i=i: E.tensor_scalar(
                    F32A[:, a:a + SC], u[:, 5:5 + SC],
                    PAR[:, P_convw + 4 * i:P_convw + 4 * i + 1], None,
                    op0=ALU.mult), waits=w0_)
                for j in range(1, 4):
                    ve(lambda E, u=ub, a=ac, i=i, j=j: E.scalar_tensor_tensor(
                        F32A[:, a:a + SC], u[:, 5 + j:5 + j + SC],
                        PAR[:, P_convw + 4 * i + j:P_convw + 4 * i + j + 1],
                        F32A[:, a:a + SC], op0=ALU.mult, op1=ALU.add))
                m_cv = cVE.n
                silu_marks[i] = sca(lambda E, i=i, a=ac: E.activation(
                    XC2[:, i * SC:(i + 1) * SC], F32A[:, a:a + SC], AF.Silu,
                    bias=PAR[:, P_convb + i:P_convb + i + 1]),
                    waits=[(cVE, m_cv)])
            m_xc = cSC.n
            if stage == 17:
                return m_xc
            for i in range(NT):
                w = [(cSC, m_xc if i == 0 else 0), (cVE, cVE.n if i == 0 else 0)]
                w = [x for x in w if x[1]]
                te(lambda E, i=i: E.matmul(
                    pb[3][:, :], PARB[:, B_xp + 160 * i:B_xp + 160 * i + 128],
                    XC2[:, i * SC:(i + 1) * SC],
                    start=(i == 0), stop=(i == NT - 1)), waits=w)
                te(lambda E, i=i: E.matmul(
                    pb[4][0:32, :],
                    PARB[:, B_xp + 160 * i + 128:B_xp + 160 * i + 160],
                    XC2[:, i * SC:(i + 1) * SC],
                    start=(i == 0), stop=(i == NT - 1)))
            m_proj = cTE.n
            m_dtin = sca(lambda E: E.copy(MSC[:, 2048:2560], pb[3][:, :]),
                         waits=[(cTE, m_proj), (cVE, cVE.n)])
            m_bcv = sca(lambda E: E.copy(F32A[0:32, 2560:2560 + SC],
                                         pb[4][0:32, :]),
                        waits=[(cTE, m_proj), (cVE, cVE.n)])
            d_bcw = gdma(s_bc[s, :, :], F32A[0:32, 2560:2560 + SC],
                         waits=[(cSC, m_bcv)])
            d_ball = sdma(BC4[:, 0:8192].rearrange("p (n c) -> p n c", c=SC),
                          s_bc[s:s + 1, 0:16, :].broadcast_to([128, 16, SC]),
                          waits=[(dGP, d_bcw), (cVE, bc_free)])
            d_call = sdma(BC4[:, 8192:16384].rearrange("p (n c) -> p n c", c=SC),
                          s_bc[s:s + 1, 16:32, :].broadcast_to([128, 16, SC]),
                          waits=[])
            if stage == 18:
                return cVE.n
            prev_sp = (0, 0)
            m_last = cVE.n
            for i in range(NT):
                dcol = 512 * (i % 2)
                wmt = [(cSC, m_dtin)]
                if i == 0:
                    wmt.append((cVE, m_u))
                if prev_sp[0]:
                    wmt += [(cSC, prev_sp[0]), (cVE, prev_sp[1])]
                mt = te(lambda E, i=i: E.matmul(
                    pb[5][:, :], PARB[:, B_dtp + 128 * i:B_dtp + 128 * (i + 1)],
                    MSC[:, 2048:2560], start=True, stop=True), waits=wmt)
                if stage == 1801:
                    return mt
                ms1 = sca(lambda E, i=i, d=dcol: E.activation(
                    F32A[:, d:d + SC], pb[5][:, :], AF.Abs,
                    bias=PAR[:, P_dtb + i:P_dtb + i + 1]), waits=[(cTE, mt)])
                if stage == 1802:
                    return cVE.n
                sca(lambda E, d=dcol: E.activation(
                    F32A[:, d:d + SC], F32A[:, d:d + SC], AF.Exp, scale=CAPS[:, 2:3]))
                if stage == 1803:
                    return cVE.n
                ms3 = sca(lambda E, d=dcol: E.activation(
                    F32A[:, d:d + SC], F32A[:, d:d + SC], AF.Ln, bias=CAPS[:, 3:4]))
                if stage == 1804:
                    return cVE.n
                mr = ve(lambda E, i=i: E.tensor_scalar(
                    F32B[:, 0:SC], pb[5][:, :],
                    PAR[:, P_dtb + i:P_dtb + i + 1], 0.0,
                    op0=ALU.add, op1=ALU.max), waits=[(cTE, mt), (cSC, ms1)])
                prev_sp = (ms1, mr)
                if stage == 1805:
                    return cVE.n
                mdelta = ve(lambda E, d=dcol: E.tensor_tensor(
                    F32A[:, d:d + SC], F32A[:, d:d + SC], F32B[:, 0:SC],
                    op=ALU.add), waits=[(cSC, ms3)])
                if stage == 181:
                    return mdelta
                mw_w = [(dSY, dSY.n)] if (s == 0 and i == 0) else []
                m_w = ve(lambda E, i=i, d=dcol: E.tensor_tensor(
                    MSC[:, 0:512], F32A[:, d:d + SC],
                    XC2[:, i * SC:(i + 1) * SC], op=ALU.mult), waits=mw_w)
                ve(lambda E, i=i: E.tensor_scalar(
                    MSC[:, 2560:3072],
                    U1[:, i * UW + UPAD:i * UW + UPAD + SC],
                    PAR[:, P_dpar + i:P_dpar + i + 1], None, op0=ALU.mult))
                if stage == 182:
                    return cVE.n
                h_marks = {}
                for n in range(NST):
                    acol = 1024 + 512 * (n % 2)
                    wsa = [(cVE, mdelta)]
                    if n >= 2:
                        wsa.append((cVE, h_marks[n - 2]))
                    sa_ = sca(lambda E, i=i, n=n, a=acol, d=dcol: E.activation(
                        F32A[:, a:a + SC], F32A[:, d:d + SC], AF.Exp,
                        scale=PAR[:, P_aneg + 16 * i + n:P_aneg + 16 * i + n + 1]),
                        waits=wsa)
                    wb = [(dSY, d_ball)] if n == 0 else []
                    ve(lambda E, n=n: E.tensor_tensor(
                        MSC[:, 512:1024], MSC[:, 0:512],
                        BC4[:, 512 * n:512 * (n + 1)], op=ALU.mult), waits=wb)
                    m_h = ve(lambda E, i=i, n=n, a=acol: E.tensor_tensor_scan(
                        MSC[:, 1024:1536], F32A[:, a:a + SC], MSC[:, 512:1024],
                        PAR[:, P_hc + 16 * i + n:P_hc + 16 * i + n + 1],
                        op0=ALU.mult, op1=ALU.add), waits=[(cSC, sa_)])
                    h_marks[n] = m_h
                    if stage == 183:
                        return m_h
                    wc = [(dSY, d_call)] if n == 0 else []
                    ve(lambda E, n=n: E.tensor_tensor(
                        MSC[:, 1536:2048], MSC[:, 1024:1536],
                        BC4[:, 8192 + 512 * n:8192 + 512 * (n + 1)],
                        op=ALU.mult), waits=wc)
                    m_ya = ve(lambda E: E.tensor_tensor(
                        MSC[:, 2560:3072], MSC[:, 2560:3072],
                        MSC[:, 1536:2048], op=ALU.add))
                    ve(lambda E, i=i, n=n: E.tensor_copy(
                        PAR[:, P_hc + 16 * i + n:P_hc + 16 * i + n + 1],
                        MSC[:, 1535:1536]))
                    if stage == 184:
                        return m_ya
                m_last = ve(lambda E, i=i, b=rbase: E.tensor_tensor(
                    R[:, CH * i + b:CH * i + b + SC],
                    R[:, CH * i + b:CH * i + b + SC],
                    MSC[:, 2560:3072], op=ALU.add))
                if stage == 19:
                    return m_last
                if feeder:
                    feeder.pop(0)()
                    if feeder:
                        feeder.pop(0)()
            return m_last

        def ffn0(c, m_u2):
            lastv = 0
            for o in range(NT):
                kb = KB2[:, (o % 2) * 2048:(o % 2) * 2048 + 2048]
                wkb = [(cTE, cTE.n)]
                if c == 0 and o < 2:
                    wkb += [(cVE, m_prep_ve), (cSC, m_prep_sc)]
                dW = sdma(kb.rearrange("p (i q) -> p i q", q=128),
                          s_sign[:, 128 * o:128 * (o + 1)]
                          .rearrange("(i p) q -> p i q", p=128), waits=wkb)
                for half in range(2):
                    bank = pb[half]
                    for i in range(NT):
                        w = []
                        if i == 0:
                            w = [(dSY, dW), (cVE, max(m_u2, lastv)),
                                 (cSC, cSC.n)]
                        te(lambda E, i=i, h=half, b=bank, kb=kb: E.matmul(
                            b[:, :], kb[:, 128 * i:128 * (i + 1)],
                            U23[:, CH * i + 512 * h:CH * i + 512 * (h + 1)],
                            start=(i == 0), stop=(i == NT - 1)), waits=w)
                    m_mm = cTE.n
                    ve(lambda E, o=o, b=bank: E.tensor_scalar(
                        F32B[:, 1024:1536], b[:, :],
                        PAR[:, P_alpha + o:P_alpha + o + 1],
                        PAR[:, P_b0 + o:P_b0 + o + 1],
                        op0=ALU.mult, op1=ALU.add), waits=[(cTE, m_mm)])
                    rsl = R[:, CH * o + 512 * half:CH * o + 512 * half + 512]
                    lastv = ve(lambda E, r=rsl: E.tensor_tensor(
                        r, r, F32B[:, 1024:1536], op=ALU.add))
            return lastv

        def qkv_feeder(c, m_u3):
            base = CH * c
            st = {"ev": {}, "sp": {}, "pend": None,
                  "mmdone": {0: cTE.n, 1: cTE.n}}
            items = []

            def qk_load(wT, o, slot):
                def emit():
                    kb = KB2[:, (o % 2) * 2048:(o % 2) * 2048 + 2048]
                    slot[0] = gdma(kb.rearrange("p (i q) -> p i q", q=128),
                                   wT[:, 128 * o:128 * (o + 1)]
                                   .rearrange("(i p) q -> p i q", p=128),
                                   waits=[(cTE, st["mmdone"].get(o % 2, 0))])
                return emit

            def qk_mm(wT, o, slot):
                def emit():
                    kb = KB2[:, (o % 2) * 2048:(o % 2) * 2048 + 2048]
                    dW = slot[0]
                    mms = []
                    for half in range(2):
                        bk = (2 * o + half) % 3 if True else 0
                        bank = pb[bk]
                        for i in range(NT):
                            w = []
                            if i == 0:
                                w = [(dGP, dW), (cVE, m_u3),
                                     (cSC, st["ev"].get(bk, 0))]
                            te(lambda E, i=i, b=bank, kb=kb, h=half: E.matmul(
                                b[:, :], kb[:, 128 * i:128 * (i + 1)],
                                U23[:, CH * i + 512 * h:CH * i + 512 * (h + 1)],
                                start=(i == 0), stop=(i == NT - 1)), waits=w)
                        mms.append((bk, cTE.n))
                    st["mmdone"][o % 2] = cTE.n
                    return mms

                return emit

            def qk_ev(sdst, o, mms):
                def emit():
                    marks = []
                    for half, (bk, m_mm) in enumerate(mms()):
                        m_ev = sca(lambda E, b=pb[mms()[half][0]], h=half: E.copy(
                            KB2[:, 4096 + 512 * h:4096 + 512 * (h + 1)], b[:, :]),
                            waits=[(cTE, m_mm), (dSY, st["sp"].get("qk", 0))])
                        st["ev"][bk] = m_ev
                        marks.append(m_ev)
                    st["sp"]["qk"] = sdma(
                        sdst[128 * o:128 * (o + 1), base:base + CH],
                        KB2[:, 4096:5120], waits=[(cSC, marks[-1])])
                return emit

            def v_mm(ob, tt, dwv):
                def emit():
                    if tt == 0:
                        dwv[0] = gdma(KB2.rearrange("p (i q) -> p i q", q=512),
                                      wvT[:, 512 * ob:512 * (ob + 1)]
                                      .rearrange("(i p) q -> p i q", p=128),
                                      waits=[(cTE, cTE.n), (dSY, dSY.n),
                                             (cSC, st["ev"].get(tt % 2, 0))])
                    bank = pb[tt % 2]
                    for i in range(NT):
                        w = []
                        if i == 0:
                            w = [(dGP, dwv[0]), (cVE, m_u3),
                                 (cSC, st["ev"].get(tt % 2, 0))]
                        te(lambda E, i=i, tt=tt, b=bank: E.matmul(
                            b[:, :],
                            U23[:, CH * i + 128 * tt:CH * i + 128 * (tt + 1)],
                            KB2[:, 512 * i:512 * (i + 1)],
                            start=(i == 0), stop=(i == NT - 1)), waits=w)
                    return [(tt % 2, cTE.n)]
                return emit

            def v_ev(ob, tt, mms):
                def emit():
                    bk, m_mm = mms()[0]
                    m_ev = sca(lambda E, b=pb[bk]: E.copy(F32B[:, 1536:2048],
                                                          b[:, :]),
                               waits=[(cTE, m_mm), (dGP, st["sp"].get("v", 0))])
                    st["ev"][bk] = m_ev
                    st["sp"]["v"] = gdma(
                        s_v[base + 128 * tt:base + 128 * (tt + 1),
                            512 * ob:512 * (ob + 1)], F32B[:, 1536:2048],
                        waits=[(cSC, m_ev)])
                return emit

            units = []
            for o in range(NT):
                units.append(("qk", wqT, s_qT, o))
            for o in range(NT):
                units.append(("qk", wkT, s_kT, o))
            for ob in range(4):
                dwv = [0]
                for tt in range(8):
                    units.append(("v", ob, tt, dwv))
            pend = [None]
            pre = []
            for u in units:
                if u[0] == "qk":
                    _, wT, sdst, o = u
                    res = {}
                    slot = [0]
                    ld = qk_load(wT, o, slot)
                    mm = qk_mm(wT, o, slot)
                    def item(mm=mm, res=res):
                        res["mms"] = mm()
                    ev = qk_ev(sdst, o, lambda res=res: res["mms"])
                else:
                    _, ob, tt, dwv = u
                    res = {}
                    ld = None
                    mm = v_mm(ob, tt, dwv)
                    def item(mm=mm, res=res):
                        res["mms"] = mm()
                    ev = v_ev(ob, tt, lambda res=res: res["mms"])
                pre.append(ld)
                units[len(pre) - 1] = (item, ev)
            for k, (item, ev) in enumerate(units):
                def combined(item=item, prev_ev=pend[0],
                             nld=(pre[k + 1] if k + 1 < len(pre) else None),
                             first=(pre[k] if k == 0 else None)):
                    if first is not None:
                        first()
                    if nld is not None:
                        nld()
                    if prev_ev is not None:
                        prev_ev()
                    item()
                items.append(combined)
                pend[0] = ev
            def final(prev_ev=pend[0]):
                if prev_ev is not None:
                    prev_ev()
            items.append(final)
            return items

        m_u3_prev = 0
        if stage < 2:
            _finish(P, cGP, dGP, dSY)
            return _close(nc, P)
        m_mamba = 0
        d_rspill = 0
        feeder = []
        for c in range(NCH):
            if c == 0:
                d_xload = d_xload0
                d_xload2 = d_xload0
            else:
                wR = [(dSY, d_rspill), (cVE, m_u3_prev)]
                Rv = R.rearrange("p (i q) -> p i q", q=CH)
                xv = (xT[:, CH * c:CH * (c + 1)]
                      .rearrange("(i p) q -> p i q", p=128))
                d_xload = gdma(Rv[:, :, 0:SC], xv[:, :, 0:SC], waits=wR,
                               slow=True)
                d_xload2 = gdma(Rv[:, :, SC:CH], xv[:, :, SC:CH], waits=[],
                                slow=True)
            for sci in range(SCPC):
                s = SCPC * c + sci
                fw = []
                if sci == 0:
                    fw = [(dGP, d_xload)]
                else:
                    fw = [(dGP, d_xload2)]
                m_mamba = mamba_sc(s, first_wait=fw, bc_free=m_mamba,
                                   feeder=feeder)
                if stage == 22 and s == 0:
                    tap("hc", PAR[:, :], [128, 768], F32)
                    _finish(P, cGP, dGP, dSY)
                    return _close(nc, P)
                if stage == 23 and s == 1:
                    tap("R_c0", R[:, :], [128, NT * CH], BF16)
                    _finish(P, cGP, dGP, dSY)
                    return _close(nc, P)
                if stage in (15, 17, 18, 1801, 1802, 1803, 1804, 1805, 181, 182, 183, 184, 19, 2):
                    tap("R_sc0", R[:, :], [128, NT * CH], BF16)
                    tap("u_sc0", U1[:, :], [128, NT * UW], BF16)
                    tap("xc_sc0", XC2[:, :], [128, NT * SC], BF16)
                    _finish(P, cGP, dGP, dSY)
                    return _close(nc, P)
            for fn in feeder:
                fn()
            feeder = []
            m_u2 = 0
            for half in range(2):
                m_u2 = ln_feat(lambda i, h=half: U23[:, CH * i + 512 * h:
                                                     CH * i + 512 * h + 512],
                               512 * half, 512,
                               wait_ve=m_mamba, wait_x=[(cTE, cTE.n)])
            m_ffn = ffn0(c, m_u2)
            if stage == 25:
                tap("R_ffn0", R[:, :], [128, NT * CH], BF16)
                _finish(P, cGP, dGP, dSY)
                return _close(nc, P)
            if stage == 24:
                tap("u2", U23[:, :], [128, NT * CH], BF16)
                _finish(P, cGP, dGP, dSY)
                return _close(nc, P)
            m_u3 = 0
            for half in range(2):
                m_u3 = ln_feat(lambda i, h=half: U23[:, CH * i + 512 * h:
                                                     CH * i + 512 * h + 512],
                               512 * half, 512,
                               wait_ve=m_ffn, wait_x=[(cTE, cTE.n)])
            d_rspill = sdma(s_R[:, CH * c:CH * (c + 1)]
                            .rearrange("(i p) q -> p i q", p=128),
                            R.rearrange("p (i q) -> p i q", q=CH),
                            waits=[(cTE, cTE.n), (cVE, m_u3)])
            feeder = qkv_feeder(c, m_u3)
            m_u3_prev = m_u3
        for fn in feeder:
            fn()
        feeder = []
        w1_prep()
        tap("R_A", R[:, :], [128, NT * CH], BF16)
        if stage < 4:
            _finish(P, cGP, dGP, dSY)
            return _close(nc, P)
        tap("qT", None, None) if False else None

        b0pre = {}
        b0q2 = {}
        b0mem = {}
        for h in range(NH):
            cb = (h % 2) * 2048
            hp = (h + 8) % 16
            if h == 0:
                d_kt = sdma(KB2[:, cb:cb + 2048],
                            s_kT[128 * h:128 * (h + 1), :],
                            waits=[(cTE, cTE.n), (cVE, cVE.n)])
                d_kp = sdma(KB2[:, 4096 + cb:4096 + cb + 2048],
                            s_kT[128 * hp:128 * (hp + 1), :], waits=[])
                d_cs = gdma(U23[:, 0:2048], cosTd[128 * h:128 * (h + 1), :],
                            waits=[(cVE, cVE.n), (cTE, cTE.n)])
                d_sn = gdma(U23[:, 2048:4096], sinTd[128 * h:128 * (h + 1), :],
                            waits=[])
                d_vh = sdma(XC2[:, 4096 + cb:4096 + cb + 2048]
                            .rearrange("p (j q) -> p j q", q=128),
                            s_v[:, 128 * h:128 * (h + 1)]
                            .rearrange("(j p) q -> p j q", p=128),
                            waits=[(cTE, cTE.n), (dGP, dGP.n)])
                kw = [(dSY, d_kp), (dGP, d_sn)]
                vw = [(dSY, d_vh)]
            else:
                d_sn, d_vh = b0pre[h]
                kw = [(dGP, d_sn)]
                vw = [(dGP, d_vh)]
            if h + 1 < NH:
                h1 = h + 1
                cb1 = (h1 % 2) * 2048
                hp1 = (h1 + 8) % 16
                wk = ([(cVE, b0q2[h - 1])] if h >= 1 else
                      [(cVE, cVE.n), (cTE, cTE.n), (cSC, cSC.n),
                       (dSY, dSY.n)])
                gdma(KB2[:, cb1:cb1 + 2048],
                     s_kT[128 * h1:128 * (h1 + 1), :], waits=wk)
                gdma(KB2[:, 4096 + cb1:4096 + cb1 + 2048],
                     s_kT[128 * hp1:128 * (hp1 + 1), :], waits=[])
                wv = ([(cTE, b0mem[h - 1])] if h >= 1 else
                      [(cVE, cVE.n), (cTE, cTE.n), (cSC, cSC.n)])
                p_vh = gdma(XC2[:, 4096 + cb1:4096 + cb1 + 2048]
                            .rearrange("p (j q) -> p j q", q=128),
                            s_v[:, 128 * h1:128 * (h1 + 1)]
                            .rearrange("(j p) q -> p j q", p=128),
                            waits=wv, slow=True)
            ve(lambda E, cb=cb: E.tensor_tensor(
                U23[:, 4096:6144], KB2[:, cb:cb + 2048], U23[:, 0:2048],
                op=ALU.mult),
               waits=kw + [(cSC, cSC.n), (cTE, cTE.n)])
            ve(lambda E, cb=cb: E.tensor_tensor(
                MSC[:, 2048:4096], KB2[:, 4096 + cb:4096 + cb + 2048],
                U23[:, 2048:4096], op=ALU.mult))
            mkr = ve(lambda E: E.tensor_tensor(
                U23[:, 4096:6144], U23[:, 4096:6144], MSC[:, 2048:4096],
                op=ALU.add))
            sdma(s_kTr[128 * h:128 * (h + 1), :], U23[:, 4096:6144],
                 waits=[(cVE, mkr)])
            mtr_ev = 0
            for g in range(2):
                mtr = 0
                for jj in range(8):
                    j = 8 * g + jj
                    wtr = [(cVE, mkr)]
                    if jj == 0 and mtr_ev:
                        wtr.append((cSC, mtr_ev))
                    mtr = te(lambda E, j=j, jj=jj: E.transpose(
                        pbt[:, 128 * jj:128 * (jj + 1)],
                        U23[:, 4096 + 128 * j:4096 + 128 * (j + 1)],
                        IDENB[:, :]), waits=wtr)
                mtr_ev = sca(lambda E, g=g, cb=cb: E.copy(
                    XC2[:, cb + 1024 * g:cb + 1024 * (g + 1)], pbt[:, :]),
                    waits=[(cTE, mtr)])
            d_qt = sdma(KB2[:, cb:cb + 2048], s_qT[128 * h:128 * (h + 1), :],
                        waits=[(cVE, cVE.n)])
            d_qp = sdma(KB2[:, 4096 + cb:4096 + cb + 2048],
                        s_qT[128 * hp:128 * (hp + 1), :], waits=[])
            ve(lambda E, cb=cb: E.tensor_tensor(
                U23[:, 6144:8192], KB2[:, cb:cb + 2048], U23[:, 0:2048],
                op=ALU.mult), waits=[(dSY, d_qp)])
            m_q2 = ve(lambda E, cb=cb: E.tensor_tensor(
                MSC[:, 2048:4096], KB2[:, 4096 + cb:4096 + cb + 2048],
                U23[:, 2048:4096], op=ALU.mult))
            b0q2[h] = m_q2
            mqr = ve(lambda E: E.tensor_tensor(
                U23[:, 6144:8192], U23[:, 6144:8192], MSC[:, 2048:4096],
                op=ALU.add))
            sdma(s_qTr[128 * h:128 * (h + 1), :], U23[:, 6144:8192],
                 waits=[(cVE, mqr)])
            if h + 1 < NH:
                gdma(U23[:, 0:2048], cosTd[128 * (h + 1):128 * (h + 2), :],
                     waits=[(cVE, m_q2)])
                p_sn = gdma(U23[:, 2048:4096],
                            sinTd[128 * (h + 1):128 * (h + 2), :], waits=[])
                b0pre[h + 1] = (p_sn, p_vh)
            mv1 = ve(lambda E, cb=cb: E.tensor_scalar_min(
                MSC[:, 2048:4096], XC2[:, cb:cb + 2048], 0.0),
                waits=[(cSC, mtr_ev)])
            ms1 = sca(lambda E: E.activation(MSC[:, 2048:4096], MSC[:, 2048:4096],
                                             AF.Exp), waits=[(cVE, mv1)])
            mv2 = ve(lambda E, cb=cb: E.tensor_relu(MSC[:, 0:2048],
                                                    XC2[:, cb:cb + 2048]))
            mkf = ve(lambda E: E.tensor_tensor(MSC[:, 0:2048], MSC[:, 0:2048],
                                               MSC[:, 2048:4096], op=ALU.add),
                     waits=[(cSC, ms1)])
            for j in range(NT):
                w = []
                if j == 0:
                    w = [(cVE, mkf), (cSC, cSC.n)] + vw
                te(lambda E, j=j, cb=cb: E.matmul(
                    pb[0][:, 0:128], MSC[:, 128 * j:128 * (j + 1)],
                    XC2[:, 4096 + cb + 128 * j:4096 + cb + 128 * (j + 1)],
                    start=(j == 0), stop=(j == NT - 1)), waits=w)
                te(lambda E, j=j: E.matmul(
                    pb[1][0:1, 0:128], PARB[:, B_ones:B_ones + 1],
                    MSC[:, 128 * j:128 * (j + 1)],
                    start=(j == 0), stop=(j == NT - 1)))
            m_mem = cTE.n
            b0mem[h] = m_mem
            sca(lambda E, h=h: E.copy(U1[:, 128 * h:128 * (h + 1)],
                                      pb[0][:, 0:128]),
                waits=[(cTE, m_mem)] + ([(cVE, cVE.n)] if h == 0 else []))
            m_e2 = sca(lambda E, h=h: E.copy(ROWS[0:1, 128 * h:128 * (h + 1)],
                                             pb[1][0:1, 0:128]),
                       waits=[(cTE, m_mem)])
        d_nw = gdma(s_row[3:4, 0:2048], ROWS[0:1, 0:2048], waits=[(cSC, cSC.n)])
        d_norm = gdma(PARB[:, B_norm:B_norm + 16],
                      s_row[3, 0:2048].rearrange("(h p) -> p h", p=128),
                      waits=[(dGP, d_nw)], slow=True)

        tap("memB", U1[:, 0:2048], [128, 2048], BF16)
        if stage < 5:
            _finish(P, cGP, dGP, dSY)
            return _close(nc, P)

        for qcc in range(2):
            qb0 = 1024 * qcc
            wR = [(cVE, cVE.n), (cTE, cTE.n)] if qcc > 0 else [(dSY, d_rspill)]
            d_rload = sdma(R.rearrange("p (i q) -> p i q", q=CH),
                           s_R[:, qb0:qb0 + CH]
                           .rearrange("(i p) q -> p i q", p=128), waits=wR)
            hq = [(qc2, h) for h in range(NH) for qc2 in range(2)]
            lds = {}
            mk = {}

            def issue_loads(j):
                qc2, h = hq[j]
                qb = qb0 + 512 * qc2
                cb = (h % 2) * 2048
                pm = mk.get(j - 2, {})
                if qc2 == 0:
                    pm4 = mk.get(j - 3, {})
                    wkv = ([(cTE, pm4.get("av", 0))] if pm4 else
                           [(cTE, cTE.n)])
                    d_kt = sdma(KB2[:, cb:cb + 2048],
                                s_kTr[128 * h:128 * (h + 1), :], waits=wkv)
                    d_vh = sdma(KB2[:, 4096 + cb:4096 + cb + 2048]
                                .rearrange("p (j q) -> p j q", q=128),
                                s_v[:, 128 * h:128 * (h + 1)]
                                .rearrange("(j p) q -> p j q", p=128),
                                waits=(wkv if pm4 else
                                       [(cTE, cTE.n), (dGP, dGP.n)]))
                    lds[("kv", h)] = (d_kt, d_vh)
                d_kt, d_vh = lds[("kv", h)]
                d_qt = sdma(MSC[:, 512 * (j % 2):512 * (j % 2) + 512],
                            s_qTr[128 * h:128 * (h + 1), qb:qb + 512],
                            waits=([(cTE, pm.get("mo", 0)),
                                    (cVE, pm.get("qdve", 0))] if pm else
                                   [(cTE, cTE.n), (cVE, cVE.n)]))
                lds[j] = (d_kt, d_vh, d_qt)

            issue_loads(0)
            issue_loads(1)
            for j, (qc2, h) in enumerate(hq):
                cb = (h % 2) * 2048
                qsl = MSC[:, 512 * (j % 2):512 * (j % 2) + 512]
                qf = MSC[:, 1536 + 512 * (j % 2):1536 + 512 * (j % 2) + 512]
                d_kt, d_vh, d_qt = lds[j]
                pm = mk.get(j - 1, {})
                m = {}
                mv1 = ve(lambda E, q=qsl: E.tensor_scalar_min(
                    MSC[:, 1024:1536], q, 0.0),
                    waits=[(dSY, d_qt), (cSC, pm.get("me2", 0))])
                ms1 = sca(lambda E: E.activation(
                    MSC[:, 1024:1536], MSC[:, 1024:1536], AF.Exp),
                    waits=[(cVE, mv1)])
                mv2 = ve(lambda E, q=qsl, f=qf: E.tensor_relu(f, q))
                m["qdve"] = mv2
                mqf = ve(lambda E, f=qf: E.tensor_tensor(
                    f, f, MSC[:, 1024:1536], op=ALU.add), waits=[(cSC, ms1)])
                m_exps = []
                mmks = []

                def emit_S(kt):
                    bank = pb[kt % 3]
                    w = []
                    if kt == 0:
                        w = [(dSY, d_kt), (cSC, pm.get("exp15", 0)),
                             (cVE, pm.get("comb", 0))]
                    elif kt in (1, 2):
                        w = [(cSC, pm.get("exp15", 0)),
                             (cVE, pm.get("comb", 0))]
                    else:
                        w = [(cSC, m_exps[kt - 3])]
                    mmk = te(lambda E, kt=kt, b=bank, cb=cb, q=qsl: E.matmul(
                        b[:, :], KB2[:, cb + 128 * kt:cb + 128 * (kt + 1)],
                        q, start=True, stop=True), waits=w)
                    me = sca(lambda E, kt=kt, b=bank: E.activation(
                        XC2[:, 512 * kt:512 * (kt + 1)], b[:, :], AF.Exp,
                        bias=CAPS[:, 4:5], scale=CAPS[:, 5:6]),
                        waits=[(cTE, mmk)])
                    m_exps.append(me)

                for kt in range(3):
                    emit_S(kt)
                for kt in range(NT):
                    if kt + 3 < NT:
                        emit_S(kt + 3)
                    w = [(cSC, m_exps[kt])]
                    if kt == 0:
                        w += [(dSY, d_vh), (cSC, pm.get("me1", 0)),
                              (cVE, pm.get("mr1", 0))]
                    te(lambda E, kt=kt, cb=cb: E.matmul(
                        pb[3][:, :],
                        KB2[:, 4096 + cb + 128 * kt:4096 + cb + 128 * (kt + 1)],
                        XC2[:, 512 * kt:512 * (kt + 1)],
                        start=(kt == 0), stop=(kt == NT - 1)), waits=w)
                    te(lambda E, kt=kt: E.matmul(
                        pb[4][0:1, :], PARB[:, B_ones:B_ones + 1],
                        XC2[:, 512 * kt:512 * (kt + 1)],
                        start=(kt == 0), stop=(kt == NT - 1)))
                m["exp15"] = m_exps[15]
                m["exp14"] = m_exps[14]
                m["av"] = cTE.n
                te(lambda E, h=h, f=qf: E.matmul(
                    pb[5][:, :], U1[:, 128 * h:128 * (h + 1)], f,
                    start=True, stop=True),
                   waits=[(cVE, mqf), (dGP, d_norm), (cSC, pm.get("me2", 0)),
                          (cVE, pm.get("mr2", 0))])
                m_mo = te(lambda E, h=h, f=qf: E.matmul(
                    pb[6][0:1, :], PARB[:, B_norm + h:B_norm + h + 1], f,
                    start=True, stop=True))
                m["mo"] = m_mo
                if j + 2 < len(hq):
                    issue_loads(j + 2)
                mr1 = ve(lambda E: E.reciprocal(ROWS[0:1, 0:512],
                                                pb[4][0:1, :]),
                         waits=[(cTE, m_mo)])
                m["mr1"] = mr1
                mr2 = ve(lambda E: E.reciprocal(ROWS[0:1, 512:1024],
                                                pb[6][0:1, :]))
                m["mr2"] = mr2
                mrb = ve(lambda E, h=h: E.tensor_scalar(
                    ROWS[0:1, 1024:1536], ROWS[0:1, 512:1024],
                    GS[0:1, h:h + 1], None, op0=ALU.mult),
                    waits=[(cTE, pm.get("rep", 0))])
                mrb = ve(lambda E, h=h: E.tensor_scalar(
                    ROWS[0:1, 1536:2048], ROWS[0:1, 0:512],
                    GS[0:1, 16 + h:16 + h + 1], None, op0=ALU.mult),
                    waits=[(cVE, m_g1m)])
                te(lambda E: E.matmul(
                    pb[0][:, :], ONER[0:1, :], ROWS[0:1, 1024:1536],
                    start=True, stop=True),
                   waits=[(cVE, mrb), (cSC, m_exps[15])])
                m["rep"] = te(lambda E: E.matmul(
                    pb[1][:, :], ONER[0:1, :], ROWS[0:1, 1536:2048],
                    start=True, stop=True), waits=[(cSC, m_exps[13])])
                m["sc"] = m["rep"]
                me1 = sca(lambda E: E.copy(MSC[:, 2560:3072], pb[3][:, :]),
                          waits=[(cTE, m["rep"])])
                m["me1"] = me1
                me2 = sca(lambda E: E.copy(MSC[:, 3072:3584], pb[5][:, :]))
                m["me2"] = me2
                mt1 = ve(lambda E: E.tensor_tensor(
                    MSC[:, 3584:4096], MSC[:, 2560:3072], pb[1][:, :],
                    op=ALU.mult), waits=[(cSC, me2), (cTE, m["rep"])])
                ve(lambda E: E.tensor_tensor(
                    MSC[:, 2560:3072], MSC[:, 3072:3584], pb[0][:, :],
                    op=ALU.mult))
                m["comb"] = ve(lambda E, h=h, q=qc2: E.tensor_tensor(
                    BC4[:, 1024 * h + 512 * q:1024 * h + 512 * (q + 1)],
                    MSC[:, 3584:4096], MSC[:, 2560:3072], op=ALU.add))
                mk[j] = m
            m_comb = cVE.n
            lastv = 0
            for o in range(NT):
                kb = KB2[:, (o % 2) * 2048:(o % 2) * 2048 + 2048]
                dW = sdma(kb.rearrange("p (i q) -> p i q", q=128),
                          s_woT[:, 128 * o:128 * (o + 1)]
                          .rearrange("(i p) q -> p i q", p=128),
                          waits=[(cTE, cTE.n)])
                for half in range(2):
                    bank = pb[2 + half]
                    for i in range(NT):
                        w = []
                        if i == 0:
                            w = [(dSY, dW), (cVE, max(m_comb, lastv)),
                                 (cSC, cSC.n)]
                        te(lambda E, i=i, h=half, b=bank, kb=kb: E.matmul(
                            b[:, :], kb[:, 128 * i:128 * (i + 1)],
                            BC4[:, 1024 * i + 512 * h:1024 * i + 512 * (h + 1)],
                            start=(i == 0), stop=(i == NT - 1)), waits=w)
                    m_mm = cTE.n
                    rsl = R[:, CH * o + 512 * half:CH * o + 512 * half + 512]
                    lastv = ve(lambda E, r=rsl, b=bank: E.tensor_tensor(
                        r, r, b[:, :], op=ALU.add),
                        waits=[(cTE, m_mm), (dSY, d_rload)])
            m_wo = lastv
            m_u4 = 0
            for half in range(2):
                m_u4 = ln_feat(lambda i, h=half: U23[:, CH * i + 512 * h:
                                                     CH * i + 512 * h + 512],
                               512 * half, 512,
                               wait_ve=m_wo, wait_x=[(cTE, cTE.n)])
            mx = ve(lambda E: E.tensor_tensor(F32A[:, 2048:3072], U23[:, 0:CH],
                                              U23[:, 0:CH], op=ALU.mult),
                    waits=[(cVE, m_u4)])
            for i in range(1, NT):
                ve(lambda E, i=i: E.tensor_tensor(
                    F32B[:, 1024:2048], U23[:, CH * i:CH * i + CH],
                    U23[:, CH * i:CH * i + CH], op=ALU.mult))
                mx = ve(lambda E: E.tensor_tensor(
                    F32A[:, 2048:3072], F32A[:, 2048:3072], F32B[:, 1024:2048],
                    op=ALU.max))
            width = 64
            while width >= 1:
                dsh = gdma(F32B[0:width, 0:1024], F32A[width:2 * width, 2048:3072],
                           waits=[(cVE, mx)])
                mx = ve(lambda E, w=width: E.tensor_tensor(
                    F32A[0:w, 2048:3072], F32A[0:w, 2048:3072],
                    F32B[0:w, 0:1024], op=ALU.max), waits=[(dGP, dsh)])
                width //= 2
            ms = sca(lambda E: E.activation(ROWS[0:1, 0:1024],
                                            F32A[0:1, 2048:3072], AF.Sqrt),
                     waits=[(cVE, mx), (cTE, cTE.n)])
            mv = ve(lambda E: E.tensor_scalar(ROWS[0:1, 1024:2048],
                                              ROWS[0:1, 0:1024], 1.0 / 127.0,
                                              1e-12, op0=ALU.mult, op1=ALU.max),
                    waits=[(cSC, ms)])
            mvr = ve(lambda E: E.reciprocal(ROWS[0:1, 2048:3072],
                                            ROWS[0:1, 1024:2048]))
            for j in range(2):
                te(lambda E, j=j: E.matmul(
                    pb[3 + j][:, :], ONER[0:1, :],
                    ROWS[0:1, 1024 + 512 * j:1024 + 512 * (j + 1)],
                    start=True, stop=True),
                   waits=[(cVE, mvr), (cSC, cSC.n)] if j == 0 else [])
                te(lambda E, j=j: E.matmul(
                    pb[5 + j][:, :], ONER[0:1, :],
                    ROWS[0:1, 2048 + 512 * j:2048 + 512 * (j + 1)],
                    start=True, stop=True))
            m_reps = cTE.n
            m_xsr = 0
            for j in range(2):
                ve(lambda E, j=j: E.tensor_copy(F32A[:, 512 * j:512 * (j + 1)],
                                                pb[3 + j][:, :]),
                   waits=[(cTE, m_reps)])
                m_xsr = ve(lambda E, j=j: E.tensor_copy(
                    F32A[:, 1024 + 512 * j:1024 + 512 * (j + 1)],
                    pb[5 + j][:, :]))
            m_xq = 0
            for i in range(NT):
                ve(lambda E, i=i: E.scalar_tensor_tensor(
                    F32B[:, 0:1024], U23[:, CH * i:CH * i + CH], 1.0,
                    F32A[:, 1024:2048], op0=ALU.mult, op1=ALU.mult),
                   waits=[(cVE, m_xsr)] if i == 0 else [])
                m_xq = ve(lambda E, i=i: E.tensor_scalar(
                    U23[:, CH * i:CH * i + CH], F32B[:, 0:1024], MAGIC, MAGIC,
                    op0=ALU.add, op1=ALU.subtract))
            tr_marks = {}
            for o in range(NT):
                kb = KB2[:, (o % 2) * 2048:(o % 2) * 2048 + 2048]
                dW = sdma(kb.rearrange("p (i q) -> p i q", q=128),
                          s_wq1[:, 128 * o:128 * (o + 1)]
                          .rearrange("(i p) q -> p i q", p=128),
                          waits=[(cTE, cTE.n)])
                for half in range(2):
                    bank = pb[half]
                    for i in range(NT):
                        w = []
                        if i == 0:
                            w = [(dSY, dW), (cVE, max(m_xq, cVE.n)), (cSC, cSC.n)]
                        te(lambda E, i=i, h=half, b=bank, kb=kb: E.matmul(
                            b[:, :], kb[:, 128 * i:128 * (i + 1)],
                            U23[:, CH * i + 512 * h:CH * i + 512 * (h + 1)],
                            start=(i == 0), stop=(i == NT - 1)), waits=w)
                    m_mm = cTE.n
                    ve(lambda E, o=o, b=bank, h=half: E.scalar_tensor_tensor(
                        F32B[:, 0:512], b[:, :],
                        PAR[:, P_ws1 + o:P_ws1 + o + 1],
                        F32A[:, 512 * h:512 * (h + 1)],
                        op0=ALU.mult, op1=ALU.mult), waits=[(cTE, m_mm)])
                    rsl = R[:, CH * o + 512 * half:CH * o + 512 * half + 512]
                    ve(lambda E, o=o, r=rsl, h=half: E.scalar_tensor_tensor(
                        OUTS[:, 512 * h:512 * (h + 1)], F32B[:, 0:512],
                        PAR[:, P_b1 + o:P_b1 + o + 1], r,
                        op0=ALU.add, op1=ALU.add))
                m_row = cVE.n
                for tt in range(8):
                    bank = pb[3 + tt % 2]
                    w = [(cVE, m_row)]
                    if (o, tt) != (0, 0):
                        pass
                    if tt >= 2 or o > 0:
                        prev = tr_marks.get((o, tt - 2)) or \
                            tr_marks.get((o - 1, tt + 6))
                        if prev:
                            w.append((cGP, prev))
                    mtr = te(lambda E, tt=tt, b=bank: E.transpose(
                        b[:, 0:128], OUTS[:, 128 * tt:128 * (tt + 1)],
                        IDEN[:, :]), waits=w)
                    msc = sca(lambda E, tt=tt, b=bank: E.copy(
                        F32A[:, 3072 + 128 * (tt % 2):3072 + 128 * (tt % 2) + 128],
                        b[:, 0:128]), waits=[(cTE, mtr)])
                    P.op("gp", lambda E, tt=tt, o=o, q=qb0: E.dma_start(
                        out=out[q + 128 * tt:q + 128 * (tt + 1),
                                128 * o:128 * (o + 1)],
                        in_=F32A[:, 3072 + 128 * (tt % 2):
                                 3072 + 128 * (tt % 2) + 128]),
                         waits=[(cSC, msc)], inc=cGP, k=16)
                    tr_marks[(o, tt)] = cGP.n
        _finish(P, cGP, dGP, dSY)
        return _close(nc, P)


def _cols16(v):
    return np.ascontiguousarray(np.asarray(v).reshape(16, 128).T)


def _host_inputs(inp):
    f32 = np.float32
    m = {}
    m["convw"] = np.ascontiguousarray(
        np.asarray(inp["L0_attn_conv_w"]).reshape(16, 128, 4)
        .transpose(1, 0, 2).reshape(128, 64)).astype(f32)
    m["convb"] = _cols16(inp["L0_attn_conv_b"]).astype(f32)
    m["xprojT"] = np.ascontiguousarray(np.asarray(inp["L0_attn_xproj_w"]).T).astype(f32)
    m["dtprojT"] = np.ascontiguousarray(np.asarray(inp["L0_attn_dtproj_w"]).T).astype(f32)
    m["dtprojb"] = _cols16(inp["L0_attn_dtproj_b"]).astype(f32)
    m["alog"] = np.ascontiguousarray(
        np.asarray(inp["L0_attn_A_log"]).reshape(16, 128, 16)
        .transpose(1, 0, 2).reshape(128, 256)).astype(f32)
    m["dpar"] = _cols16(inp["L0_attn_D_param"]).astype(f32)
    m["w0T"] = np.ascontiguousarray(np.asarray(inp["L0_ffn_weight"]).T).astype(f32)
    m["b0c"] = _cols16(inp["L0_ffn_bias"]).astype(f32)
    m["wqT"] = np.ascontiguousarray(np.asarray(inp["L1_attn_Wq"]).T).astype(f32)
    m["wkT"] = np.ascontiguousarray(np.asarray(inp["L1_attn_Wk"]).T).astype(f32)
    m["wvT"] = np.ascontiguousarray(np.asarray(inp["L1_attn_Wv"]).T).astype(f32)
    m["woT"] = np.ascontiguousarray(np.asarray(inp["L1_attn_Wo"]).T).astype(f32)
    m["gate"] = np.ascontiguousarray(np.asarray(inp["L1_attn_gate"]).reshape(1, 16)).astype(f32)
    m["w1T"] = np.ascontiguousarray(np.asarray(inp["L1_ffn_weight"]).T).astype(f32)
    m["b1c"] = _cols16(inp["L1_ffn_bias"]).astype(f32)
    t = np.arange(T, dtype=np.float64)
    invf = 1.0 / (10000.0 ** (np.arange(0, D, 2, dtype=np.float64) / D))
    ang = invf[:, None] * t[None, :]
    m["cosT"] = np.concatenate([np.cos(ang), np.cos(ang)], 0).astype(f32)
    m["sinT"] = np.concatenate([-np.sin(ang), np.sin(ang)], 0).astype(f32)
    m["ident"] = np.eye(128, dtype=f32)
    return m


def kernel(**inputs):
    shared = _host_inputs(inputs)
    x = np.asarray(inputs["L0_attn_x"])
    nc = build_program()
    in_maps = []
    for b in range(8):
        mm = dict(shared)
        mm["xT"] = np.ascontiguousarray(x[b].T).astype(np.float32)
        in_maps.append(mm)
    res = run_bass_kernel_spmd(nc, in_maps, core_ids=list(range(8)))
    return np.stack([np.asarray(r["y"], np.float32) for r in res.results], 0)
